# version 03
# baseline (speedup 1.0000x reference)
import numpy as np
import ml_dtypes
from collections import deque
from contextlib import ExitStack

import concourse.bass as bass
import concourse.mybir as mybir
from concourse.bass_utils import run_bass_kernel_spmd

F32 = mybir.dt.float32
BF16 = mybir.dt.bfloat16
AF = mybir.ActivationFunctionType
ALU = mybir.AluOpType
AX = mybir.AxisListType.X

B, S, D = 8, 4096, 1024
TC = 1024
NB = TC // 128
NCH = S // TC
DCH = D // 128
NTG = TC // 512

BF = ml_dtypes.bfloat16


def host_constants():
    c = {}
    c["identf"] = np.eye(128, dtype=np.float32)
    c["identb"] = np.eye(128, dtype=BF)
    import jax
    import jax.numpy as jnp
    with jax.default_device(jax.devices("cpu")[0]):
        half = 64
        pos = jnp.arange(S, dtype=jnp.float32)[:, None]
        inv_freq = 10000.0 ** (jnp.arange(half, dtype=jnp.float32) / half)
        sinusoid = pos * inv_freq[None, :]
        c["rope_sin"] = np.asarray(jnp.sin(sinusoid), dtype=np.float32)
        c["rope_cos"] = np.asarray(jnp.cos(sinusoid), dtype=np.float32)
    return c


ENGS = ("y", "g", "t", "v", "s")


class Sem:
    def __init__(self, stack, nc, name):
        self.h = stack.enter_context(nc.semaphore(name))
        self.n = 0


class Emitter:

    CHAIN = ("v", "s")

    def __init__(self, nc, stack):
        self.nc = nc
        self.stack = stack
        self.q = {e: [] for e in ENGS}
        self.t = {}
        self._uid = 0
        self._last = {e: None for e in ENGS}
        self._chain = {}
        self.bgq = {"g": deque(), "s": deque()}

    def bg(self, eng, fn):
        self.bgq[eng].append(fn)

    def pump(self, eng, n=1):
        q = self.bgq[eng]
        for _ in range(n):
            if not q or q[0] == "STOP":
                return
            q.popleft()()

    def pop_stop(self, eng):
        q = self.bgq[eng]
        if q and q[0] == "STOP":
            q.popleft()

    def drain_bg(self):
        for eng in ("g", "s"):
            q = self.bgq[eng]
            while q:
                it = q.popleft()
                if it != "STOP":
                    it()

    def sem(self, name):
        self._uid += 1
        nm = f"{name}_{self._uid}"
        if getattr(self, "_phase_active", False):
            h = self.nc.alloc_semaphore(nm)
            sm = Sem.__new__(Sem)
            sm.h = h
            sm.n = 0
            self._psems.append(h)
            return sm
        return Sem(self.stack, self.nc, nm)

    def begin_phase(self):
        from contextlib import ExitStack
        self._pstack = ExitStack()
        self._pt = []
        self._psems = []
        self._phase_active = True
        self._last = {e: None for e in ENGS}
        self._chain = {}

    def salloc(self, name, shape, dt):
        self._uid += 1
        h = self._pstack.enter_context(
            self.nc.sbuf_tensor(f"{name}_{self._uid}", shape, dt))
        self.t[name] = h
        self._pt.append(name)
        return h

    def end_phase(self):
        self._phase_active = False
        if self._psems:
            self.nc.clear_and_free_semaphores(self._psems)
            self.nc.all_engine_barrier()
        self._pstack.close()
        for n in self._pt:
            self.t.pop(n, None)

    def on(self, eng, fn):
        self.q[eng].append(fn)

    def wait(self, eng, sem, val=None):
        v = sem.n if val is None else val
        if v > 0:
            self.q[eng].append(lambda e: e.wait_ge(sem.h, v))

    def op(self, eng, fn, sem=None, k=1):
        if eng in self.CHAIN:
            if self._last[eng] is not None:
                csem, cval = self._last[eng]
                self.q[eng].append(lambda e: e.wait_ge(csem.h, cval))
            if sem is None:
                if eng not in self._chain:
                    self._chain[eng] = self.sem(f"chain_{eng}")
                sem = self._chain[eng]
                k = 1
            sem.n += k
            self.q[eng].append(lambda e: fn(e).then_inc(sem.h, k))
            self._last[eng] = (sem, sem.n)
        elif sem is None:
            self.q[eng].append(fn)
        else:
            sem.n += k
            self.q[eng].append(lambda e: fn(e).then_inc(sem.h, k))

    def nopinc(self, eng, sem, k=1):
        if eng in self.CHAIN and self._last[eng] is not None:
            csem, cval = self._last[eng]
            self.q[eng].append(lambda e: e.wait_ge(csem.h, cval))
        sem.n += k
        self.q[eng].append(lambda e: e.nop().then_inc(sem.h, k))
        if eng in self.CHAIN:
            self._last[eng] = (sem, sem.n)

    def flush(self):
        nc, q = self.nc, self.q
        with nc.Block() as blk:
            @blk.sync
            def _(s):
                for f in q["y"]:
                    f(s)

            @blk.gpsimd
            def _(g):
                for f in q["g"]:
                    f(g)

            @blk.tensor
            def _(t):
                for f in q["t"]:
                    f(t)

            @blk.vector
            def _(v):
                for f in q["v"]:
                    f(v)

            @blk.scalar
            def _(s):
                for f in q["s"]:
                    f(s)
        self.q = {e: [] for e in ENGS}


def alloc_arenas(E):
    def sb(n, sh, dt):
        E.t[n] = E.stack.enter_context(E.nc.sbuf_tensor(n, sh, dt))

    def ps(n, sh, dt):
        E.t[n] = E.stack.enter_context(E.nc.psum_tensor(n, sh, dt))

    sb("XT", [128, NB * 1024], F32)
    sb("XF", [128, DCH * TC], BF16)
    sb("HH", [128, 32 * TC], BF16)
    sb("YY", [128, NB * 1024], F32)
    sb("WA", [128, 18432], BF16)
    sb("GS", [128, 1024], F32)
    sb("GB", [128, 2048], F32)
    sb("SS", [128, 272], F32)
    sb("IDF", [128, 128], F32)
    sb("CONST", [128, 8], F32)
    sb("MISC", [128, 1040], F32)
    for i in range(8):
        ps(f"P{i}", [128, 512], F32)


def emit_weight_prep(E, items_cast, items_tp):
    s_cast = E.sem("pw_cast")
    for it in items_cast:
        src, dst = it[0], it[1]
        E.op("g", lambda g, a=src, b=dst: g.dma_start(b.ap(), a), s_cast, 16)
    E.wait("y", s_cast)
    WA = E.t["WA"]
    s_ld = [E.sem("pw_ld0"), E.sem("pw_ld1")]
    s_st = [E.sem("pw_st0"), E.sem("pw_st1")]
    idx = 0
    for wbf, wt, R, C in items_tp:
        assert R <= 9216
        for c in range(C // 128):
            par = idx % 2
            if idx >= 2:
                E.wait("y", s_st[par], (idx // 2) * 16)
            E.op("y", lambda s, w=wbf, cc=c, p=par, r=R: s.dma_start(
                WA[:, p * 9216:p * 9216 + r],
                w.ap()[:, cc * 128:(cc + 1) * 128], transpose=True),
                s_ld[par], 16)
            E.wait("s", s_ld[par], (idx // 2 + 1) * 16)
            E.op("s", lambda s, w=wt, cc=c, p=par, r=R: s.dma_start(
                w.ap().rearrange("(c p r) -> c p r", p=128, r=r)[cc],
                WA[:, p * 9216:p * 9216 + r]), s_st[par], 16)
            idx += 1
    E.wait("y", s_st[0])
    E.wait("y", s_st[1])
    E.wait("g", s_st[0])
    E.wait("g", s_st[1])
    E.wait("g", s_cast)


def queue_bg_prep(E, cast_items, tp_items, marks_after):
    HH = E.t["HH"]
    E.bg_cast = E.sem("bgcast")
    cast_cnt = 0
    cast_need = {}
    cast_marks = {}
    after_idx = {v: k for k, v in marks_after.items()}
    for i, (src, dst, Rr, Cc) in enumerate(cast_items):
        rp = max(16, (8 << 20) // (Cc * 4))
        r0 = 0
        while r0 < Rr:
            r1 = min(Rr, r0 + rp)
            cast_cnt += 16
            def item(E=E, s=src, d=dst, a=r0, b=r1):
                E.on("g", lambda g: g.dma_start(d.ap()[a:b], s[a:b])
                     .then_inc(E.bg_cast.h, 16))
            E.bg("g", item)
            r0 = r1
        cast_need[id(dst)] = cast_cnt
        if i in after_idx:
            cast_marks[after_idx[i]] = cast_cnt
    E.bg_cast.n = cast_cnt

    NSLOT, SLOTW, BASE, LAG = 6, 2176, 16384, 4
    SEG_A = 560
    sld = [E.sem(f"bgtl{q}") for q in range(NSLOT)]
    sst = [E.sem(f"bgts{q}") for q in range(NSLOT)]
    ldc = [0] * NSLOT
    stc = [0] * NSLOT
    pend = deque()
    piece = 0
    nitem = 0
    seg_marks = []

    def put(it):
        nonlocal nitem
        E.bg("s", it)
        nitem += 1

    for (wbf, wt, Rr, Cc) in tp_items:
        need = cast_need[id(wbf)]
        nrp = -(-Rr // SLOTW)
        rps = [(i * Rr // nrp, (i + 1) * Rr // nrp) for i in range(nrp)]
        for cc in range(Cc // 128):
            for (a, b) in rps:
                q = piece % NSLOT
                prev_st = stc[q]
                ldc[q] += 16
                myld = ldc[q]
                def ld_item(E=E, w=wbf, cc=cc, a=a, b=b, q=q, need=need,
                            prev_st=prev_st):
                    if need:
                        E.on("s", lambda e, n=need: e.wait_ge(E.bg_cast.h, n))
                    if prev_st:
                        E.on("s", lambda e, n=prev_st, qq=q: e.wait_ge(
                            sst[qq].h, n))
                    E.on("s", lambda e, qq=q: e.dma_start(
                        HH[:, BASE + qq * SLOTW: BASE + qq * SLOTW + (b - a)],
                        w.ap()[a:b, cc * 128:(cc + 1) * 128], transpose=True)
                        .then_inc(sld[qq].h, 16))
                stc[q] += 16
                myld2 = myld
                def st_item(E=E, w=wt, cc=cc, a=a, b=b, q=q, Rr=Rr,
                            myld=myld2):
                    E.on("s", lambda e, n=myld, qq=q: e.wait_ge(sld[qq].h, n))
                    E.on("s", lambda e, qq=q: e.dma_start(
                        bass.AP(w, cc * 128 * Rr + a, [[Rr, 128], [1, b - a]]),
                        HH[:, BASE + qq * SLOTW: BASE + qq * SLOTW + (b - a)])
                        .then_inc(sst[qq].h, 16))
                put(ld_item)
                pend.append(st_item)
                if len(pend) > LAG:
                    put(pend.popleft())
                piece += 1
                if not seg_marks and nitem >= SEG_A:
                    while pend:
                        put(pend.popleft())
                    seg_marks.append(list(stc))
                    E.bg("s", "STOP")
    while pend:
        put(pend.popleft())
    if not seg_marks:
        seg_marks.append(list(stc))
        E.bg("s", "STOP")
    for q in range(NSLOT):
        sld[q].n = ldc[q]
        sst[q].n = stc[q]
    return cast_marks, list(zip(sst, seg_marks[0])), list(zip(sst, stc))


def emit_ln(E, sems, k, eps, apply=True):
    XT, SS, SQ = E.t["XT"], E.t["SS"], E.t["MISC"]
    s_x, s_sq, s_var, s_rstd, s_ln = sems
    if not hasattr(E, "_cinit"):
        E._cinit = E.sem("cinit")
        E.op("v", lambda v: v.memset(E.t["CONST"][:, 1:2], -0.5), E._cinit)
        E.wait("s", E._cinit, 1)
    E.wait("v", s_x, (k + 1) * 16)
    E.op("v", lambda v: v.tensor_reduce(
        SS[:, 0:NB], XT[:, :].rearrange("p (b d) -> p b d", b=NB), AX, ALU.add))
    E.op("v", lambda v: v.tensor_scalar_mul(SS[:, 0:NB], SS[:, 0:NB], 1.0 / D))
    E.op("v", lambda v: v.tensor_mul(SS[:, 16:24], SS[:, 0:NB], SS[:, 0:NB]))
    E.wait("s", s_x, (k + 1) * 16)
    for b in range(NB):
        E.wait("s", s_sq)
        E.op("s", lambda s, bb=b: s.activation(
            SQ[:, 0:1024], XT[:, bb * 1024:(bb + 1) * 1024], AF.Square,
            accum_out=SS[:, 8 + bb:9 + bb]), s_sq)
    E.wait("v", s_sq)
    E.op("v", lambda v: v.scalar_tensor_tensor(
        SS[:, 24:32], SS[:, 8:16], 1.0 / D, SS[:, 16:24],
        ALU.mult, ALU.subtract))
    E.op("v", lambda v: v.tensor_scalar_add(SS[:, 24:32], SS[:, 24:32],
                                            float(eps)), s_var)
    E.wait("s", s_var)
    E.op("s", lambda s: s.activation(SS[:, 40:48], SS[:, 24:32], AF.Ln), s_rstd)
    E.wait("s", s_rstd)
    E.op("s", lambda s: s.activation(SS[:, 32:40], SS[:, 40:48], AF.Exp,
                                     scale=E.t["CONST"][:, 1:2]), s_rstd)
    E.wait("v", s_rstd)
    if apply:
        for b in range(NB):
            E.op("v", lambda v, bb=b: v.tensor_scalar(
                XT[:, bb * 1024:(bb + 1) * 1024],
                XT[:, bb * 1024:(bb + 1) * 1024],
                SS[:, bb:bb + 1], SS[:, 32 + bb:33 + bb],
                ALU.subtract, ALU.mult), s_ln)


def emit_xpose(E, k, s_ln, s_tp, s_tpe, s_free, free_val):
    XT, XF, IDF = E.t["XT"], E.t["XF"], E.t["IDF"]
    P6, P7 = E.t["P6"], E.t["P7"]
    NT = NB * DCH
    for ti in range(NT):
        bank = (P6, P7)[ti % 2]
        b, c = ti // DCH, ti % DCH
        E.wait("t", s_ln, k * NB + b + 1)
        if ti >= 2:
            E.wait("t", s_tpe, k * NT + ti - 1)
        else:
            E.wait("t", s_free, free_val)
        E.op("t", lambda t, bb=b, cc=c, bk=bank: t.transpose(
            bk[:, 0:128],
            XT[:, bb * 1024 + cc * 128: bb * 1024 + (cc + 1) * 128],
            IDF[:, :]), s_tp)
        E.wait("v", s_tp)
        E.op("v", lambda v, bb=b, cc=c, bk=bank: v.tensor_copy(
            XF[:, cc * TC + bb * 128: cc * TC + (bb + 1) * 128],
            bk[:, 0:128]), s_tpe)


def emit_moe_phase(E, cfg):
    tag = cfg["tag"]
    F = cfg["F"]
    NE = cfg["n_exp"]
    R = cfg["R"]
    Gf = cfg["Gf"]
    shared = cfg.get("shared")
    mode = cfg["gate_mode"]

    XT, XF, HH, YY = E.t["XT"], E.t["XF"], E.t["HH"], E.t["YY"]
    WA, GS, GB, SS = E.t["WA"], E.t["GS"], E.t["GB"], E.t["SS"]
    IDF, MISC = E.t["IDF"], E.t["MISC"]
    GROW = E.salloc("GROW", [8, 1024], F32)
    GWS = E.salloc("GWS", [128, 128], BF16)
    GWL = E.salloc("GWL", [8, 1024], F32)
    BR = E.salloc("BR", [128, 16], F32)
    P = [E.t[f"P{i}"] for i in range(8)]

    WUP_OFF = [0, 4096, 8192, 12288]
    WD_OFF = [16384, 17408]

    sems = {n: E.sem(f"{tag}_{n}") for n in
            ("x sq var rstd ln tp tpe gw rt smx exp gv gtp gev gst gbf "
             "wuf wdf mg mu sil ue dmm ye yst").split()}
    s = sems
    s_gb = [E.sem(f"{tag}_gb{i}") for i in range(2)]
    s_wu = [E.sem(f"{tag}_wu{i}") for i in range(2)]
    s_wd = [E.sem(f"{tag}_wd{i}") for i in range(2)]

    exps = list(range(NE)) + (["S"] if shared else [])

    def nfc(e):
        return (shared["F"] // 128) if e == "S" else F // 128

    def nfs(e):
        return (shared["F"] // 512) if e == "S" else F // 512

    ne = NE
    s_gwe = E.sem(f"{tag}_gwe")
    E.op("y", lambda y: y.dma_start(GWL[:, :], cfg["gw"].ap()), s["gw"], 16)
    E.wait("t", s["gw"], 16)
    for c in range(DCH):
        bank = P[6 + c % 2]
        if c >= 2:
            E.wait("t", s_gwe, c - 1)
        E.op("t", lambda t, cc=c, bk=bank: t.transpose(
            bk[:, 0:8],
            GWL[:, cc * 128:(cc + 1) * 128], IDF[0:8, 0:8]), s["gtp"])
        E.wait("v", s["gtp"])
        E.op("v", lambda v, cc=c, bk=bank: v.tensor_copy(
            GWS[:, cc * ne:cc * ne + ne], bk[:, 0:8]), s_gwe)
    s_br = E.sem(f"{tag}_br")
    if mode == "l0":
        E.op("g", lambda g: g.dma_start(
            BR[:, 0:ne], bass.AP(cfg["gate_bias"], 0, [[0, 128], [1, ne]])),
            s_br, 16)

    slab_i = 0
    dfc_i = 0
    pair_i = 0
    gbe_i = 0
    ft_slab = []
    ft_gb = []
    ft_wd = []

    for k in range(NCH):
        if k >= 1:
            E.wait("g", s["tp"], k * NB * DCH)
        E.op("g", lambda g, kk=k: g.dma_start(
            XT[:, :].rearrange("p (b d) -> p b d", b=NB),
            R.ap().rearrange("(k b p) d -> k p b d", p=128, b=NB)[kk]),
            s["x"], 16)
        E.pump("g", 2)

        emit_ln(E, (s["x"], s["sq"], s["var"], s["rstd"], s["ln"]), k, 1e-5)
        if k == 0:
            emit_xpose(E, k, s["ln"], s["tp"], s["tpe"], s_gwe, s_gwe.n)
        else:
            emit_xpose(E, k, s["ln"], s["tp"], s["tpe"], s["ye"], s["ye"].n)

        E.wait("t", s["tpe"], (k + 1) * NB * DCH)
        if k > 0:
            E.wait("t", s["ye"])
        for b in range(NB):
            for c in range(DCH):
                E.op("t", lambda t, bb=b, cc=c, bk=P[b]: t.matmul(
                    bk[:, 0:8],
                    XF[:, cc * TC + bb * 128: cc * TC + (bb + 1) * 128],
                    GWS[:, cc * ne:cc * ne + ne],
                    start=(cc == 0), stop=(cc == DCH - 1)),
                    s["rt"] if c == DCH - 1 else None)
        for b in range(NB):
            lg = P[b][:, 0:8]
            pro = SS[:, 64 + b * 8: 72 + b * 8]
            mx = SS[:, 48 + b: 49 + b]
            sm = SS[:, 56 + b: 57 + b]
            Gt = SS[:, 128 + b * 8: 136 + b * 8]
            sel = SS[:, 192 + b * 8: 200 + b * 8]
            m1 = SS[:, 256:257]
            m2 = SS[:, 257:258]
            t1 = MISC[:, 1024:1032]
            t2 = MISC[:, 1032:1040]
            E.wait("v", s["rt"], s["rt"].n - NB + b + 1)
            E.op("v", lambda v, lg=lg, mx=mx: v.reduce_max(mx, lg, AX))
            E.op("v", lambda v, mx=mx: v.tensor_scalar_mul(mx, mx, -1.0),
                 s["smx"])
            E.wait("s", s["smx"])
            E.op("s", lambda sc, lg=lg, pro=pro, mx=mx: sc.activation(
                pro, lg, AF.Exp, bias=mx), s["exp"])
            E.wait("v", s["exp"])
            E.op("v", lambda v, pro=pro, sm=sm: v.tensor_reduce(
                sm, pro, AX, ALU.add))
            E.op("v", lambda v, sm=sm: v.reciprocal(sm, sm))
            E.op("v", lambda v, pro=pro, sm=sm: v.tensor_scalar_mul(
                pro, pro, sm))
            if mode == "l0":
                E.wait("v", s_br, 16)
                E.op("v", lambda v, sel=sel, pro=pro: v.tensor_add(
                    sel, pro, BR[:, 0:ne]))
            else:
                E.op("v", lambda v, sel=sel, pro=pro: v.tensor_copy(sel, pro))
            E.op("v", lambda v, sel=sel, m1=m1: v.reduce_max(m1, sel, AX))
            E.op("v", lambda v, sel=sel, m1=m1, t1=t1: v.tensor_single_scalar(
                t1, sel, m1, ALU.is_equal))
            E.op("v", lambda v, sel=sel, t1=t1, t2=t2: v.scalar_tensor_tensor(
                t2, t1, -1e30, sel, ALU.mult, ALU.add))
            E.op("v", lambda v, t2=t2, m2=m2: v.reduce_max(m2, t2, AX))
            E.op("v", lambda v, t2=t2, m2=m2: v.tensor_single_scalar(
                t2, t2, m2, ALU.is_equal))
            E.op("v", lambda v, t1=t1, t2=t2: v.tensor_add(t1, t1, t2))
            E.op("v", lambda v, Gt=Gt, pro=pro, t1=t1: v.tensor_mul(
                Gt, pro, t1))
            if mode == "l0":
                E.op("v", lambda v, Gt=Gt: v.tensor_scalar_mul(Gt, Gt, 0.5),
                     s["gv"])
            else:
                E.op("v", lambda v, m1=m1, m2=m2: v.tensor_add(m1, m1, m2))
                E.op("v", lambda v, m1=m1: v.reciprocal(m1, m1))
                E.op("v", lambda v, m1=m1: v.tensor_scalar_mul(m1, m1, 0.5))
                E.op("v", lambda v, Gt=Gt, m1=m1: v.tensor_scalar_mul(
                    Gt, Gt, m1), s["gv"])
        E.wait("t", s["gv"])
        gev_base = s["gev"].n
        for b in range(NB):
            bank = P[6 + b % 2]
            if b >= 2:
                E.wait("t", s["gev"], gev_base + b - 1)
            E.op("t", lambda t, bb=b, bk=bank: t.transpose(
                bk[0:8, 0:128],
                SS[:, 128 + bb * 8: 136 + bb * 8], IDF[:, :]),
                s["gtp"])
            E.wait("v", s["gtp"])
            E.op("v", lambda v, bb=b, bk=bank: v.tensor_copy(
                GROW[:, bb * 128:(bb + 1) * 128], bk[0:8, 0:128]), s["gev"])
        E.wait("g", s["gev"])
        E.op("g", lambda g, kk=k: g.dma_start(
            Gf.ap()[:, kk * TC:(kk + 1) * TC], GROW[:, :]), s["gst"], 16)
        E.pump("g", 2)
        E.wait("g", s["gst"])
        gbe_k = gbe_i

        def emit_gb_load(e, kk):
            nonlocal_i = gbe_k + e
            par = nonlocal_i % 2
            if nonlocal_i >= 2:
                fs_sem, fs_val = ft_gb[nonlocal_i - 2]
                E.wait("g", fs_sem, fs_val)
            E.op("g", lambda g, ee=e, k2=kk, pp=par: g.dma_start(
                GB[:, pp * 1024:(pp + 1) * 1024],
                bass.AP(Gf, ee * S + k2 * TC, [[0, 128], [1, TC]])),
                s_gb[par], 16)
            E.pump("g", 2)

        emit_gb_load(0, k)
        emit_gb_load(1, k)
        gbe_i += NE

        units_chunk = []
        for e in exps:
            fdim = shared["F"] if e == "S" else F
            for sl in range(nfs(e)):
                if e == "S":
                    units_chunk.append((cfg["swT"][0], cfg["swT"][1], sl, fdim))
                else:
                    units_chunk.append((cfg["wgT"][e], cfg["wuT"][e], sl, fdim))
        unit_meta = {}
        next_load = [0]
        unit_base = slab_i

        def load_next_unit():
            if next_load[0] >= len(units_chunk):
                return
            u = unit_base + next_load[0]
            src_g, src_u, sl, fdim = units_chunk[next_load[0]]
            par = u % 2
            if u >= 2:
                fs_sem, fs_val = ft_slab[u - 2]
                E.wait("y", fs_sem, fs_val)
            for m, src in ((0, src_g), (1, src_u)):
                E.op("y", lambda y, a=src, ss=sl, p=par, mm=m, f=fdim: y.dma_start(
                    WA[:, WUP_OFF[2 * p + mm]:WUP_OFF[2 * p + mm] + 4096]
                    .rearrange("p (d c) -> p d c", d=8),
                    a.ap().rearrange("(c p r) -> p c r", p=128, r=f)
                    [:, :, ss * 512:(ss + 1) * 512]),
                    s_wu[par], 16)
            unit_meta[u] = (par, s_wu[par].n)
            next_load[0] += 1

        load_next_unit()
        load_next_unit()
        unit_idx = unit_base
        for ei, e in enumerate(exps):
            expert_unit0 = unit_idx
            for fs in range(nfc(e)):
                sl = fs // 4
                slpar, slneed = unit_meta[expert_unit0 + sl]
                off_g = WUP_OFF[2 * slpar + 0]
                off_u = WUP_OFF[2 * slpar + 1]
                for tg in range(NTG):
                    pp = pair_i % 2
                    bg, bu = P[2 + pp * 2], P[3 + pp * 2]
                    if pair_i >= 2:
                        E.wait("t", s["ue"], pair_i - 1)
                    if fs == 0 and tg == 0:
                        E.wait("t", s["ye"])
                        if ei == 0:
                            E.wait("t", s["exp"])
                    E.wait("t", s_wu[slpar], slneed)
                    for c in range(DCH):
                        E.op("t", lambda t, cc=c, fss=fs, tgg=tg, o=off_g, bk=bg: t.matmul(
                            bk[:, :],
                            WA[:, o + cc * 512 + (fss % 4) * 128:
                               o + cc * 512 + (fss % 4) * 128 + 128],
                            XF[:, cc * TC + tgg * 512: cc * TC + (tgg + 1) * 512],
                            start=(cc == 0), stop=(cc == DCH - 1)),
                            s["mg"] if c == DCH - 1 else None)
                    for c in range(DCH):
                        E.op("t", lambda t, cc=c, fss=fs, tgg=tg, o=off_u, bk=bu: t.matmul(
                            bk[:, :],
                            WA[:, o + cc * 512 + (fss % 4) * 128:
                               o + cc * 512 + (fss % 4) * 128 + 128],
                            XF[:, cc * TC + tgg * 512: cc * TC + (tgg + 1) * 512],
                            start=(cc == 0), stop=(cc == DCH - 1)),
                            s["mu"] if c == DCH - 1 else None)
                    if tg == NTG - 1 and (fs % 4 == 3 or fs == nfc(e) - 1):
                        ft_slab.append((s["mu"], s["mu"].n))
                        slab_i += 1
                        unit_idx += 1
                        load_next_unit()
                    E.wait("s", s["mg"])
                    if pair_i >= 2:
                        E.wait("s", s["ue"], pair_i - 1)
                    E.op("s", lambda sc, bk=bg, pp=pp: sc.activation(
                        GS[:, pp * 512:(pp + 1) * 512], bk[:, :], AF.Silu),
                        s["sil"])
                    ht = HH[:, fs * TC + tg * 512: fs * TC + (tg + 1) * 512]
                    E.wait("v", s["mu"])
                    E.wait("v", s["sil"])
                    if fs == 0 and tg == 0:
                        E.wait("v", s["dmm"])
                    if e != "S":
                        gbe = gbe_k + e
                        E.wait("v", s_gb[gbe % 2], (gbe // 2 + 1) * 16)
                    if e != "S":
                        E.op("v", lambda v, bk=bu, pp=pp, ht=ht: v.tensor_mul(
                            ht, bk[:, :], GS[:, pp * 512:(pp + 1) * 512]))
                        E.op("v", lambda v, ht=ht, gbs=(gbe_k + e) % 2, tgg=tg:
                             v.tensor_mul(ht, ht,
                                          GB[:, gbs * 1024 + tgg * 512:
                                             gbs * 1024 + (tgg + 1) * 512]),
                             s["ue"])
                    else:
                        E.op("v", lambda v, bk=bu, pp=pp, ht=ht: v.tensor_mul(
                            ht, bk[:, :], GS[:, pp * 512:(pp + 1) * 512]),
                            s["ue"])
                    E.pump("s", 1)
                    pair_i += 1
            if e != "S":
                ft_gb.append((s["ue"], s["ue"].n))
                if e + 2 < NE:
                    emit_gb_load(e + 2, k)
            E.wait("t", s["ue"])
            for ps_ in range(2):
                for fc in range(nfc(e)):
                    par = dfc_i % 2
                    if dfc_i >= 2:
                        fs_sem, fs_val = ft_wd[dfc_i - 2]
                        E.wait("y", fs_sem, fs_val)
                    src = cfg["sw2T"] if e == "S" else cfg["wdT"][e]
                    E.op("y", lambda y, a=src, f=fc, p=par: y.dma_start(
                        WA[:, WD_OFF[p]:WD_OFF[p] + 1024],
                        a.ap().rearrange("(c p r) -> c p r", p=128, r=1024)[f]),
                        s_wd[par], 16)
                    need = s_wd[par].n
                    E.wait("t", s_wd[par], need)
                    if fc == 0:
                        E.wait("t", s["ye"])
                        E.wait("t", s["tpe"], (k + 1) * NB * DCH)
                        E.wait("t", s["gev"])
                    last_fc = (fc == nfc(e) - 1)
                    for ts_ in range(4):
                        tsub = ps_ * 4 + ts_
                        for half in range(2):
                            bk = P[2 * ts_ + half]
                            if last_fc and half == 1:
                                msem = s["dmm"]
                            elif ts_ == 3 and half == 1:
                                msem = s["wdf"]
                            else:
                                msem = None
                            E.op("t", lambda t, f=fc, tb=tsub, h=half, bk=bk,
                                 p=par, e_=e: t.matmul(
                                bk[:, :],
                                HH[:, f * TC + tb * 128: f * TC + (tb + 1) * 128],
                                WA[:, WD_OFF[p] + h * 512: WD_OFF[p] + (h + 1) * 512],
                                start=(f == 0), stop=(f == nfc(e_) - 1)),
                                msem)
                    if last_fc:
                        ft_wd.append((s["dmm"], s["dmm"].n))
                    else:
                        ft_wd.append((s["wdf"], s["wdf"].n))
                    dfc_i += 1
                for ts_ in range(4):
                    tsub = ps_ * 4 + ts_
                    E.wait("v", s["dmm"], s["dmm"].n - 4 + ts_ + 1)
                    if ei == 0 and ps_ == 0 and ts_ == 0 and k > 0:
                        E.wait("v", s["yst"], k * 16)
                    for half in range(2):
                        bk = P[2 * ts_ + half]
                        yy = YY[:, tsub * 1024 + half * 512:
                                tsub * 1024 + (half + 1) * 512]
                        if ei == 0:
                            E.op("v", lambda v, bk=bk, yy=yy: v.tensor_copy(
                                yy, bk[:, :]), s["ye"])
                        elif e == "S":
                            E.op("v", lambda v, bk=bk, yy=yy: v.scalar_tensor_tensor(
                                yy, bk[:, :], 0.5, yy, ALU.mult, ALU.add),
                                s["ye"])
                        else:
                            E.op("v", lambda v, bk=bk, yy=yy: v.tensor_add(
                                yy, yy, bk[:, :]), s["ye"])

        E.wait("g", s["ye"])
        E.op("g", lambda g, kk=k: g.dma_start(
            R.ap().rearrange("(k b p) d -> k p b d", p=128, b=NB)[kk],
            YY[:, :].rearrange("p (b d) -> p b d", b=NB),
            accum_op=ALU.add), s["yst"], 16)
        E.pump("g", 2)

    E.wait("g", s["yst"])
    return s["yst"]


def moe_weight_scratch(E, tag, F, NE, shared_F, names):
    nc = E.nc
    dr = lambda n, sh, dt: nc.dram_tensor(n, sh, dt)
    t = {}
    casts, tps = [], []
    t["gw"] = names["gw"]
    t["wgT"], t["wuT"], t["wdT"] = [], [], []
    for e in range(NE):
        for key, lst, R, C in (("w1", t["wgT"], F, 1024),
                               ("w3", t["wuT"], F, 1024),
                               ("w2", t["wdT"], 1024, F)):
            bf = dr(f"{tag}_{key}{e}_bf", [R, C], BF16)
            wt = dr(f"{tag}_{key}{e}T", [(C // 128) * 128 * R], BF16)
            casts.append((names[key].ap()[e], bf, R, C))
            tps.append((bf, wt, R, C))
            lst.append(wt)
    if shared_F:
        t["swT"] = []
        for key, R, C in (("sw1", shared_F, 1024), ("sw3", shared_F, 1024)):
            bf = dr(f"{tag}_{key}_bf", [R, C], BF16)
            wt = dr(f"{tag}_{key}T", [(C // 128) * 128 * R], BF16)
            casts.append((names[key].ap(), bf, R, C))
            tps.append((bf, wt, R, C))
            t["swT"].append(wt)
        bf = dr(f"{tag}_sw2_bf", [1024, shared_F], BF16)
        wt = dr(f"{tag}_sw2T", [(shared_F // 128) * 128 * 1024], BF16)
        casts.append((names["sw2"].ap(), bf, 1024, shared_F))
        tps.append((bf, wt, 1024, shared_F))
        t["sw2T"] = wt
    return t, casts, tps


def moe_cfg_l0(E, w):
    t, casts, tps = moe_weight_scratch(
        E, "l0f", 1024, 8, 2048,
        dict(gw=w["gate_w"], w1=w["w1"], w3=w["w3"], w2=w["w2"],
             sw1=w["sw1"], sw3=w["sw3"], sw2=w["sw2"]))
    cfg = dict(tag="l0f", F=1024, n_exp=8, gate_mode="l0",
               shared=dict(F=2048), gw=t["gw"], wgT=t["wgT"], wuT=t["wuT"],
               wdT=t["wdT"], swT=t["swT"], sw2T=t["sw2T"],
               gate_bias=w["gate_bias"])
    return cfg, casts, tps


def moe_cfg_l1(E, w):
    t, casts, tps = moe_weight_scratch(
        E, "l1f", 4096, 8, None,
        dict(gw=w["W_router"], w1=w["W_gate"], w3=w["W_up"], w2=w["W_down"]))
    cfg = dict(tag="l1f", F=4096, n_exp=8, gate_mode="l1",
               shared=None, gw=t["gw"], wgT=t["wgT"], wuT=t["wuT"],
               wdT=t["wdT"])
    return cfg, casts, tps


def build_moe_single(kind):
    nc = bass.Bass("TRN2", target_bir_lowering=False)
    stack = ExitStack()
    E = Emitter(nc, stack)
    with stack:
        alloc_arenas(E)
        xin = nc.dram_tensor("xin", [S, D], F32, kind="ExternalInput")
        out = nc.dram_tensor("out", [S, D], F32, kind="ExternalOutput")
        identf = nc.dram_tensor("identf", [128, 128], F32, kind="ExternalInput")
        w = {}
        if kind == "l0":
            w["gate_w"] = nc.dram_tensor("gate_w", [8, 1024], F32, kind="ExternalInput")
            w["gate_bias"] = nc.dram_tensor("gate_bias", [8], F32, kind="ExternalInput")
            for n, sh in (("w1", [8, 1024, 1024]), ("w3", [8, 1024, 1024]),
                          ("w2", [8, 1024, 1024]), ("sw1", [2048, 1024]),
                          ("sw3", [2048, 1024]), ("sw2", [1024, 2048])):
                w[n] = nc.dram_tensor(n, sh, F32, kind="ExternalInput")
            cfg, casts, tps = moe_cfg_l0(E, w)
        else:
            w["W_router"] = nc.dram_tensor("W_router", [8, 1024], F32, kind="ExternalInput")
            for n, sh in (("W_gate", [8, 4096, 1024]), ("W_up", [8, 4096, 1024]),
                          ("W_down", [8, 1024, 4096])):
                w[n] = nc.dram_tensor(n, sh, F32, kind="ExternalInput")
            cfg, casts, tps = moe_cfg_l1(E, w)
        R = nc.dram_tensor("Rres", [S, D], F32)
        Gf = nc.dram_tensor("Gf", [8, S], F32)
        cfg["R"], cfg["Gf"] = R, Gf

        s_init_y = E.sem("inity")
        s_init_g = E.sem("initg")
        E.op("y", lambda y: y.dma_start(E.t["IDF"][:, :], identf.ap()), s_init_y, 16)
        E.op("g", lambda g: g.dma_start(R.ap(), xin.ap()), s_init_g, 16)
        emit_weight_prep(E, casts, tps)
        E.wait("y", s_init_y)
        E.wait("g", s_init_g)
        E.wait("t", s_init_y)
        E.flush()

        E.begin_phase()
        emit_moe_phase(E, cfg)
        E.flush()
        E.end_phase()

        s_out = E.sem("outc")
        E.op("g", lambda g: g.dma_start(out.ap(), R.ap()), s_out, 16)
        import os
        if os.environ.get("DBGDUMP") == "1":
            dXT = nc.dram_tensor("dXT", [128, NB * 1024], F32, kind="ExternalOutput")
            dXF = nc.dram_tensor("dXF", [128, DCH * TC], F32, kind="ExternalOutput")
            dHH = nc.dram_tensor("dHH", [128, 32 * TC], F32, kind="ExternalOutput")
            dYY = nc.dram_tensor("dYY", [128, NB * 1024], F32, kind="ExternalOutput")
            dGf = nc.dram_tensor("dGf", [8, S], F32, kind="ExternalOutput")
            E.op("g", lambda g: g.dma_start(dXT.ap(), E.t["XT"][:, :]), s_out, 16)
            E.op("g", lambda g: g.dma_start(dXF.ap(), E.t["XF"][:, :]), s_out, 16)
            E.op("g", lambda g: g.dma_start(dHH.ap(), E.t["HH"][:, :]), s_out, 16)
            E.op("g", lambda g: g.dma_start(dYY.ap(), E.t["YY"][:, :]), s_out, 16)
            E.op("g", lambda g: g.dma_start(dGf.ap(), Gf.ap()), s_out, 16)
        E.wait("g", s_out)
        E.flush()
    return nc



def emit_l0attn_phase(E, cfg):
    tag = "l0a"
    R = cfg["R"]
    W = cfg["w"]
    PHIQ, ZBUF, RDH, KSROW = cfg["PHIQ"], cfg["ZBUF"], cfg["RDH"], cfg["KSROW"]

    XT, XF, HH, YY = E.t["XT"], E.t["XF"], E.t["HH"], E.t["YY"]
    WA, GS, GB, SS, MISC = E.t["WA"], E.t["GS"], E.t["GB"], E.t["SS"], E.t["MISC"]
    IDF, CONST = E.t["IDF"], E.t["CONST"]
    ROWS = E.salloc("ROWS", [1, 4096], BF16)
    BIAS = E.salloc("BIAS", [128, 64], F32)
    ONESR = E.salloc("ONESR", [1, 128], BF16)
    ONESC = E.salloc("ONESC", [128, 1], BF16)
    KVS = E.salloc("KVS", [128, 512], BF16)
    P = [E.t[f"P{i}"] for i in range(8)]
    KEPS = 0.001
    NT = NB * DCH

    s = {n: E.sem(f"{tag}_{n}") for n in
         ("x sq var rstd ln tp tpe bld bt bte rows wkv wq zacc zst qm qe "
          "kvm kve acc kevac ks1 ks2 ksst ksld pqst pqld pqtd zld den rdst "
          "num ae wom woe ye yst wo w1 w2 mm mhe dmm wdf").split()}
    s_rb = [E.sem(f"{tag}_rb{i}") for i in range(2)]
    s_w1p = [E.sem(f"{tag}_w1p{i}") for i in range(2)]
    s_w2p = [E.sem(f"{tag}_w2p{i}") for i in range(2)]

    E.op("v", lambda v: v.memset(ONESR[0:1, 0:128], 1.0))
    E.op("v", lambda v: v.memset(ONESC[:, 0:1], 1.0))
    E.op("v", lambda v: v.memset(CONST[:, 2:3], KEPS), s["bte"])
    bias_routes = [
        (cfg["ln1_scale"], 0, 8), (cfg["ln1_bias"], 8, 8),
        (cfg["ln2_scale"], 16, 8), (cfg["ln2_bias"], 24, 8),
        (cfg["b1"], 32, 32),
    ]
    for i, (src, col, w_) in enumerate(bias_routes):
        if i >= 1:
            E.wait("y", s["bt"], i)
        E.op("y", lambda y, a=src, ww=w_: y.dma_start(
            MISC[0:ww, 0:128], a.ap().rearrange("(c p) -> c p", p=128)),
            s["bld"], 16)
        E.wait("t", s["bld"])
        bank = P[i % 2]
        if i >= 2:
            E.wait("t", s["bte"], i)
        E.op("t", lambda t, ww=w_, bk=bank: t.transpose(
            bk[:, 0:ww], MISC[0:ww, 0:128], IDF[0:ww, 0:ww]), s["bt"])
        E.wait("v", s["bt"])
        E.op("v", lambda v, ww=w_, cc=col, bk=bank: v.tensor_copy(
            BIAS[:, cc:cc + ww], bk[:, 0:ww]), s["bte"])
    E.op("g", lambda g: g.dma_start(
        ROWS[0:1, 0:1024], cfg["bk"].ap().rearrange("h m -> (h m)")
        .rearrange("(a b) -> a b", a=1)), s["rows"], 16)
    E.op("g", lambda g: g.dma_start(
        ROWS[0:1, 1024:2048], cfg["bv"].ap().rearrange("h m -> (h m)")
        .rearrange("(a b) -> a b", a=1)), s["rows"], 16)
    E.op("g", lambda g: g.dma_start(
        ROWS[0:1, 3072:4096], cfg["bo"].ap().rearrange("(a b) -> a b", a=1)),
        s["rows"], 16)
    E.op("g", lambda g: g.dma_start(
        ROWS[0:1, 2048:3072], cfg["bq"].ap().rearrange("h m -> (h m)")
        .rearrange("(a b) -> a b", a=1)), s["rows"], 16)
    WQ_OFF = 24576
    for c in range(DCH):
        E.op("y", lambda y, cc=c: y.dma_start(
            WA[:, cc * 2048: cc * 2048 + 1024],
            W["wk"].ap()[cc * 128:(cc + 1) * 128, :]), s["wkv"], 16)
        E.op("y", lambda y, cc=c: y.dma_start(
            WA[:, cc * 2048 + 1024: (cc + 1) * 2048],
            W["wv"].ap()[cc * 128:(cc + 1) * 128, :]), s["wkv"], 16)
        E.op("y", lambda y, cc=c: y.dma_start(
            HH[:, WQ_OFF + cc * 1024: WQ_OFF + (cc + 1) * 1024],
            W["wq"].ap()[cc * 128:(cc + 1) * 128, :]), s["wq"], 16)

    PQ_OFF, PK_OFF, PV_OFF = 0, 8192, 16384
    kv_started = {0: False, 64: False}
    ks_started = [False]

    for k in range(NCH):
        if k >= 1:
            E.wait("g", s["tp"], k * NT)
        E.op("g", lambda g, kk=k: g.dma_start(
            XT[:, :].rearrange("p (b d) -> p b d", b=NB),
            R.ap().rearrange("(k b p) d -> k p b d", p=128, b=NB)[kk]),
            s["x"], 16)
        E.pump("g", 2)
        if k == 0:
            E.wait("s", s["bt"], 5)
        emit_ln(E, (s["x"], s["sq"], s["var"], s["rstd"], s["ln"]), k, 1e-5)
        E.wait("g", s["ln"])
        E.op("g", lambda g, kk=k: g.dma_start(
            R.ap().rearrange("(k b p) d -> k p b d", p=128, b=NB)[kk],
            XT[:, :].rearrange("p (b d) -> p b d", b=NB),
            accum_op=ALU.add), s["zacc"], 16)
        E.op("g", lambda g, kk=k: g.dma_start(
            ZBUF.ap().rearrange("(k b p) d -> k p b d", p=128, b=NB)[kk],
            XT[:, :].rearrange("p (b d) -> p b d", b=NB)), s["zst"], 16)
        E.pump("g", 2)
        E.wait("v", s["zacc"])
        E.wait("s", s["zacc"])
        emit_ln(E, (s["zst"], s["sq"], s["var"], s["rstd"], s["ln"]), k, 1e-6)
        for ti in range(NT):
            bank = (P[0], P[1])[ti % 2]
            b, c = ti // DCH, ti % DCH
            E.wait("t", s["ln"], 2 * k * NB + NB + b + 1)
            if ti >= 2:
                E.wait("t", s["tpe"], k * NT + ti - 1)
            else:
                E.wait("t", s["qe"])
                E.wait("t", s["bte"])
            E.op("t", lambda t, bb=b, cc=c, bk=bank: t.transpose(
                bk[:, 0:128],
                XT[:, bb * 1024 + cc * 128: bb * 1024 + (cc + 1) * 128],
                IDF[:, :]), s["tp"])
            E.wait("v", s["tp"])
            E.op("v", lambda v, bb=b, cc=c, bk=bank: v.tensor_scalar(
                XF[:, cc * TC + bb * 128: cc * TC + (bb + 1) * 128],
                bk[:, 0:128], BIAS[:, cc:cc + 1], BIAS[:, 8 + cc:9 + cc],
                ALU.mult, ALU.add), s["tpe"])
        E.wait("t", s["wkv"])
        E.wait("t", s["wq"])
        E.wait("t", s["rows"], 64)
        for b in range(NB):
            if b == 0:
                E.wait("t", s["tpe"], (k + 1) * NT)
            else:
                E.wait("t", s["qe"])
            for g_ in range(2):
                bank = P[g_]
                for c in range(DCH):
                    E.op("t", lambda t, bb=b, cc=c, gg=g_, bk=bank: t.matmul(
                        bk[:, :],
                        XF[:, cc * TC + bb * 128: cc * TC + (bb + 1) * 128],
                        HH[:, WQ_OFF + cc * 1024 + gg * 512:
                           WQ_OFF + cc * 1024 + (gg + 1) * 512],
                        start=(cc == 0), stop=False))
                E.op("t", lambda t, gg=g_, bk=bank: t.matmul(
                    bk[:, :], ONESR[0:1, 0:128],
                    ROWS[0:1, 2048 + gg * 512: 2048 + (gg + 1) * 512],
                    start=False, stop=True), s["qm"])
            if b == 0:
                E.wait("s", s["pqst"])
            for g_ in range(2):
                E.wait("s", s["qm"], s["qm"].n - 2 + g_ + 1)
                E.op("s", lambda sc, bb=b, gg=g_, bk=P[g_]: sc.activation(
                    HH[:, PQ_OFF + bb * 1024 + gg * 512:
                       PQ_OFF + bb * 1024 + (gg + 1) * 512],
                    bk[:, :], AF.Relu))
                E.op("s", lambda sc, bb=b, gg=g_: sc.add(
                    HH[:, PQ_OFF + bb * 1024 + gg * 512:
                       PQ_OFF + bb * 1024 + (gg + 1) * 512],
                    HH[:, PQ_OFF + bb * 1024 + gg * 512:
                       PQ_OFF + bb * 1024 + (gg + 1) * 512],
                    CONST[:, 2:3]), s["qe"])
            E.wait("t", s["kve"])
            for g_ in range(4):
                bank = P[2 + g_]
                for c in range(DCH):
                    E.op("t", lambda t, bb=b, cc=c, gg=g_, bk=bank: t.matmul(
                        bk[:, :],
                        XF[:, cc * TC + bb * 128: cc * TC + (bb + 1) * 128],
                        WA[:, cc * 2048 + gg * 512:cc * 2048 + (gg + 1) * 512],
                        start=(cc == 0), stop=False))
                E.op("t", lambda t, gg=g_, bk=bank: t.matmul(
                    bk[:, :], ONESR[0:1, 0:128],
                    ROWS[0:1, gg * 512:(gg + 1) * 512],
                    start=False, stop=True), s["kvm"])
            if b == 0:
                E.wait("s", s["acc"])
            for g_ in range(4):
                E.wait("s", s["kvm"], s["kvm"].n - 4 + g_ + 1)
                if g_ < 2:
                    E.op("s", lambda sc, bb=b, gg=g_, bk=P[2 + g_]: sc.activation(
                        HH[:, PK_OFF + bb * 1024 + gg * 512:
                           PK_OFF + bb * 1024 + (gg + 1) * 512],
                        bk[:, :], AF.Relu))
                    E.op("s", lambda sc, bb=b, gg=g_: sc.add(
                        HH[:, PK_OFF + bb * 1024 + gg * 512:
                           PK_OFF + bb * 1024 + (gg + 1) * 512],
                        HH[:, PK_OFF + bb * 1024 + gg * 512:
                           PK_OFF + bb * 1024 + (gg + 1) * 512],
                        CONST[:, 2:3]), s["kve"])
                else:
                    E.op("s", lambda sc, bb=b, gg=g_, bk=P[2 + g_]: sc.copy(
                        HH[:, PV_OFF + bb * 1024 + (gg - 2) * 512:
                           PV_OFF + bb * 1024 + (gg - 1) * 512],
                        bk[:, :]), s["kve"])
            E.wait("t", s["kve"])
            for h in range(16):
                pb_ = 64 * (h % 2)
                st_ = not kv_started[pb_]
                kv_started[pb_] = True
                E.op("t", lambda t, bb=b, hh=h, st2=st_: t.matmul(
                    P[7][64 * (hh % 2):64 * (hh % 2) + 64,
                         (hh // 2) * 64:(hh // 2) * 64 + 64],
                    HH[:, PK_OFF + bb * 1024 + hh * 64:
                       PK_OFF + bb * 1024 + (hh + 1) * 64],
                    HH[:, PV_OFF + bb * 1024 + hh * 64:
                       PV_OFF + bb * 1024 + (hh + 1) * 64],
                    start=st2, stop=False, skip_group_check=True), s["acc"])
            for hs in range(DCH):
                st_ = not ks_started[0]
                ks_started[0] = True
                E.op("t", lambda t, bb=b, ss2=hs, st2=st_: t.matmul(
                    P[6][:, ss2:ss2 + 1],
                    HH[:, PK_OFF + bb * 1024 + ss2 * 128:
                       PK_OFF + bb * 1024 + (ss2 + 1) * 128],
                    ONESC[:, 0:1],
                    start=st2, stop=False, skip_group_check=True), s["acc"])
        E.wait("g", s["qe"])
        E.op("g", lambda g, kk=k: g.dma_start(
            PHIQ.ap().rearrange("(k b p) d -> k p b d", p=128, b=NB)[kk],
            HH[:, PQ_OFF:PQ_OFF + 8192].rearrange("p (b d) -> p b d", b=NB)),
            s["pqst"], 16)
        E.pump("g", 2)

    E.wait("v", s["acc"])
    E.op("v", lambda v: v.tensor_copy(KVS[:, 0:512], P[7][:, 0:512]))
    E.op("v", lambda v: v.tensor_copy(SS[:, 64:72], P[6][:, 0:8]), s["kevac"])
    E.wait("t", s["kevac"])
    E.op("t", lambda t: t.transpose(P[0][0:8, 0:128], SS[:, 64:72], IDF[:, :]),
         s["ks1"])
    E.wait("v", s["ks1"])
    E.op("v", lambda v: v.tensor_copy(MISC[0:8, 0:128], P[0][0:8, 0:128]),
         s["ks2"])
    E.wait("g", s["ks2"])
    E.op("g", lambda g: g.dma_start(KSROW.ap(), MISC[0:8, 0:128]), s["ksst"], 16)
    E.wait("g", s["ksst"])
    E.op("g", lambda g: g.dma_start(
        GS[:, :], bass.AP(KSROW, 0, [[0, 128], [1, 1024]])), s["ksld"], 16)
    if cfg.get("KVSD") is not None:
        s["kvd"] = E.sem("l0a_kvd")
        E.wait("g", s["kevac"])
        E.op("g", lambda g: g.dma_start(cfg["KVSD"].ap(), KVS[:, :]), s["kvd"], 16)
        E.wait("g", s["kvd"])
    E.wait("y", s["acc"])
    for c in range(DCH):
        E.op("y", lambda y, cc=c: y.dma_start(
            WA[:, cc * 1024:(cc + 1) * 1024],
            W["wo"].ap()[cc * 128:(cc + 1) * 128, :]), s["wo"], 16)

    E.wait("g", s["kvm"])
    E.op("g", lambda g: g.dma_start(
        ROWS[0:1, 0:1024], cfg["b2m"].ap().rearrange("(a b) -> a b", a=1)),
        s["rows"], 16)
    W1_OFF = [8192, 12288]
    W2_OFF = [16384, 17408]
    slab_i = 0
    dfc_i = 0
    rd_i = 0
    ft_w1 = []
    ft_w2 = []
    ft_rb = []
    for k in range(NCH):
        E.wait("g", s["dmm"])
        E.wait("g", s["pqst"])
        E.op("g", lambda g, kk=k: g.dma_start(
            HH[:, PQ_OFF:PQ_OFF + 8192].rearrange("p (b d) -> p b d", b=NB),
            PHIQ.ap().rearrange("(k b p) d -> k p b d", p=128, b=NB)[kk]),
            s["pqld"], 16)
        E.pump("g", 2)
        E.wait("y", s["mm"])
        for c in range(DCH):
            E.op("y", lambda y, kk=k, cc=c: y.dma_start(
                XF[:, cc * TC:(cc + 1) * TC],
                PHIQ.ap()[kk * TC:(kk + 1) * TC, cc * 128:(cc + 1) * 128],
                transpose=True), s["pqtd"], 16)
        E.wait("v", s["pqld"])
        E.wait("v", s["ksld"])
        E.wait("v", s["sq"])
        for b in range(NB):
            if k > 0 or b > 0:
                E.wait("v", s["rdst"])
            E.op("v", lambda v, bb=b: v.tensor_mul(
                MISC[:, 0:1024],
                HH[:, PQ_OFF + bb * 1024:PQ_OFF + (bb + 1) * 1024], GS[:, :]))
            E.op("v", lambda v: v.tensor_reduce(
                SS[:, 80:96], MISC[:, 0:1024].rearrange("p (h m) -> p h m", h=16),
                AX, ALU.add))
            E.op("v", lambda v: v.reciprocal(SS[:, 96:112], SS[:, 80:96]),
                 s["den"])
            E.wait("t", s["den"])
            if k > 0 or b > 0:
                E.wait("t", s["ks2"])
            E.op("t", lambda t: t.transpose(
                P[6][0:16, 0:128], SS[:, 96:112], IDF[:, :]), s["ks1"])
            E.wait("v", s["ks1"])
            E.op("v", lambda v: v.tensor_copy(SS[0:16, 144:272],
                                              P[6][0:16, 0:128]), s["ks2"])
            E.wait("g", s["ks2"])
            E.op("g", lambda g, kk=k, bb=b: g.dma_start(
                RDH.ap()[:, kk * TC + bb * 128: kk * TC + (bb + 1) * 128],
                SS[0:16, 144:272]), s["rdst"], 16)
        E.wait("g", s["rdst"])
        rb_need = {}

        def load_rb(h, kk):
            nonlocal rd_i
            pb = 64 * (h % 2)
            slot = rd_i % 2
            if rd_i >= 2:
                fsem, fval = ft_rb[rd_i - 2]
                E.wait("g", fsem, fval)
            E.op("g", lambda g, hh=h, k2=kk, sl=slot, pb2=pb: g.dma_start(
                GB[pb2:pb2 + 64, sl * 1024:(sl + 1) * 1024],
                bass.AP(RDH, hh * S + k2 * TC, [[0, 64], [1, TC]])),
                s_rb[slot], 16)
            E.pump("g", 1)
            rb_need[h] = (slot, s_rb[slot].n)
            rd_i += 1

        load_rb(0, k)
        load_rb(1, k)
        ae_base = s["ae"].n
        for h in range(16):
            pb = 64 * (h % 2)
            slot, rbv = rb_need[h]
            c = h // 2
            for tg in range(2):
                pi = h * 2 + tg
                bank = P[2 + pi % 4]
                if pi >= 4:
                    E.wait("t", s["ae"], ae_base + pi - 4 + 1)
                else:
                    E.wait("t", s["mhe"])
                E.wait("t", s["pqtd"], (k + 1) * DCH * 16)
                E.op("t", lambda t, hh=h, cc=c, tgg=tg, bk=bank, pb2=pb: t.matmul(
                    bk[pb2:pb2 + 64, :],
                    KVS[64 * (hh % 2):64 * (hh % 2) + 64,
                        (hh // 2) * 64:(hh // 2) * 64 + 64],
                    XF[64 * (hh % 2):64 * (hh % 2) + 64,
                       cc * TC + tgg * 512: cc * TC + (tgg + 1) * 512],
                    start=True, stop=True), s["num"])
                E.wait("v", s["num"])
                E.wait("v", s_rb[slot], rbv)
                E.op("v", lambda v, cc=c, tgg=tg, bk=bank, pb2=pb, sl=slot: v.tensor_mul(
                    XF[pb2:pb2 + 64, cc * TC + tgg * 512: cc * TC + (tgg + 1) * 512],
                    bk[pb2:pb2 + 64, :],
                    GB[pb2:pb2 + 64, sl * 1024 + tgg * 512:
                       sl * 1024 + (tgg + 1) * 512]), s["ae"])
            ft_rb.append((s["ae"], s["ae"].n))
            if h + 2 < 16:
                load_rb(h + 2, k)
        import os as _os2
        if _os2.environ.get("DBGDUMP") == "1" and "DATTN" in E.t:
            s.setdefault("datn", E.sem("l0a_datn"))
            E.wait("g", s["ae"])
            E.op("g", lambda g, kk=k: g.dma_start(
                E.t["DATTN"].ap()[:, kk * 8192:(kk + 1) * 8192], XF[:, :]),
                s["datn"], 16)
            E.wait("t", s["datn"])
        E.op("g", lambda g, kk=k: g.dma_start(
            XT[:, :].rearrange("p (b d) -> p b d", b=NB),
            ZBUF.ap().rearrange("(k b p) d -> k p b d", p=128, b=NB)[kk]),
            s["zld"], 16)
        E.wait("t", s["wo"])
        E.wait("t", s["ae"])
        E.wait("t", s["tpe"])
        E.wait("t", s["ye"])
        for b in range(NB):
            if not (b == 0 and k == 0):
                E.wait("t", s["woe"])
            for g_ in range(2):
                bank = P[g_]
                for c in range(DCH):
                    E.op("t", lambda t, bb=b, cc=c, gg=g_, bk=bank: t.matmul(
                        bk[:, :],
                        XF[:, cc * TC + bb * 128: cc * TC + (bb + 1) * 128],
                        WA[:, cc * 1024 + gg * 512: cc * 1024 + (gg + 1) * 512],
                        start=(cc == 0), stop=False))
                E.op("t", lambda t, gg=g_, bk=bank: t.matmul(
                    bk[:, :], ONESR[0:1, 0:128],
                    ROWS[0:1, 3072 + gg * 512: 3072 + (gg + 1) * 512],
                    start=False, stop=True), s["wom"])
            E.wait("v", s["wom"])
            E.wait("v", s["zld"])
            if b == 0 and k > 0:
                E.wait("v", s["yst"], k * 16)
            for g_ in range(2):
                bank = P[g_]
                yy = YY[:, b * 1024 + g_ * 512: b * 1024 + (g_ + 1) * 512]
                xt_ = XT[:, b * 1024 + g_ * 512: b * 1024 + (g_ + 1) * 512]
                E.op("v", lambda v, bk=bank, yy=yy: v.tensor_copy(yy, bk[:, :]))
                E.op("v", lambda v, bk=bank, xt2=xt_: v.tensor_add(
                    xt2, xt2, bk[:, :]), s["woe"])
        import os as _os
        if _os.environ.get("DBGDUMP") == "1" and "DXRES" in E.t:
            s.setdefault("dxr", E.sem("l0a_dxr"))
            E.wait("g", s["woe"])
            E.op("g", lambda g, kk=k: g.dma_start(
                E.t["DXRES"].ap().rearrange("(k b p) d -> k p b d", p=128, b=NB)[kk],
                XT[:, :].rearrange("p (b d) -> p b d", b=NB)), s["dxr"], 16)
            E.wait("v", s["dxr"])
            E.wait("s", s["dxr"])
        emit_ln(E, (s["woe"], s["sq"], s["var"], s["rstd"], s["ln"]), k, 1e-6)
        for ti in range(NT):
            bank = (P[0], P[1])[ti % 2]
            b, c = ti // DCH, ti % DCH
            E.wait("t", s["ln"], (2 * NCH + k) * NB + b + 1)
            if ti >= 2:
                E.wait("t", s["tpe"], (NCH + k) * NT + ti - 1)
            else:
                E.wait("t", s["woe"])
            E.op("t", lambda t, bb=b, cc=c, bk=bank: t.transpose(
                bk[:, 0:128],
                XT[:, bb * 1024 + cc * 128: bb * 1024 + (cc + 1) * 128],
                IDF[:, :]), s["tp"])
            E.wait("v", s["tp"])
            E.op("v", lambda v, bb=b, cc=c, bk=bank: v.tensor_scalar(
                XF[:, cc * TC + bb * 128: cc * TC + (bb + 1) * 128],
                bk[:, 0:128], BIAS[:, 16 + cc:17 + cc], BIAS[:, 24 + cc:25 + cc],
                ALU.mult, ALU.add), s["tpe"])
        w1need = 0
        slab_par = 0
        for fs in range(32):
            if fs % 4 == 0:
                sl = fs // 4
                par = slab_i % 2
                if slab_i >= 2:
                    fsem, fval = ft_w1[slab_i - 2]
                    E.wait("y", fsem, fval)
                E.op("y", lambda y, ss=sl, p=par: y.dma_start(
                    WA[:, W1_OFF[p]:W1_OFF[p] + 4096]
                    .rearrange("p (d c) -> p d c", d=8),
                    W["w1"].ap().rearrange("(d p) f -> p d f", p=128)
                    [:, :, ss * 512:(ss + 1) * 512]), s_w1p[par], 16)
                w1need = s_w1p[par].n
                slab_par = par
                slab_i += 1
            for tg in range(NTG):
                pi = fs * 2 + tg
                gpi = k * 64 + pi
                bank = P[2 + pi % 4]
                if gpi >= 4:
                    E.wait("t", s["mhe"], gpi - 4 + 1)
                else:
                    E.wait("t", s["ae"])
                E.wait("t", s_w1p[slab_par], w1need)
                if pi == 0:
                    E.wait("t", s["tpe"], (NCH + k + 1) * NT)
                for c in range(DCH):
                    E.op("t", lambda t, cc=c, fss=fs, tgg=tg, p=slab_par,
                         bk=bank: t.matmul(
                        bk[:, :],
                        WA[:, W1_OFF[p] + cc * 512 + (fss % 4) * 128:
                           W1_OFF[p] + cc * 512 + (fss % 4) * 128 + 128],
                        XF[:, cc * TC + tgg * 512: cc * TC + (tgg + 1) * 512],
                        start=(cc == 0), stop=(cc == DCH - 1)),
                        s["mm"] if c == DCH - 1 else None)
                if tg == NTG - 1 and fs % 4 == 3:
                    ft_w1.append((s["mm"], s["mm"].n))
                E.wait("s", s["mm"])
                if gpi >= 4:
                    E.wait("s", s["mhe"], gpi - 4 + 1)
                if pi == 0:
                    E.wait("s", s["dmm"])
                E.op("s", lambda sc, fss=fs, tgg=tg, bk=bank: sc.activation(
                    HH[:, fss * TC + tgg * 512: fss * TC + (tgg + 1) * 512],
                    bk[:, :], AF.Gelu_apprx_tanh,
                    bias=BIAS[:, 32 + fss:33 + fss]), s["mhe"])
        E.wait("t", s["mhe"])
        for ps_ in range(2):
            for fc in range(32):
                par = dfc_i % 2
                if dfc_i >= 2:
                    fsem, fval = ft_w2[dfc_i - 2]
                    E.wait("y", fsem, fval)
                E.op("y", lambda y, f=fc, p=par: y.dma_start(
                    WA[:, W2_OFF[p]:W2_OFF[p] + 1024],
                    W["w2"].ap()[f * 128:(f + 1) * 128, :]), s_w2p[par], 16)
                need = s_w2p[par].n
                E.wait("t", s_w2p[par], need)
                if fc == 0:
                    E.wait("t", s["ye"])
                    E.wait("t", s["woe"])
                    E.wait("t", s["ks2"])
                last_fc = (fc == 31)
                if fc == 0:
                    E.wait("t", s["rows"], 80)
                for ts_ in range(4):
                    tsub = ps_ * 4 + ts_
                    for half in range(2):
                        bk = P[2 * ts_ + half]
                        msem = s["wdf"] if (ts_ == 3 and half == 1) else None
                        E.op("t", lambda t, f=fc, tb=tsub, h=half, bk=bk,
                             p=par: t.matmul(
                            bk[:, :],
                            HH[:, f * TC + tb * 128: f * TC + (tb + 1) * 128],
                            WA[:, W2_OFF[p] + h * 512: W2_OFF[p] + (h + 1) * 512],
                            start=(f == 0), stop=False), msem)
                if last_fc:
                    for ts_ in range(4):
                        for half in range(2):
                            bk = P[2 * ts_ + half]
                            E.op("t", lambda t, h=half, bk=bk: t.matmul(
                                bk[:, :], ONESR[0:1, 0:128],
                                ROWS[0:1, h * 512:(h + 1) * 512],
                                start=False, stop=True),
                                s["dmm"] if half == 1 else None)
                    ft_w2.append((s["dmm"], s["dmm"].n))
                else:
                    ft_w2.append((s["wdf"], s["wdf"].n))
                dfc_i += 1
            for ts_ in range(4):
                tsub = ps_ * 4 + ts_
                E.wait("v", s["dmm"], s["dmm"].n - 4 + ts_ + 1)
                for half in range(2):
                    bk = P[2 * ts_ + half]
                    yy = YY[:, tsub * 1024 + half * 512:
                            tsub * 1024 + (half + 1) * 512]
                    E.op("v", lambda v, bk=bk, yy=yy: v.tensor_add(
                        yy, yy, bk[:, :]), s["ye"])
        E.wait("g", s["ye"])
        E.op("g", lambda g, kk=k: g.dma_start(
            R.ap().rearrange("(k b p) d -> k p b d", p=128, b=NB)[kk],
            YY[:, :].rearrange("p (b d) -> p b d", b=NB),
            accum_op=ALU.add), s["yst"], 16)
        E.pump("g", 2)
    E.wait("g", s["yst"])
    return s

def build_l0attn_single():
    nc = bass.Bass("TRN2", target_bir_lowering=False)
    stack = ExitStack()
    E = Emitter(nc, stack)
    with stack:
        alloc_arenas(E)
        xin = nc.dram_tensor("xin", [S, D], F32, kind="ExternalInput")
        out = nc.dram_tensor("out", [S, D], F32, kind="ExternalOutput")
        identf = nc.dram_tensor("identf", [128, 128], F32, kind="ExternalInput")
        ext = {}
        for n, sh in (("Wq", [1024, 16, 64]), ("bq", [16, 64]),
                      ("Wk", [1024, 16, 64]), ("bk", [16, 64]),
                      ("Wv", [1024, 16, 64]), ("bv", [16, 64]),
                      ("Wo", [16, 64, 1024]), ("bo", [1024]),
                      ("ln1_scale", [1024]), ("ln1_bias", [1024]),
                      ("ln2_scale", [1024]), ("ln2_bias", [1024]),
                      ("W1", [1024, 4096]), ("b1", [4096]),
                      ("W2", [4096, 1024]), ("b2", [1024])):
            ext[n] = nc.dram_tensor(n, sh, F32, kind="ExternalInput")
        wbf = {}
        casts = []
        for n, src_ap, sh in (
            ("wq", ext["Wq"].ap().rearrange("a h m -> a (h m)"), [1024, 1024]),
            ("wk", ext["Wk"].ap().rearrange("a h m -> a (h m)"), [1024, 1024]),
            ("wv", ext["Wv"].ap().rearrange("a h m -> a (h m)"), [1024, 1024]),
            ("wo", ext["Wo"].ap().rearrange("h m d -> (h m) d"), [1024, 1024]),
            ("w1", ext["W1"].ap(), [1024, 4096]),
            ("w2", ext["W2"].ap(), [4096, 1024]),
        ):
            wbf[n] = nc.dram_tensor(f"l0a_{n}_bf", sh, BF16)
            casts.append((src_ap, wbf[n]))
        R = nc.dram_tensor("Rres", [S, D], F32)
        import os as _os0
        DXRES = nc.dram_tensor("DXRES", [S, D], F32)
        KVSD = nc.dram_tensor("KVSD", [128, 512], F32)
        PHIQ = nc.dram_tensor("PHIQ", [S, D], BF16)
        ZBUF = nc.dram_tensor("ZBUF", [S, D], F32)
        RDH = nc.dram_tensor("RDH", [16, S], F32)
        KSROW = nc.dram_tensor("KSROW", [8, 128], F32)
        cfg = dict(R=R, PHIQ=PHIQ, ZBUF=ZBUF, RDH=RDH, KSROW=KSROW,
                   w=wbf,
                   ln1_scale=ext["ln1_scale"], ln1_bias=ext["ln1_bias"],
                   ln2_scale=ext["ln2_scale"], ln2_bias=ext["ln2_bias"],
                   b1=ext["b1"],
                   bk=ext["bk"], bv=ext["bv"], bq=ext["bq"],
                   bo=ext["bo"], b2m=ext["b2"])

        s_init_y = E.sem("inity")
        s_init_g = E.sem("initg")
        E.op("y", lambda y: y.dma_start(E.t["IDF"][:, :], identf.ap()),
             s_init_y, 16)
        E.op("g", lambda g: g.dma_start(R.ap(), xin.ap()), s_init_g, 16)
        emit_weight_prep(E, casts, [])
        E.wait("y", s_init_y)
        E.wait("g", s_init_g)
        E.wait("t", s_init_y)
        E.flush()

        E.begin_phase()
        if _os0.environ.get("DBGDUMP") == "1":
            E.t["DXRES"] = DXRES
            E.t["DATTN"] = nc.dram_tensor("DATTN", [128, 4 * 8192], F32)
            cfg["KVSD"] = KVSD
        emit_l0attn_phase(E, cfg)
        E.flush()
        E.end_phase()

        s_out = E.sem("outc")
        E.op("g", lambda g: g.dma_start(out.ap(), R.ap()), s_out, 16)
        import os as _os
        if _os.environ.get("DBGDUMP") == "1":
            dkv = nc.dram_tensor("dkv", [128, 512], F32, kind="ExternalOutput")
            datn = nc.dram_tensor("datn", [128, 4 * 8192], F32, kind="ExternalOutput")
            E.op("g", lambda g: g.dma_start(datn.ap(), E.t["DATTN"].ap()), s_out, 16)
            dxr2 = nc.dram_tensor("dxr2", [S, D], F32, kind="ExternalOutput")
            E.op("g", lambda g: g.dma_start(dxr2.ap(), E.t2_DXRES.ap() if False else DXRES.ap()), s_out, 16)
            E.op("g", lambda g: g.dma_start(dkv.ap(), KVSD.ap()), s_out, 16)
            dz = nc.dram_tensor("dz", [S, D], F32, kind="ExternalOutput")
            dpq = nc.dram_tensor("dpq", [S, D], F32, kind="ExternalOutput")
            drd = nc.dram_tensor("drd", [16, S], F32, kind="ExternalOutput")
            dks = nc.dram_tensor("dks", [8, 128], F32, kind="ExternalOutput")
            E.op("g", lambda g: g.dma_start(dz.ap(), ZBUF.ap()), s_out, 16)
            E.op("g", lambda g: g.dma_start(dpq.ap(), PHIQ.ap()), s_out, 16)
            E.op("g", lambda g: g.dma_start(drd.ap(), RDH.ap()), s_out, 16)
            E.op("g", lambda g: g.dma_start(dks.ap(), KSROW.ap()), s_out, 16)
        E.wait("g", s_out)
        E.flush()
    return nc

def emit_l1attn_phase(E, cfg):
    tag = "l1a"
    R = cfg["R"]
    UVT, OWT = cfg["uvT"], cfg["owT"]
    UF, VTOK, WPAD = cfg["UF"], cfg["VT"], cfg["WPAD"]

    XT, XF, HH, YY = E.t["XT"], E.t["XF"], E.t["HH"], E.t["YY"]
    WA, GS, GB, SS, MISC = E.t["WA"], E.t["GS"], E.t["GB"], E.t["SS"], E.t["MISC"]
    IDF, CONST = E.t["IDF"], E.t["CONST"]
    BIAS = E.salloc("BIAS", [128, 8], F32)
    GBR = E.salloc("GBR", [128, 640], F32)
    TMP = E.salloc("TMP", [128, 256], F32)
    VSTG = E.salloc("VSTG", [128, 2048], BF16)
    P = [E.t[f"P{i}"] for i in range(8)]
    NT = NB * DCH
    SQS = 0.08838834764831845

    s = {n: E.sem(f"{tag}_{n}") for n in
         ("x sq var rstd ln fac t1 t1st tp tpe gld wp1 wp2 rcs "
          "um ue ust vm ve vst bm be rop rtp rte uvb ow qkm qke sqr "
          "acc accp ae ul om oe ot ote ye yst").split()}
    s_uv = [E.sem(f"{tag}_uv{i}") for i in range(2)]
    s_vl = [E.sem(f"{tag}_vl{i}") for i in range(2)]
    s_wr = [E.sem(f"{tag}_wr{i}") for i in range(2)]
    s_vb = [E.sem(f"{tag}_vb{i}") for i in range(2)]

    E.op("v", lambda v: v.memset(CONST[:, 4:5], SQS), s["rte"])
    E.op("g", lambda g: g.dma_start(
        WPAD.ap().rearrange("(a b) -> a b", a=1)[:, 6144:8191],
        cfg["w_rel_rev"].ap().rearrange("(a b) -> a b", a=1)), s["wp1"], 16)
    E.wait("g", s["wp1"])
    E.op("g", lambda g: g.dma_start(
        WPAD.ap().rearrange("(a b) -> a b", a=1)[:, 0:6144],
        bass.AP(WPAD, 6144, [[0, 6144], [1, 1]])), s["wp2"], 16)
    WREP = cfg["WREP"]
    E.wait("g", s["wp2"])
    for p_ in range(128):
        E.op("g", lambda g, pp=p_: g.dma_start(
            WREP.ap().rearrange("(a b) -> a b", a=128)[pp:pp + 1, :],
            bass.AP(WPAD, 127 - pp, [[8064, 1], [1, 8064]])), s["wp2"], 16)
    for i, off in ((0, 0), (1, 128), (2, 256), (3, 384)):
        src = cfg["gamma"] if i % 2 == 0 else cfg["beta"]
        row = i // 2
        E.op("g", lambda g, a=src, r=row, o=off: g.dma_start(
            GBR[:, o:o + 128], bass.AP(a, r * 128, [[0, 128], [1, 128]])),
            s["gld"], 16)
    E.op("g", lambda g: g.dma_start(
        BIAS[:, 4:5], bass.AP(cfg["g"], 0, [[0, 128], [1, 1]])), s["gld"], 16)
    E.op("g", lambda g: g.dma_start(
        GB[:, 0:1024], bass.AP(cfg["res_scale"], 0, [[0, 128], [1, 1024]])),
        s["gld"], 16)

    UOFF = 0
    QOFF, KOFF = 24576, 28672
    ft_uv = []
    ft_vl = []

    for k in range(NCH):
        if k >= 1:
            E.wait("g", s["tp"], k * NT)
        E.op("g", lambda g, kk=k: g.dma_start(
            XT[:, :].rearrange("p (b d) -> p b d", b=NB),
            R.ap().rearrange("(k b p) d -> k p b d", p=128, b=NB)[kk]),
            s["x"], 16)
        E.wait("s", s["rtp"])
        emit_ln(E, (s["x"], s["sq"], s["var"], s["rstd"], s["ln"]), k, 1e-5,
                apply=False)
        E.wait("v", s["rstd"])
        E.op("v", lambda v: v.tensor_mul(SS[:, 112:120], SS[:, 32:40],
                                         SS[:, 32:40]))
        E.op("v", lambda v: v.tensor_mul(SS[:, 120:128], SS[:, 24:32],
                                         SS[:, 112:120]))
        E.op("v", lambda v: v.scalar_tensor_tensor(
            SS[:, 120:128], SS[:, 112:120], -1e-5, SS[:, 120:128],
            ALU.mult, ALU.add))
        E.op("v", lambda v: v.tensor_scalar_max(SS[:, 120:128], SS[:, 120:128],
                                                1e-10), s["fac"])
        E.wait("s", s["fac"])
        E.op("s", lambda sc: sc.activation(SS[:, 104:112], SS[:, 120:128],
                                           AF.Ln))
        E.op("s", lambda sc: sc.activation(SS[:, 104:112], SS[:, 104:112],
                                           AF.Exp, scale=CONST[:, 1:2]),
             s["fac"])
        E.wait("v", s["fac"])
        E.wait("v", s["gld"])
        E.op("v", lambda v: v.tensor_scalar_mul(SS[:, 104:112], SS[:, 104:112],
                                                BIAS[:, 4:5]))
        E.op("v", lambda v: v.tensor_mul(SS[:, 104:112], SS[:, 104:112],
                                         SS[:, 32:40]), s["fac"])
        for b in range(NB):
            E.op("v", lambda v, bb=b: v.tensor_scalar(
                XT[:, bb * 1024:(bb + 1) * 1024],
                XT[:, bb * 1024:(bb + 1) * 1024],
                SS[:, bb:bb + 1], SS[:, 104 + bb:105 + bb],
                ALU.subtract, ALU.mult), s["ln"])
        if k > 0:
            E.wait("v", s["t1st"], k * 16)
        for b in range(NB):
            E.op("v", lambda v, bb=b: v.tensor_mul(
                YY[:, bb * 1024:(bb + 1) * 1024],
                XT[:, bb * 1024:(bb + 1) * 1024], GB[:, 0:1024]), s["t1"])
        E.wait("g", s["t1"])
        E.op("g", lambda g, kk=k: g.dma_start(
            R.ap().rearrange("(k b p) d -> k p b d", p=128, b=NB)[kk],
            YY[:, :].rearrange("p (b d) -> p b d", b=NB),
            accum_op=ALU.add), s["t1st"], 16)
        emit_xpose(E, k, s["ln"], s["tp"], s["tpe"], s["rte"], s["rte"].n)
        E.wait("t", s["tpe"], (k + 1) * NT)
        for es in range(16):
            ui = k * 16 + es
            par = ui % 2
            if ui >= 2:
                fsem, fval = ft_uv[ui - 2]
                E.wait("y", fsem, fval)
            E.op("y", lambda y, ess=es, p=par: y.dma_start(
                WA[:, p * 1024:(p + 1) * 1024].rearrange("q (d c) -> q d c", d=8),
                UVT.ap().rearrange("(c p r) -> p c r", p=128, r=4224)
                [:, :, ess * 128:(ess + 1) * 128]), s_uv[par], 16)
            need = s_uv[par].n
            E.wait("t", s_uv[par], need)
            for tg in range(2):
                bank = P[tg]
                E.wait("t", s["ue"], ui * 2 + tg - 1)
                for c in range(DCH):
                    E.op("t", lambda t, cc=c, tgg=tg, p=par, bk=bank: t.matmul(
                        bk[:, :],
                        WA[:, p * 1024 + cc * 128: p * 1024 + (cc + 1) * 128],
                        XF[:, cc * TC + tgg * 512: cc * TC + (tgg + 1) * 512],
                        start=(cc == 0), stop=(cc == DCH - 1)),
                        s["um"] if c == DCH - 1 else None)
                E.wait("s", s["um"])
                if es == 0 and tg == 0:
                    E.wait("s", s["ust"])
                E.op("s", lambda sc, ess=es, tgg=tg, bk=bank: sc.activation(
                    HH[:, UOFF + ess * 1024 + tgg * 512:
                       UOFF + ess * 1024 + (tgg + 1) * 512],
                    bk[:, :], AF.Silu), s["ue"])
            ft_uv.append((s["um"], s["um"].n))
        E.wait("g", s["ue"])
        E.op("g", lambda g, kk=k: g.dma_start(
            UF.ap().rearrange("(e p r) -> p e r", p=128, r=S)
            [:, :, kk * TC:(kk + 1) * TC],
            HH[:, UOFF:UOFF + 16384].rearrange("p (e c) -> p e c", e=16)),
            s["ust"], 16)
        for b in range(NB):
            vi = k * NB + b
            if vi >= 1:
                E.wait("t", s["ve"], vi * 4)
            for c in range(DCH):
                vli = vi * DCH + c
                par = vli % 2
                if vli >= 2:
                    fsem, fval = ft_vl[vli - 2]
                    E.wait("y", fsem, fval)
                E.op("y", lambda y, cc=c, p=par: y.dma_start(
                    WA[:, 2048 + p * 2048: 2048 + (p + 1) * 2048],
                    UVT.ap().rearrange("(c p r) -> c p r", p=128, r=4224)
                    [cc][:, 2048:4096]), s_vl[par], 16)
                need = s_vl[par].n
                E.wait("t", s_vl[par], need)
                for g_ in range(4):
                    E.op("t", lambda t, bb=b, cc=c, gg=g_, p=par,
                         bk=P[2 + g_]: t.matmul(
                        bk[:, :],
                        XF[:, cc * TC + bb * 128: cc * TC + (bb + 1) * 128],
                        WA[:, 2048 + p * 2048 + gg * 512:
                           2048 + p * 2048 + (gg + 1) * 512],
                        start=(cc == 0), stop=(cc == DCH - 1)),
                        s["vm"] if g_ == 3 else None)
                ft_vl.append((s["vm"], s["vm"].n))
            E.wait("s", s["vm"])
            if vi >= 1:
                E.wait("s", s["vst"], vi * 16)
            for g_ in range(4):
                E.op("s", lambda sc, gg=g_, bk=P[2 + g_]: sc.activation(
                    VSTG[:, gg * 512:(gg + 1) * 512], bk[:, :], AF.Silu),
                    s["ve"])
            E.wait("g", s["ve"])
            E.op("g", lambda g, kk=k, bb=b: g.dma_start(
                VTOK.ap()[kk * TC + bb * 128: kk * TC + (bb + 1) * 128, :],
                VSTG[:, :]), s["vst"], 16)
        E.op("y", lambda y: y.dma_start(
            WA[:, 6144:7168].rearrange("q (d c) -> q d c", d=8),
            UVT.ap().rearrange("(c p r) -> p c r", p=128, r=4224)
            [:, :, 4096:4224]), s["uvb"], 16)
        E.wait("t", s["uvb"], (k + 1) * 16)
        for b in range(NB):
            bi = k * NB + b
            E.wait("g", s["rop"], bi * 2)
            E.op("g", lambda g, kk=k, bb=b: g.dma_start(
                GBR[:, 512:576],
                cfg["rcos"].ap()[kk * TC + bb * 128: kk * TC + (bb + 1) * 128, :]),
                s["rcs"], 16)
            E.op("g", lambda g, kk=k, bb=b: g.dma_start(
                GBR[:, 576:640],
                cfg["rsin"].ap()[kk * TC + bb * 128: kk * TC + (bb + 1) * 128, :]),
                s["rcs"], 16)
            E.wait("t", s["be"])
            for c in range(DCH):
                E.op("t", lambda t, bb=b, cc=c: t.matmul(
                    P[6][:, 0:128],
                    XF[:, cc * TC + bb * 128: cc * TC + (bb + 1) * 128],
                    WA[:, 6144 + cc * 128: 6144 + (cc + 1) * 128],
                    start=(cc == 0), stop=(cc == DCH - 1)),
                    s["bm"] if c == DCH - 1 else None)
            E.wait("s", s["bm"])
            E.wait("s", s["rop"])
            E.op("s", lambda sc: sc.activation(MISC[:, 0:128], P[6][:, 0:128],
                                               AF.Silu), s["be"])
            E.wait("v", s["be"])
            E.wait("v", s["rcs"])
            for qk_i, (gc, bc) in enumerate(((0, 128), (256, 384))):
                pre = TMP[:, 0:128]
                m1 = MISC[:, 128:192]
                m2 = MISC[:, 192:256]
                t1 = TMP[:, 128:192]
                E.op("v", lambda v, gcc=gc: v.tensor_mul(
                    pre, MISC[:, 0:128], GBR[:, gcc:gcc + 128]))
                E.op("v", lambda v, bcc=bc: v.tensor_add(
                    pre, pre, GBR[:, bcc:bcc + 128]))
                E.op("v", lambda v, m1=m1: v.tensor_mul(
                    m1, pre[:, 0:64], GBR[:, 512:576]))
                E.op("v", lambda v, t1=t1: v.tensor_mul(
                    t1, pre[:, 64:128], GBR[:, 576:640]))
                E.op("v", lambda v, m1=m1, t1=t1: v.tensor_sub(m1, m1, t1))
                E.op("v", lambda v, m2=m2: v.tensor_mul(
                    m2, pre[:, 64:128], GBR[:, 512:576]))
                E.op("v", lambda v, t1=t1: v.tensor_mul(
                    t1, pre[:, 0:64], GBR[:, 576:640]))
                E.op("v", lambda v, m2=m2, t1=t1: v.tensor_add(m2, m2, t1),
                     s["rop"])
                E.wait("t", s["rop"])
                E.wait("t", s["rte"])
                E.op("t", lambda t: t.transpose(
                    P[7][:, 0:128], MISC[:, 128:256], IDF[:, :]), s["rtp"])
                E.wait("v", s["rtp"])
                off = QOFF if qk_i == 0 else KOFF
                E.op("v", lambda v, kk=k, bb=b, o=off: v.tensor_copy(
                    HH[:, o + kk * TC + bb * 128: o + kk * TC + (bb + 1) * 128],
                    P[7][:, 0:128]), s["rte"])
    E.wait("g", s["ust"])
    E.wait("g", s["vst"])
    E.wait("g", s["wp2"])
    import os as _os3
    if _os3.environ.get("DBGDUMP") == "1" and "DQK" in E.t:
        s["dqk"] = E.sem("l1a_dqk")
        E.wait("g", s["rte"])
        E.op("g", lambda g: g.dma_start(
            E.t["DQK"].ap()[:, 0:4096], HH[:, QOFF:QOFF + 4096]), s["dqk"], 16)
        E.op("g", lambda g: g.dma_start(
            E.t["DQK"].ap()[:, 4096:8192], HH[:, KOFF:KOFF + 4096]), s["dqk"], 16)
        E.wait("y", s["dqk"])

    E.wait("y", s["vm"])
    E.wait("y", s["um"])
    E.wait("y", s["bm"])
    E.wait("y", s["wp2"])
    for ec in range(16):
        E.op("y", lambda y, cc=ec: y.dma_start(
            WA[:, cc * 1024:(cc + 1) * 1024],
            OWT.ap().rearrange("(c p r) -> c p r", p=128, r=1024)[cc]),
            s["ow"], 16)
    KER, VSL, USL = 0, 16384, 20480
    ft_v2 = []
    v2_i = 0
    for ib in range(8):
        for js in range(32):
            qi = ib * 32 + js
            bank = P[js % 2]
            wslot = js % 2
            if qi >= 2:
                E.wait("y", s["qke"], qi - 1)
            E.op("y", lambda y, i0=ib * 512, j0=js * 128, w=wslot: y.dma_start(
                GB[:, 1024 + w * 512: 1024 + (w + 1) * 512],
                bass.AP(cfg["WREP"], 3968 + i0 - j0, [[8064, 128], [1, 512]])),
                s_wr[wslot], 16)
            wrneed = s_wr[wslot].n
            if ib == 0 and js < 2:
                E.wait("t", s["ue"])
                E.wait("t", s["rte"])
            if qi >= 2:
                E.wait("t", s["qke"], qi - 1)
            E.op("t", lambda t, jss=js, i0=ib * 512, bk=bank: t.matmul(
                bk[:, :],
                HH[:, KOFF + jss * 128: KOFF + (jss + 1) * 128],
                HH[:, QOFF + i0: QOFF + i0 + 512],
                start=True, stop=True), s["qkm"])
            E.wait("v", s["qkm"])
            E.wait("v", s_wr[wslot], wrneed)
            if qi >= 1:
                E.wait("v", s["sqr"], qi)
            E.op("v", lambda v, w=wslot, bk=bank: v.tensor_add(
                MISC[:, 512:1024], bk[:, :],
                GB[:, 1024 + w * 512: 1024 + (w + 1) * 512]))
            E.op("v", lambda v: v.tensor_scalar_max(
                MISC[:, 512:1024], MISC[:, 512:1024], 0.0), s["qke"])
            E.wait("s", s["qke"])
            if ib == 0 and js == 0:
                E.wait("s", s["ust"])
            if ib > 0 and js == 0:
                E.wait("s", s["acc"])
            E.op("s", lambda sc, jss=js: sc.activation(
                HH[:, KER + jss * 512: KER + (jss + 1) * 512],
                MISC[:, 512:1024], AF.Square, scale=CONST[:, 4:5]), s["sqr"])
        for eg in range(2):
            for js in range(32):
                par = v2_i % 2
                if v2_i == 0:
                    E.wait("y", s["vst"])
                if v2_i >= 2:
                    fsem, fval = ft_v2[v2_i - 2]
                    E.wait("y", fsem, fval)
                E.op("y", lambda y, jss=js, p=par: y.dma_start(
                    HH[:, VSL + p * 2048: VSL + (p + 1) * 2048],
                    VTOK.ap()[jss * 128:(jss + 1) * 128, :]), s_vb[par], 16)
                vneed = s_vb[par].n
                E.wait("t", s_vb[par], vneed)
                if js == 0:
                    E.wait("t", s["ae"])
                    E.wait("t", s["sqr"], (ib + 1) * 32)
                for es8 in range(8):
                    es = eg * 8 + es8
                    msem = None
                    if js == 31:
                        msem = s["acc"]
                    elif es8 == 7:
                        msem = s["accp"]
                    E.op("t", lambda t, jss=js, ess=es, p=par,
                         bk=P[es8]: t.matmul(
                        bk[:, :],
                        HH[:, VSL + p * 2048 + ess * 128:
                           VSL + p * 2048 + ess * 128 + 128],
                        HH[:, KER + jss * 512: KER + (jss + 1) * 512],
                        start=(jss == 0), stop=(jss == 31)), msem)
                if js == 31:
                    ft_v2.append((s["acc"], s["acc"].n))
                else:
                    ft_v2.append((s["accp"], s["accp"].n))
                v2_i += 1
            E.wait("y", s["ae"])
            E.op("y", lambda y, egg=eg, i0=ib: y.dma_start(
                HH[:, USL:USL + 4096].rearrange("q (e c) -> q e c", e=8),
                UF.ap().rearrange("(e p r) -> p e r", p=128, r=S)
                [:, egg * 8:(egg + 1) * 8, i0 * 512:(i0 + 1) * 512]),
                s["ul"], 16)
            E.wait("v", s["ul"])
            E.wait("v", s["acc"])
            for es8 in range(8):
                es = eg * 8 + es8
                E.op("v", lambda v, ess=es, e8=es8, bk=P[es8]: v.tensor_mul(
                    XF[:, ess * 512:(ess + 1) * 512], bk[:, :],
                    HH[:, USL + e8 * 512: USL + (e8 + 1) * 512]), s["ae"])
        E.wait("t", s["ow"])
        E.wait("t", s["ae"])
        if ib == 0:
            E.wait("v", s["t1st"])
        if ib > 0:
            E.wait("v", s["yst"], ib * 16)
        for dsg in range(2):
            for ds4 in range(4):
                ds = dsg * 4 + ds4
                bank = P[ds4]
                if dsg == 1 or ib > 0:
                    E.wait("t", s["oe"], s["oe"].n - 3 + ds4)
                for ec in range(16):
                    E.op("t", lambda t, dd=ds, cc=ec, bk=bank: t.matmul(
                        bk[:, :],
                        WA[:, cc * 1024 + dd * 128: cc * 1024 + (dd + 1) * 128],
                        XF[:, cc * 512:(cc + 1) * 512],
                        start=(cc == 0), stop=(cc == 15)),
                        s["om"] if ec == 15 else None)
            for ds4 in range(4):
                ds = dsg * 4 + ds4
                E.wait("v", s["om"], s["om"].n - 4 + ds4 + 1)
                if s["ote"].n > 0:
                    E.wait("v", s["ote"])
                E.op("v", lambda v, bk=P[ds4]: v.tensor_copy(
                    MISC[:, 0:512], bk[:, :]), s["oe"])
                for isub in range(4):
                    E.wait("t", s["oe"])
                    E.wait("t", s["ote"])
                    E.op("t", lambda t, ii=isub: t.transpose(
                        P[6][:, 0:128], MISC[:, ii * 128:(ii + 1) * 128],
                        IDF[:, :]), s["ot"])
                    E.wait("v", s["ot"])
                    E.op("v", lambda v, dd=ds, ii=isub: v.tensor_copy(
                        YY[:, ii * 1024 + dd * 128: ii * 1024 + (dd + 1) * 128],
                        P[6][:, 0:128]), s["ote"])
        E.wait("g", s["ote"])
        E.op("g", lambda g, i0=ib: g.dma_start(
            R.ap().rearrange("(k b p) d -> k p b d", p=128, b=4)[i0],
            YY[:, 0:4096].rearrange("p (b d) -> p b d", b=4),
            accum_op=ALU.add), s["yst"], 16)
    E.wait("g", s["yst"])
    E.wait("g", s["t1st"])
    return s


def build_l1attn_single():
    nc = bass.Bass("TRN2", target_bir_lowering=False)
    stack = ExitStack()
    E = Emitter(nc, stack)
    consts = host_constants()
    with stack:
        alloc_arenas(E)
        xin = nc.dram_tensor("xin", [S, D], F32, kind="ExternalInput")
        out = nc.dram_tensor("out", [S, D], F32, kind="ExternalOutput")
        identf = nc.dram_tensor("identf", [128, 128], F32, kind="ExternalInput")
        rcos = nc.dram_tensor("rcos", [S, 64], F32, kind="ExternalInput")
        rsin = nc.dram_tensor("rsin", [S, 64], F32, kind="ExternalInput")
        ext = {}
        for n, sh in (("g", [1]), ("uv_w", [4224, 1024]), ("gamma", [2, 128]),
                      ("beta", [2, 128]), ("w_rel", [2047]),
                      ("w_rel_rev", [2047]),
                      ("o_w", [1024, 2048]), ("res_scale", [1024])):
            ext[n] = nc.dram_tensor(n, sh, F32, kind="ExternalInput")
        uv_bf = nc.dram_tensor("uv_bf", [4224, 1024], BF16)
        ow_bf = nc.dram_tensor("ow_bf", [1024, 2048], BF16)
        uvT = nc.dram_tensor("uvT", [8 * 128 * 4224], BF16)
        owT = nc.dram_tensor("owT", [16 * 128 * 1024], BF16)
        casts = [(ext["uv_w"].ap(), uv_bf), (ext["o_w"].ap(), ow_bf)]
        tps = [(uv_bf, uvT, 4224, 1024), (ow_bf, owT, 1024, 2048)]
        R = nc.dram_tensor("Rres", [S, D], F32)
        UF = nc.dram_tensor("UF", [16 * 128 * S], BF16)
        VTOK = nc.dram_tensor("VTOK", [S, 2048], BF16)
        WPAD = nc.dram_tensor("WPAD", [8192], F32)
        WREP = nc.dram_tensor("WREP", [128 * 8064], F32)
        cfg = dict(R=R, uvT=uvT, owT=owT, UF=UF, VT=VTOK, WPAD=WPAD, WREP=WREP,
                   rcos=rcos, rsin=rsin,
                   g=ext["g"], gamma=ext["gamma"], beta=ext["beta"],
                   w_rel=ext["w_rel"], w_rel_rev=ext["w_rel_rev"],
                   res_scale=ext["res_scale"])

        s_init_y = E.sem("inity")
        s_init_g = E.sem("initg")
        E.op("y", lambda y: y.dma_start(E.t["IDF"][:, :], identf.ap()),
             s_init_y, 16)
        E.op("g", lambda g: g.dma_start(R.ap(), xin.ap()), s_init_g, 16)
        emit_weight_prep(E, casts, tps)
        E.wait("y", s_init_y)
        E.wait("g", s_init_g)
        E.wait("t", s_init_y)
        E.flush()

        import os as _os4
        E.begin_phase()
        if _os4.environ.get("DBGDUMP") == "1":
            E.t["DQK"] = nc.dram_tensor("DQK", [128, 8192], F32)
        emit_l1attn_phase(E, cfg)
        E.flush()
        E.end_phase()

        s_out = E.sem("outc")
        E.op("g", lambda g: g.dma_start(out.ap(), R.ap()), s_out, 16)
        if _os4.environ.get("DBGDUMP") == "1":
            for nm, src in (("duf", UF), ("dvt", VTOK)):
                dd = nc.dram_tensor(nm, src.shape, F32, kind="ExternalOutput")
                E.op("g", lambda g, a=src, b=dd: g.dma_start(b.ap(), a.ap()),
                     s_out, 16)
            dwp = nc.dram_tensor("dwp", [8191], F32, kind="ExternalOutput")
            E.op("g", lambda g: g.dma_start(
                dwp.ap().rearrange("(a b) -> a b", a=1),
                WPAD.ap().rearrange("(a b) -> a b", a=1)[:, 0:8191]), s_out, 16)
            dqk2 = nc.dram_tensor("dqk2", [128, 8192], F32, kind="ExternalOutput")
            E.op("g", lambda g: g.dma_start(dqk2.ap(), E.t["DQK"].ap()),
                 s_out, 16)
        E.wait("g", s_out)
        E.flush()
    return nc


_PROG_CACHE = {}


def _get_prog(kind):
    if kind not in _PROG_CACHE:
        if kind == "l0":
            _PROG_CACHE[kind] = build_moe_single("l0")
        elif kind == "l1":
            _PROG_CACHE[kind] = build_moe_single("l1")
        elif kind == "l0a":
            _PROG_CACHE[kind] = build_l0attn_single()
        elif kind == "l1a":
            _PROG_CACHE[kind] = build_l1attn_single()
    return _PROG_CACHE[kind]


def kernel_unfused(**inputs):
    consts = host_constants()
    ident = np.eye(128, dtype=np.float32)
    f32 = lambda a: np.ascontiguousarray(np.asarray(a, dtype=np.float32))
    R = [f32(inputs["L0_attn_inputs"][b]) for b in range(B)]

    def launch(kind, names):
        nc = _get_prog(kind)
        base = {k: f32(inputs[v]) for k, v in names.items()}
        base["identf"] = ident
        if kind == "l1a":
            base["w_rel_rev"] = np.ascontiguousarray(
                f32(inputs["L1_attn_w_rel"])[::-1])
            base["rcos"] = consts["rope_cos"]
            base["rsin"] = consts["rope_sin"]
        in_maps = [dict(base, xin=R[b]) for b in range(B)]
        res = run_bass_kernel_spmd(nc, in_maps, core_ids=list(range(B)))
        for b in range(B):
            R[b] = np.ascontiguousarray(res.results[b]["out"])

    l0f = dict(gate_w="L0_ffn_gate_w", gate_bias="L0_ffn_gate_bias",
               w1="L0_ffn_w1", w3="L0_ffn_w3", w2="L0_ffn_w2",
               sw1="L0_ffn_sw1", sw3="L0_ffn_sw3", sw2="L0_ffn_sw2")
    l0a = dict(Wq="L0_attn_Wq", bq="L0_attn_bq", Wk="L0_attn_Wk",
               bk="L0_attn_bk", Wv="L0_attn_Wv", bv="L0_attn_bv",
               Wo="L0_attn_Wo", bo="L0_attn_bo",
               ln1_scale="L0_attn_ln1_scale", ln1_bias="L0_attn_ln1_bias",
               ln2_scale="L0_attn_ln2_scale", ln2_bias="L0_attn_ln2_bias",
               W1="L0_attn_W1", b1="L0_attn_b1", W2="L0_attn_W2",
               b2="L0_attn_b2")
    l1f = dict(W_router="L1_ffn_W_router", W_gate="L1_ffn_W_gate",
               W_up="L1_ffn_W_up", W_down="L1_ffn_W_down")
    l1a = dict(g="L1_attn_g", uv_w="L1_attn_uv_w", gamma="L1_attn_gamma",
               beta="L1_attn_beta", w_rel="L1_attn_w_rel",
               o_w="L1_attn_o_w", res_scale="L1_attn_res_scale")

    launch("l0", l0f)
    launch("l0a", l0a)
    launch("l0", l0f)
    launch("l1", l1f)
    launch("l1a", l1a)
    launch("l1", l1f)
    return np.stack(R).astype(np.float32)


def build_full():
    nc = bass.Bass("TRN2", target_bir_lowering=False)
    stack = ExitStack()
    E = Emitter(nc, stack)
    with stack:
        alloc_arenas(E)
        xin = nc.dram_tensor("xin", [S, D], F32, kind="ExternalInput")
        out = nc.dram_tensor("out", [S, D], F32, kind="ExternalOutput")
        identf = nc.dram_tensor("identf", [128, 128], F32, kind="ExternalInput")
        rcos = nc.dram_tensor("rcos", [S, 64], F32, kind="ExternalInput")
        rsin = nc.dram_tensor("rsin", [S, 64], F32, kind="ExternalInput")
        wrev = nc.dram_tensor("w_rel_rev", [2047], F32, kind="ExternalInput")
        ext = {}
        shapes = {
            "L0_attn_ln1_scale": [1024], "L0_attn_ln1_bias": [1024],
            "L0_attn_Wq": [1024, 16, 64], "L0_attn_bq": [16, 64],
            "L0_attn_Wk": [1024, 16, 64], "L0_attn_bk": [16, 64],
            "L0_attn_Wv": [1024, 16, 64], "L0_attn_bv": [16, 64],
            "L0_attn_Wo": [16, 64, 1024], "L0_attn_bo": [1024],
            "L0_attn_ln2_scale": [1024], "L0_attn_ln2_bias": [1024],
            "L0_attn_W1": [1024, 4096], "L0_attn_b1": [4096],
            "L0_attn_W2": [4096, 1024], "L0_attn_b2": [1024],
            "L0_ffn_gate_w": [8, 1024], "L0_ffn_gate_bias": [8],
            "L0_ffn_w1": [8, 1024, 1024], "L0_ffn_w3": [8, 1024, 1024],
            "L0_ffn_w2": [8, 1024, 1024], "L0_ffn_sw1": [2048, 1024],
            "L0_ffn_sw3": [2048, 1024], "L0_ffn_sw2": [1024, 2048],
            "L1_attn_g": [1], "L1_attn_uv_w": [4224, 1024],
            "L1_attn_gamma": [2, 128], "L1_attn_beta": [2, 128],
            "L1_attn_w_rel": [2047], "L1_attn_o_w": [1024, 2048],
            "L1_attn_res_scale": [1024],
            "L1_ffn_W_router": [8, 1024], "L1_ffn_W_gate": [8, 4096, 1024],
            "L1_ffn_W_up": [8, 4096, 1024], "L1_ffn_W_down": [8, 1024, 4096],
        }
        for n, sh in shapes.items():
            ext[n] = nc.dram_tensor(n, sh, F32, kind="ExternalInput")

        l0f_t, c0, t0 = moe_weight_scratch(
            E, "l0f", 1024, 8, 2048,
            dict(gw=ext["L0_ffn_gate_w"], w1=ext["L0_ffn_w1"],
                 w3=ext["L0_ffn_w3"], w2=ext["L0_ffn_w2"],
                 sw1=ext["L0_ffn_sw1"], sw3=ext["L0_ffn_sw3"],
                 sw2=ext["L0_ffn_sw2"]))
        l1f_t, c1, t1 = moe_weight_scratch(
            E, "l1f", 4096, 8, None,
            dict(gw=ext["L1_ffn_W_router"], w1=ext["L1_ffn_W_gate"],
                 w3=ext["L1_ffn_W_up"], w2=ext["L1_ffn_W_down"]))
        l0a_bf = {}
        c2 = []
        for n, src_ap, sh in (
            ("wq", ext["L0_attn_Wq"].ap().rearrange("a h m -> a (h m)"), [1024, 1024]),
            ("wk", ext["L0_attn_Wk"].ap().rearrange("a h m -> a (h m)"), [1024, 1024]),
            ("wv", ext["L0_attn_Wv"].ap().rearrange("a h m -> a (h m)"), [1024, 1024]),
            ("wo", ext["L0_attn_Wo"].ap().rearrange("h m d -> (h m) d"), [1024, 1024]),
            ("w1", ext["L0_attn_W1"].ap(), [1024, 4096]),
            ("w2", ext["L0_attn_W2"].ap(), [4096, 1024]),
        ):
            l0a_bf[n] = nc.dram_tensor(f"l0a_{n}_bf", sh, BF16)
            c2.append((src_ap, l0a_bf[n], sh[0], sh[1]))
        uv_bf = nc.dram_tensor("uv_bf", [4224, 1024], BF16)
        ow_bf = nc.dram_tensor("ow_bf", [1024, 2048], BF16)
        uvT = nc.dram_tensor("uvT", [8 * 128 * 4224], BF16)
        owT = nc.dram_tensor("owT", [16 * 128 * 1024], BF16)
        c3 = [(ext["L1_attn_uv_w"].ap(), uv_bf, 4224, 1024),
              (ext["L1_attn_o_w"].ap(), ow_bf, 1024, 2048)]
        t3 = [(uv_bf, uvT, 4224, 1024), (ow_bf, owT, 1024, 2048)]

        R = nc.dram_tensor("Rres", [S, D], F32)
        GF = nc.dram_tensor("GF", [8, S], F32)
        PHIQ = nc.dram_tensor("PHIQ", [S, D], BF16)
        ZBUF = nc.dram_tensor("ZBUF", [S, D], F32)
        RDH = nc.dram_tensor("RDH", [16, S], F32)
        KSROW = nc.dram_tensor("KSROW", [8, 128], F32)
        UF = nc.dram_tensor("UF", [16 * 128 * S], BF16)
        VTOK = nc.dram_tensor("VTOK", [S, 2048], BF16)
        WPAD = nc.dram_tensor("WPAD", [8192], F32)
        WREP = nc.dram_tensor("WREP", [128 * 8064], F32)

        cfg_l0f = dict(tag="l0f", F=1024, n_exp=8, gate_mode="l0",
                       shared=dict(F=2048), gw=l0f_t["gw"], wgT=l0f_t["wgT"],
                       wuT=l0f_t["wuT"], wdT=l0f_t["wdT"], swT=l0f_t["swT"],
                       sw2T=l0f_t["sw2T"], gate_bias=ext["L0_ffn_gate_bias"],
                       R=R, Gf=GF)
        cfg_l1f = dict(tag="l1f", F=4096, n_exp=8, gate_mode="l1",
                       shared=None, gw=l1f_t["gw"], wgT=l1f_t["wgT"],
                       wuT=l1f_t["wuT"], wdT=l1f_t["wdT"], R=R, Gf=GF)
        cfg_l0a = dict(R=R, PHIQ=PHIQ, ZBUF=ZBUF, RDH=RDH, KSROW=KSROW,
                       w=l0a_bf,
                       ln1_scale=ext["L0_attn_ln1_scale"],
                       ln1_bias=ext["L0_attn_ln1_bias"],
                       ln2_scale=ext["L0_attn_ln2_scale"],
                       ln2_bias=ext["L0_attn_ln2_bias"],
                       b1=ext["L0_attn_b1"],
                       bk=ext["L0_attn_bk"], bv=ext["L0_attn_bv"],
                       bq=ext["L0_attn_bq"], bo=ext["L0_attn_bo"],
                       b2m=ext["L0_attn_b2"])
        cfg_l1a = dict(R=R, uvT=uvT, owT=owT, UF=UF, VT=VTOK, WPAD=WPAD,
                       WREP=WREP, rcos=rcos, rsin=rsin,
                       g=ext["L1_attn_g"], gamma=ext["L1_attn_gamma"],
                       beta=ext["L1_attn_beta"], w_rel=ext["L1_attn_w_rel"],
                       w_rel_rev=wrev, res_scale=ext["L1_attn_res_scale"])

        s_init_y = E.sem("inity")
        s_init_g = E.sem("initg")
        E.op("y", lambda y: y.dma_start(E.t["IDF"][:, :], identf.ap()),
             s_init_y, 16)
        E.op("g", lambda g: g.dma_start(R.ap(), xin.ap()), s_init_g, 16)
        emit_weight_prep(E, c0, t0)
        cast_marks, seg_a_sts, st_finals = queue_bg_prep(
            E, c1[:3] + c2 + c1[3:] + c3, t1 + t3, {"l0a": 8})
        E.wait("y", s_init_y)
        E.wait("g", s_init_g)
        E.wait("t", s_init_y)
        E.flush()

        for pi, (emitfn, cfg) in enumerate((
                (emit_moe_phase, cfg_l0f),
                (emit_l0attn_phase, cfg_l0a),
                (emit_moe_phase, cfg_l0f),
                (emit_moe_phase, cfg_l1f),
                (emit_l1attn_phase, cfg_l1a),
                (emit_moe_phase, cfg_l1f))):
            E.begin_phase()
            if pi == 1:
                E.wait("y", E.bg_cast, cast_marks["l0a"])
                for sem, cnt in seg_a_sts:
                    if cnt:
                        E.wait("s", sem, cnt)
                        E.wait("v", sem, cnt)
            elif pi == 2:
                E.pop_stop("s")
            elif pi == 3:
                E.drain_bg()
                for sem, cnt in st_finals:
                    if cnt:
                        for eng in ENGS:
                            E.wait(eng, sem, cnt)
                E.wait("y", E.bg_cast, E.bg_cast.n)
            emitfn(E, cfg)
            E.flush()
            E.end_phase()

        s_out = E.sem("outc")
        E.op("g", lambda g: g.dma_start(out.ap(), R.ap()), s_out, 16)
        E.wait("g", s_out)
        E.flush()
    return nc


def kernel(**inputs):
    consts = host_constants()
    f32 = lambda a: np.ascontiguousarray(np.asarray(a, dtype=np.float32))
    base = {}
    for n in ("L0_attn_ln1_scale L0_attn_ln1_bias L0_attn_Wq L0_attn_bq "
              "L0_attn_Wk L0_attn_bk L0_attn_Wv L0_attn_bv L0_attn_Wo "
              "L0_attn_bo L0_attn_ln2_scale L0_attn_ln2_bias L0_attn_W1 "
              "L0_attn_b1 L0_attn_W2 L0_attn_b2 L0_ffn_gate_w "
              "L0_ffn_gate_bias L0_ffn_w1 L0_ffn_w3 L0_ffn_w2 L0_ffn_sw1 "
              "L0_ffn_sw3 L0_ffn_sw2 L1_attn_g L1_attn_uv_w L1_attn_gamma "
              "L1_attn_beta L1_attn_w_rel L1_attn_o_w L1_attn_res_scale "
              "L1_ffn_W_router L1_ffn_W_gate L1_ffn_W_up "
              "L1_ffn_W_down").split():
        base[n] = f32(inputs[n])
    base["identf"] = np.eye(128, dtype=np.float32)
    base["rcos"] = consts["rope_cos"]
    base["rsin"] = consts["rope_sin"]
    base["w_rel_rev"] = np.ascontiguousarray(f32(inputs["L1_attn_w_rel"])[::-1])
    x = f32(inputs["L0_attn_inputs"])
    nc = build_full()
    in_maps = [dict(base, xin=np.ascontiguousarray(x[b])) for b in range(B)]
    res = run_bass_kernel_spmd(nc, in_maps, core_ids=list(range(B)))
    return np.stack([res.results[b]["out"] for b in range(B)]).astype(np.float32)


# version 04
# speedup vs baseline: 1.0266x; 1.0266x over previous
import numpy as np
import ml_dtypes
from collections import deque
from contextlib import ExitStack

import concourse.bass as bass
import concourse.mybir as mybir
from concourse.bass_utils import run_bass_kernel_spmd

F32 = mybir.dt.float32
BF16 = mybir.dt.bfloat16
AF = mybir.ActivationFunctionType
ALU = mybir.AluOpType
AX = mybir.AxisListType.X

B, S, D = 8, 4096, 1024
TC = 1024
NB = TC // 128
NCH = S // TC
DCH = D // 128
NTG = TC // 512

BF = ml_dtypes.bfloat16


def host_constants():
    c = {}
    c["identf"] = np.eye(128, dtype=np.float32)
    c["identb"] = np.eye(128, dtype=BF)
    import jax
    import jax.numpy as jnp
    with jax.default_device(jax.devices("cpu")[0]):
        half = 64
        pos = jnp.arange(S, dtype=jnp.float32)[:, None]
        inv_freq = 10000.0 ** (jnp.arange(half, dtype=jnp.float32) / half)
        sinusoid = pos * inv_freq[None, :]
        c["rope_sin"] = np.asarray(jnp.sin(sinusoid), dtype=np.float32)
        c["rope_cos"] = np.asarray(jnp.cos(sinusoid), dtype=np.float32)
    return c


ENGS = ("y", "g", "t", "v", "s")


class Sem:
    def __init__(self, stack, nc, name):
        self.h = stack.enter_context(nc.semaphore(name))
        self.n = 0


class Emitter:

    CHAIN = ("v", "s")

    def __init__(self, nc, stack):
        self.nc = nc
        self.stack = stack
        self.q = {e: [] for e in ENGS}
        self.t = {}
        self._uid = 0
        self._last = {e: None for e in ENGS}
        self._chain = {}
        self.bgq = {"g": deque(), "s": deque()}

    def bg(self, eng, fn):
        self.bgq[eng].append(fn)

    def pump(self, eng, n=1):
        q = self.bgq[eng]
        for _ in range(n):
            if not q or q[0] == "STOP":
                return
            q.popleft()()

    def pop_stop(self, eng):
        q = self.bgq[eng]
        if q and q[0] == "STOP":
            q.popleft()

    def drain_bg(self):
        for eng in ("g", "s"):
            q = self.bgq[eng]
            while q:
                it = q.popleft()
                if it != "STOP":
                    it()

    def sem(self, name):
        self._uid += 1
        nm = f"{name}_{self._uid}"
        if getattr(self, "_phase_active", False):
            h = self.nc.alloc_semaphore(nm)
            sm = Sem.__new__(Sem)
            sm.h = h
            sm.n = 0
            self._psems.append(h)
            return sm
        return Sem(self.stack, self.nc, nm)

    def begin_phase(self):
        from contextlib import ExitStack
        self._pstack = ExitStack()
        self._pt = []
        self._psems = []
        self._phase_active = True
        self._last = {e: None for e in ENGS}
        self._chain = {}

    def salloc(self, name, shape, dt):
        self._uid += 1
        h = self._pstack.enter_context(
            self.nc.sbuf_tensor(f"{name}_{self._uid}", shape, dt))
        self.t[name] = h
        self._pt.append(name)
        return h

    def end_phase(self):
        self._phase_active = False
        if self._psems:
            self.nc.clear_and_free_semaphores(self._psems)
            self.nc.all_engine_barrier()
        self._pstack.close()
        for n in self._pt:
            self.t.pop(n, None)

    def on(self, eng, fn):
        self.q[eng].append(fn)

    def wait(self, eng, sem, val=None):
        v = sem.n if val is None else val
        if v > 0:
            self.q[eng].append(lambda e: e.wait_ge(sem.h, v))

    def op(self, eng, fn, sem=None, k=1):
        if eng in self.CHAIN:
            if self._last[eng] is not None:
                csem, cval = self._last[eng]
                self.q[eng].append(lambda e: e.wait_ge(csem.h, cval))
            if sem is None:
                if eng not in self._chain:
                    self._chain[eng] = self.sem(f"chain_{eng}")
                sem = self._chain[eng]
                k = 1
            sem.n += k
            self.q[eng].append(lambda e: fn(e).then_inc(sem.h, k))
            self._last[eng] = (sem, sem.n)
        elif sem is None:
            self.q[eng].append(fn)
        else:
            sem.n += k
            self.q[eng].append(lambda e: fn(e).then_inc(sem.h, k))

    def nopinc(self, eng, sem, k=1):
        if eng in self.CHAIN and self._last[eng] is not None:
            csem, cval = self._last[eng]
            self.q[eng].append(lambda e: e.wait_ge(csem.h, cval))
        sem.n += k
        self.q[eng].append(lambda e: e.nop().then_inc(sem.h, k))
        if eng in self.CHAIN:
            self._last[eng] = (sem, sem.n)

    def flush(self):
        nc, q = self.nc, self.q
        with nc.Block() as blk:
            @blk.sync
            def _(s):
                for f in q["y"]:
                    f(s)

            @blk.gpsimd
            def _(g):
                for f in q["g"]:
                    f(g)

            @blk.tensor
            def _(t):
                for f in q["t"]:
                    f(t)

            @blk.vector
            def _(v):
                for f in q["v"]:
                    f(v)

            @blk.scalar
            def _(s):
                for f in q["s"]:
                    f(s)
        self.q = {e: [] for e in ENGS}


def alloc_arenas(E):
    def sb(n, sh, dt):
        E.t[n] = E.stack.enter_context(E.nc.sbuf_tensor(n, sh, dt))

    def ps(n, sh, dt):
        E.t[n] = E.stack.enter_context(E.nc.psum_tensor(n, sh, dt))

    sb("XT", [128, NB * 1024], F32)
    sb("XF", [128, DCH * TC], BF16)
    sb("HH", [128, 32 * TC], BF16)
    sb("YY", [128, NB * 1024], F32)
    sb("WA", [128, 18432], BF16)
    sb("GS", [128, 1024], F32)
    sb("GB", [128, 2048], F32)
    sb("SS", [128, 272], F32)
    sb("IDF", [128, 128], F32)
    sb("CONST", [128, 8], F32)
    sb("MISC", [128, 1040], F32)
    for i in range(8):
        ps(f"P{i}", [128, 512], F32)


def emit_weight_prep(E, items_cast, items_tp):
    s_cast = E.sem("pw_cast")
    for it in items_cast:
        src, dst = it[0], it[1]
        E.op("g", lambda g, a=src, b=dst: g.dma_start(b.ap(), a), s_cast, 16)
    E.wait("y", s_cast)
    WA = E.t["WA"]
    s_ld = [E.sem("pw_ld0"), E.sem("pw_ld1")]
    s_st = [E.sem("pw_st0"), E.sem("pw_st1")]
    idx = 0
    for wbf, wt, R, C in items_tp:
        assert R <= 9216
        for c in range(C // 128):
            par = idx % 2
            if idx >= 2:
                E.wait("y", s_st[par], (idx // 2) * 16)
            E.op("y", lambda s, w=wbf, cc=c, p=par, r=R: s.dma_start(
                WA[:, p * 9216:p * 9216 + r],
                w.ap()[:, cc * 128:(cc + 1) * 128], transpose=True),
                s_ld[par], 16)
            E.wait("s", s_ld[par], (idx // 2 + 1) * 16)
            E.op("s", lambda s, w=wt, cc=c, p=par, r=R: s.dma_start(
                w.ap().rearrange("(c p r) -> c p r", p=128, r=r)[cc],
                WA[:, p * 9216:p * 9216 + r]), s_st[par], 16)
            idx += 1
    E.wait("y", s_st[0])
    E.wait("y", s_st[1])
    E.wait("g", s_st[0])
    E.wait("g", s_st[1])
    E.wait("g", s_cast)


def queue_bg_prep(E, cast_items, tp_items, marks_after):
    HH = E.t["HH"]
    E.bg_cast = E.sem("bgcast")
    cast_cnt = 0
    cast_need = {}
    cast_marks = {}
    after_idx = {v: k for k, v in marks_after.items()}
    for i, (src, dst, Rr, Cc) in enumerate(cast_items):
        rp = max(16, (8 << 20) // (Cc * 4))
        r0 = 0
        while r0 < Rr:
            r1 = min(Rr, r0 + rp)
            cast_cnt += 16
            def item(E=E, s=src, d=dst, a=r0, b=r1):
                E.on("g", lambda g: g.dma_start(d.ap()[a:b], s[a:b])
                     .then_inc(E.bg_cast.h, 16))
            E.bg("g", item)
            r0 = r1
        cast_need[id(dst)] = cast_cnt
        if i in after_idx:
            cast_marks[after_idx[i]] = cast_cnt
    E.bg_cast.n = cast_cnt

    NSLOT, SLOTW, BASE, LAG = 6, 2176, 16384, 4
    SEG_A = 460
    sld = [E.sem(f"bgtl{q}") for q in range(NSLOT)]
    sst = [E.sem(f"bgts{q}") for q in range(NSLOT)]
    ldc = [0] * NSLOT
    stc = [0] * NSLOT
    pend = deque()
    piece = 0
    nitem = 0
    seg_marks = []

    def put(it):
        nonlocal nitem
        E.bg("s", it)
        nitem += 1

    for (wbf, wt, Rr, Cc) in tp_items:
        need = cast_need[id(wbf)]
        nrp = -(-Rr // SLOTW)
        rps = [(i * Rr // nrp, (i + 1) * Rr // nrp) for i in range(nrp)]
        for cc in range(Cc // 128):
            for (a, b) in rps:
                q = piece % NSLOT
                prev_st = stc[q]
                ldc[q] += 16
                myld = ldc[q]
                def ld_item(E=E, w=wbf, cc=cc, a=a, b=b, q=q, need=need,
                            prev_st=prev_st):
                    if need:
                        E.on("s", lambda e, n=need: e.wait_ge(E.bg_cast.h, n))
                    if prev_st:
                        E.on("s", lambda e, n=prev_st, qq=q: e.wait_ge(
                            sst[qq].h, n))
                    E.on("s", lambda e, qq=q: e.dma_start(
                        HH[:, BASE + qq * SLOTW: BASE + qq * SLOTW + (b - a)],
                        w.ap()[a:b, cc * 128:(cc + 1) * 128], transpose=True)
                        .then_inc(sld[qq].h, 16))
                stc[q] += 16
                myld2 = myld
                def st_item(E=E, w=wt, cc=cc, a=a, b=b, q=q, Rr=Rr,
                            myld=myld2):
                    E.on("s", lambda e, n=myld, qq=q: e.wait_ge(sld[qq].h, n))
                    E.on("s", lambda e, qq=q: e.dma_start(
                        bass.AP(w, cc * 128 * Rr + a, [[Rr, 128], [1, b - a]]),
                        HH[:, BASE + qq * SLOTW: BASE + qq * SLOTW + (b - a)])
                        .then_inc(sst[qq].h, 16))
                put(ld_item)
                pend.append(st_item)
                if len(pend) > LAG:
                    put(pend.popleft())
                piece += 1
                if not seg_marks and nitem >= SEG_A:
                    while pend:
                        put(pend.popleft())
                    seg_marks.append(list(stc))
                    E.bg("s", "STOP")
    while pend:
        put(pend.popleft())
    if not seg_marks:
        seg_marks.append(list(stc))
        E.bg("s", "STOP")
    for q in range(NSLOT):
        sld[q].n = ldc[q]
        sst[q].n = stc[q]
    return cast_marks, list(zip(sst, seg_marks[0])), list(zip(sst, stc))


def emit_ln(E, sems, k, eps, apply=True):
    XT, SS, SQ = E.t["XT"], E.t["SS"], E.t["MISC"]
    s_x, s_sq, s_var, s_rstd, s_ln = sems
    if not hasattr(E, "_cinit"):
        E._cinit = E.sem("cinit")
        E.op("v", lambda v: v.memset(E.t["CONST"][:, 1:2], -0.5), E._cinit)
        E.wait("s", E._cinit, 1)
    E.wait("v", s_x, (k + 1) * 16)
    E.op("v", lambda v: v.tensor_reduce(
        SS[:, 0:NB], XT[:, :].rearrange("p (b d) -> p b d", b=NB), AX, ALU.add))
    E.op("v", lambda v: v.tensor_scalar_mul(SS[:, 0:NB], SS[:, 0:NB], 1.0 / D))
    E.op("v", lambda v: v.tensor_mul(SS[:, 16:24], SS[:, 0:NB], SS[:, 0:NB]))
    E.wait("s", s_x, (k + 1) * 16)
    for b in range(NB):
        E.wait("s", s_sq)
        E.op("s", lambda s, bb=b: s.activation(
            SQ[:, 0:1024], XT[:, bb * 1024:(bb + 1) * 1024], AF.Square,
            accum_out=SS[:, 8 + bb:9 + bb]), s_sq)
    E.wait("v", s_sq)
    E.op("v", lambda v: v.scalar_tensor_tensor(
        SS[:, 24:32], SS[:, 8:16], 1.0 / D, SS[:, 16:24],
        ALU.mult, ALU.subtract))
    E.op("v", lambda v: v.tensor_scalar_add(SS[:, 24:32], SS[:, 24:32],
                                            float(eps)), s_var)
    E.wait("s", s_var)
    E.op("s", lambda s: s.activation(SS[:, 40:48], SS[:, 24:32], AF.Ln), s_rstd)
    E.wait("s", s_rstd)
    E.op("s", lambda s: s.activation(SS[:, 32:40], SS[:, 40:48], AF.Exp,
                                     scale=E.t["CONST"][:, 1:2]), s_rstd)
    E.wait("v", s_rstd)
    if apply:
        for b in range(NB):
            E.op("v", lambda v, bb=b: v.tensor_scalar(
                XT[:, bb * 1024:(bb + 1) * 1024],
                XT[:, bb * 1024:(bb + 1) * 1024],
                SS[:, bb:bb + 1], SS[:, 32 + bb:33 + bb],
                ALU.subtract, ALU.mult), s_ln)


def emit_xpose(E, k, s_ln, s_tp, s_tpe, s_free, free_val):
    XT, XF, IDF = E.t["XT"], E.t["XF"], E.t["IDF"]
    P6, P7 = E.t["P6"], E.t["P7"]
    NT = NB * DCH
    for ti in range(NT):
        bank = (P6, P7)[ti % 2]
        b, c = ti // DCH, ti % DCH
        E.wait("t", s_ln, k * NB + b + 1)
        if ti >= 2:
            E.wait("t", s_tpe, k * NT + ti - 1)
        else:
            E.wait("t", s_free, free_val)
        E.op("t", lambda t, bb=b, cc=c, bk=bank: t.transpose(
            bk[:, 0:128],
            XT[:, bb * 1024 + cc * 128: bb * 1024 + (cc + 1) * 128],
            IDF[:, :]), s_tp)
        E.wait("v", s_tp)
        E.op("v", lambda v, bb=b, cc=c, bk=bank: v.tensor_copy(
            XF[:, cc * TC + bb * 128: cc * TC + (bb + 1) * 128],
            bk[:, 0:128]), s_tpe)


def emit_moe_phase(E, cfg):
    tag = cfg["tag"]
    F = cfg["F"]
    NE = cfg["n_exp"]
    R = cfg["R"]
    Gf = cfg["Gf"]
    shared = cfg.get("shared")
    mode = cfg["gate_mode"]

    XT, XF, HH, YY = E.t["XT"], E.t["XF"], E.t["HH"], E.t["YY"]
    WA, GS, GB, SS = E.t["WA"], E.t["GS"], E.t["GB"], E.t["SS"]
    IDF, MISC = E.t["IDF"], E.t["MISC"]
    GROW = E.salloc("GROW", [8, 1024], F32)
    GWS = E.salloc("GWS", [128, 128], BF16)
    GWL = E.salloc("GWL", [8, 1024], F32)
    BR = E.salloc("BR", [128, 16], F32)
    P = [E.t[f"P{i}"] for i in range(8)]

    WUP_OFF = [0, 4096, 8192, 12288]
    WD_OFF = [16384, 17408]

    sems = {n: E.sem(f"{tag}_{n}") for n in
            ("x sq var rstd ln tp tpe gw rt smx exp gv gtp gev gst gbf "
             "wuf wdf mg mu sil ue dmm ye yst").split()}
    s = sems
    s_gb = [E.sem(f"{tag}_gb{i}") for i in range(2)]
    s_wu = [E.sem(f"{tag}_wu{i}") for i in range(2)]
    s_wd = [E.sem(f"{tag}_wd{i}") for i in range(2)]

    exps = list(range(NE)) + (["S"] if shared else [])

    def nfc(e):
        return (shared["F"] // 128) if e == "S" else F // 128

    def nfs(e):
        return (shared["F"] // 512) if e == "S" else F // 512

    ne = NE
    s_gwe = E.sem(f"{tag}_gwe")
    E.op("y", lambda y: y.dma_start(GWL[:, :], cfg["gw"].ap()), s["gw"], 16)
    E.wait("t", s["gw"], 16)
    for c in range(DCH):
        bank = P[6 + c % 2]
        if c >= 2:
            E.wait("t", s_gwe, c - 1)
        E.op("t", lambda t, cc=c, bk=bank: t.transpose(
            bk[:, 0:8],
            GWL[:, cc * 128:(cc + 1) * 128], IDF[0:8, 0:8]), s["gtp"])
        E.wait("v", s["gtp"])
        E.op("v", lambda v, cc=c, bk=bank: v.tensor_copy(
            GWS[:, cc * ne:cc * ne + ne], bk[:, 0:8]), s_gwe)
    s_br = E.sem(f"{tag}_br")
    if mode == "l0":
        E.op("g", lambda g: g.dma_start(
            BR[:, 0:ne], bass.AP(cfg["gate_bias"], 0, [[0, 128], [1, ne]])),
            s_br, 16)

    slab_i = 0
    dfc_i = 0
    pair_i = 0
    gbe_i = 0
    ft_slab = []
    ft_gb = []
    ft_wd = []

    for k in range(NCH):
        if k >= 1:
            E.wait("g", s["tp"], k * NB * DCH)
        E.op("g", lambda g, kk=k: g.dma_start(
            XT[:, :].rearrange("p (b d) -> p b d", b=NB),
            R.ap().rearrange("(k b p) d -> k p b d", p=128, b=NB)[kk]),
            s["x"], 16)
        E.pump("g", 2)

        emit_ln(E, (s["x"], s["sq"], s["var"], s["rstd"], s["ln"]), k, 1e-5)
        if k == 0:
            emit_xpose(E, k, s["ln"], s["tp"], s["tpe"], s_gwe, s_gwe.n)
        else:
            emit_xpose(E, k, s["ln"], s["tp"], s["tpe"], s["ye"], s["ye"].n)

        E.wait("t", s["tpe"], (k + 1) * NB * DCH)
        if k > 0:
            E.wait("t", s["ye"])
        for b in range(NB):
            for c in range(DCH):
                E.op("t", lambda t, bb=b, cc=c, bk=P[b]: t.matmul(
                    bk[:, 0:8],
                    XF[:, cc * TC + bb * 128: cc * TC + (bb + 1) * 128],
                    GWS[:, cc * ne:cc * ne + ne],
                    start=(cc == 0), stop=(cc == DCH - 1)),
                    s["rt"] if c == DCH - 1 else None)
        for b in range(NB):
            lg = P[b][:, 0:8]
            pro = SS[:, 64 + b * 8: 72 + b * 8]
            mx = SS[:, 48 + b: 49 + b]
            sm = SS[:, 56 + b: 57 + b]
            Gt = SS[:, 128 + b * 8: 136 + b * 8]
            sel = SS[:, 192 + b * 8: 200 + b * 8]
            m1 = SS[:, 256:257]
            m2 = SS[:, 257:258]
            t1 = MISC[:, 1024:1032]
            t2 = MISC[:, 1032:1040]
            E.wait("v", s["rt"], s["rt"].n - NB + b + 1)
            E.op("v", lambda v, lg=lg, mx=mx: v.reduce_max(mx, lg, AX))
            E.op("v", lambda v, mx=mx: v.tensor_scalar_mul(mx, mx, -1.0),
                 s["smx"])
            E.wait("s", s["smx"])
            E.op("s", lambda sc, lg=lg, pro=pro, mx=mx: sc.activation(
                pro, lg, AF.Exp, bias=mx), s["exp"])
            E.wait("v", s["exp"])
            E.op("v", lambda v, pro=pro, sm=sm: v.tensor_reduce(
                sm, pro, AX, ALU.add))
            E.op("v", lambda v, sm=sm: v.reciprocal(sm, sm))
            E.op("v", lambda v, pro=pro, sm=sm: v.tensor_scalar_mul(
                pro, pro, sm))
            if mode == "l0":
                E.wait("v", s_br, 16)
                E.op("v", lambda v, sel=sel, pro=pro: v.tensor_add(
                    sel, pro, BR[:, 0:ne]))
            else:
                E.op("v", lambda v, sel=sel, pro=pro: v.tensor_copy(sel, pro))
            E.op("v", lambda v, sel=sel, m1=m1: v.reduce_max(m1, sel, AX))
            E.op("v", lambda v, sel=sel, m1=m1, t1=t1: v.tensor_single_scalar(
                t1, sel, m1, ALU.is_equal))
            E.op("v", lambda v, sel=sel, t1=t1, t2=t2: v.scalar_tensor_tensor(
                t2, t1, -1e30, sel, ALU.mult, ALU.add))
            E.op("v", lambda v, t2=t2, m2=m2: v.reduce_max(m2, t2, AX))
            E.op("v", lambda v, t2=t2, m2=m2: v.tensor_single_scalar(
                t2, t2, m2, ALU.is_equal))
            E.op("v", lambda v, t1=t1, t2=t2: v.tensor_add(t1, t1, t2))
            E.op("v", lambda v, Gt=Gt, pro=pro, t1=t1: v.tensor_mul(
                Gt, pro, t1))
            if mode == "l0":
                E.op("v", lambda v, Gt=Gt: v.tensor_scalar_mul(Gt, Gt, 0.5),
                     s["gv"])
            else:
                E.op("v", lambda v, m1=m1, m2=m2: v.tensor_add(m1, m1, m2))
                E.op("v", lambda v, m1=m1: v.reciprocal(m1, m1))
                E.op("v", lambda v, m1=m1: v.tensor_scalar_mul(m1, m1, 0.5))
                E.op("v", lambda v, Gt=Gt, m1=m1: v.tensor_scalar_mul(
                    Gt, Gt, m1), s["gv"])
        E.wait("t", s["gv"])
        gev_base = s["gev"].n
        for b in range(NB):
            bank = P[6 + b % 2]
            if b >= 2:
                E.wait("t", s["gev"], gev_base + b - 1)
            E.op("t", lambda t, bb=b, bk=bank: t.transpose(
                bk[0:8, 0:128],
                SS[:, 128 + bb * 8: 136 + bb * 8], IDF[:, :]),
                s["gtp"])
            E.wait("v", s["gtp"])
            E.op("v", lambda v, bb=b, bk=bank: v.tensor_copy(
                GROW[:, bb * 128:(bb + 1) * 128], bk[0:8, 0:128]), s["gev"])
        E.wait("g", s["gev"])
        E.op("g", lambda g, kk=k: g.dma_start(
            Gf.ap()[:, kk * TC:(kk + 1) * TC], GROW[:, :]), s["gst"], 16)
        E.pump("g", 2)
        E.wait("g", s["gst"])
        gbe_k = gbe_i

        def emit_gb_load(e, kk):
            nonlocal_i = gbe_k + e
            par = nonlocal_i % 2
            if nonlocal_i >= 2:
                fs_sem, fs_val = ft_gb[nonlocal_i - 2]
                E.wait("g", fs_sem, fs_val)
            E.op("g", lambda g, ee=e, k2=kk, pp=par: g.dma_start(
                GB[:, pp * 1024:(pp + 1) * 1024],
                bass.AP(Gf, ee * S + k2 * TC, [[0, 128], [1, TC]])),
                s_gb[par], 16)
            E.pump("g", 2)

        emit_gb_load(0, k)
        emit_gb_load(1, k)
        gbe_i += NE

        units_chunk = []
        for e in exps:
            fdim = shared["F"] if e == "S" else F
            for sl in range(nfs(e)):
                if e == "S":
                    units_chunk.append((cfg["swT"][0], cfg["swT"][1], sl, fdim))
                else:
                    units_chunk.append((cfg["wgT"][e], cfg["wuT"][e], sl, fdim))
        unit_meta = {}
        next_load = [0]
        unit_base = slab_i

        def load_next_unit():
            if next_load[0] >= len(units_chunk):
                return
            u = unit_base + next_load[0]
            src_g, src_u, sl, fdim = units_chunk[next_load[0]]
            par = u % 2
            if u >= 2:
                fs_sem, fs_val = ft_slab[u - 2]
                E.wait("y", fs_sem, fs_val)
            for m, src in ((0, src_g), (1, src_u)):
                E.op("y", lambda y, a=src, ss=sl, p=par, mm=m, f=fdim: y.dma_start(
                    WA[:, WUP_OFF[2 * p + mm]:WUP_OFF[2 * p + mm] + 4096]
                    .rearrange("p (d c) -> p d c", d=8),
                    a.ap().rearrange("(c p r) -> p c r", p=128, r=f)
                    [:, :, ss * 512:(ss + 1) * 512]),
                    s_wu[par], 16)
            unit_meta[u] = (par, s_wu[par].n)
            next_load[0] += 1

        load_next_unit()
        load_next_unit()
        unit_idx = unit_base
        for ei, e in enumerate(exps):
            expert_unit0 = unit_idx
            for fs in range(nfc(e)):
                sl = fs // 4
                slpar, slneed = unit_meta[expert_unit0 + sl]
                off_g = WUP_OFF[2 * slpar + 0]
                off_u = WUP_OFF[2 * slpar + 1]
                for tg in range(NTG):
                    pp = pair_i % 2
                    bg, bu = P[2 + pp * 2], P[3 + pp * 2]
                    if pair_i >= 2:
                        E.wait("t", s["ue"], pair_i - 1)
                    if fs == 0 and tg == 0:
                        E.wait("t", s["ye"])
                        if ei == 0:
                            E.wait("t", s["exp"])
                    E.wait("t", s_wu[slpar], slneed)
                    for c in range(DCH):
                        E.op("t", lambda t, cc=c, fss=fs, tgg=tg, o=off_g, bk=bg: t.matmul(
                            bk[:, :],
                            WA[:, o + cc * 512 + (fss % 4) * 128:
                               o + cc * 512 + (fss % 4) * 128 + 128],
                            XF[:, cc * TC + tgg * 512: cc * TC + (tgg + 1) * 512],
                            start=(cc == 0), stop=(cc == DCH - 1)),
                            s["mg"] if c == DCH - 1 else None)
                    for c in range(DCH):
                        E.op("t", lambda t, cc=c, fss=fs, tgg=tg, o=off_u, bk=bu: t.matmul(
                            bk[:, :],
                            WA[:, o + cc * 512 + (fss % 4) * 128:
                               o + cc * 512 + (fss % 4) * 128 + 128],
                            XF[:, cc * TC + tgg * 512: cc * TC + (tgg + 1) * 512],
                            start=(cc == 0), stop=(cc == DCH - 1)),
                            s["mu"] if c == DCH - 1 else None)
                    if tg == NTG - 1 and (fs % 4 == 3 or fs == nfc(e) - 1):
                        ft_slab.append((s["mu"], s["mu"].n))
                        slab_i += 1
                        unit_idx += 1
                        load_next_unit()
                    E.wait("s", s["mg"])
                    if pair_i >= 2:
                        E.wait("s", s["ue"], pair_i - 1)
                    E.op("s", lambda sc, bk=bg, pp=pp: sc.activation(
                        GS[:, pp * 512:(pp + 1) * 512], bk[:, :], AF.Silu),
                        s["sil"])
                    ht = HH[:, fs * TC + tg * 512: fs * TC + (tg + 1) * 512]
                    E.wait("v", s["mu"])
                    E.wait("v", s["sil"])
                    if fs == 0 and tg == 0:
                        E.wait("v", s["dmm"])
                    if e != "S":
                        gbe = gbe_k + e
                        E.wait("v", s_gb[gbe % 2], (gbe // 2 + 1) * 16)
                    if e != "S":
                        E.op("v", lambda v, bk=bu, pp=pp, ht=ht: v.tensor_mul(
                            ht, bk[:, :], GS[:, pp * 512:(pp + 1) * 512]))
                        E.op("v", lambda v, ht=ht, gbs=(gbe_k + e) % 2, tgg=tg:
                             v.tensor_mul(ht, ht,
                                          GB[:, gbs * 1024 + tgg * 512:
                                             gbs * 1024 + (tgg + 1) * 512]),
                             s["ue"])
                    else:
                        E.op("v", lambda v, bk=bu, pp=pp, ht=ht: v.tensor_mul(
                            ht, bk[:, :], GS[:, pp * 512:(pp + 1) * 512]),
                            s["ue"])
                    E.pump("s", 1)
                    pair_i += 1
            if e != "S":
                ft_gb.append((s["ue"], s["ue"].n))
                if e + 2 < NE:
                    emit_gb_load(e + 2, k)
            E.wait("t", s["ue"])
            for ps_ in range(2):
                for fc in range(nfc(e)):
                    par = dfc_i % 2
                    if dfc_i >= 2:
                        fs_sem, fs_val = ft_wd[dfc_i - 2]
                        E.wait("y", fs_sem, fs_val)
                    src = cfg["sw2T"] if e == "S" else cfg["wdT"][e]
                    E.op("y", lambda y, a=src, f=fc, p=par: y.dma_start(
                        WA[:, WD_OFF[p]:WD_OFF[p] + 1024],
                        a.ap().rearrange("(c p r) -> c p r", p=128, r=1024)[f]),
                        s_wd[par], 16)
                    need = s_wd[par].n
                    E.wait("t", s_wd[par], need)
                    if fc == 0:
                        E.wait("t", s["ye"])
                        E.wait("t", s["tpe"], (k + 1) * NB * DCH)
                        E.wait("t", s["gev"])
                    last_fc = (fc == nfc(e) - 1)
                    for ts_ in range(4):
                        tsub = ps_ * 4 + ts_
                        for half in range(2):
                            bk = P[2 * ts_ + half]
                            if last_fc and half == 1:
                                msem = s["dmm"]
                            elif ts_ == 3 and half == 1:
                                msem = s["wdf"]
                            else:
                                msem = None
                            E.op("t", lambda t, f=fc, tb=tsub, h=half, bk=bk,
                                 p=par, e_=e: t.matmul(
                                bk[:, :],
                                HH[:, f * TC + tb * 128: f * TC + (tb + 1) * 128],
                                WA[:, WD_OFF[p] + h * 512: WD_OFF[p] + (h + 1) * 512],
                                start=(f == 0), stop=(f == nfc(e_) - 1)),
                                msem)
                    if last_fc:
                        ft_wd.append((s["dmm"], s["dmm"].n))
                    else:
                        ft_wd.append((s["wdf"], s["wdf"].n))
                    dfc_i += 1
                for ts_ in range(4):
                    tsub = ps_ * 4 + ts_
                    E.wait("v", s["dmm"], s["dmm"].n - 4 + ts_ + 1)
                    if ei == 0 and ps_ == 0 and ts_ == 0 and k > 0:
                        E.wait("v", s["yst"], k * 16)
                    for half in range(2):
                        bk = P[2 * ts_ + half]
                        yy = YY[:, tsub * 1024 + half * 512:
                                tsub * 1024 + (half + 1) * 512]
                        if ei == 0:
                            E.op("v", lambda v, bk=bk, yy=yy: v.tensor_copy(
                                yy, bk[:, :]), s["ye"])
                        elif e == "S":
                            E.op("v", lambda v, bk=bk, yy=yy: v.scalar_tensor_tensor(
                                yy, bk[:, :], 0.5, yy, ALU.mult, ALU.add),
                                s["ye"])
                        else:
                            E.op("v", lambda v, bk=bk, yy=yy: v.tensor_add(
                                yy, yy, bk[:, :]), s["ye"])

        E.wait("g", s["ye"])
        E.op("g", lambda g, kk=k: g.dma_start(
            R.ap().rearrange("(k b p) d -> k p b d", p=128, b=NB)[kk],
            YY[:, :].rearrange("p (b d) -> p b d", b=NB),
            accum_op=ALU.add), s["yst"], 16)
        E.pump("g", 2)

    E.wait("g", s["yst"])
    return s["yst"]


def moe_weight_scratch(E, tag, F, NE, shared_F, names):
    nc = E.nc
    dr = lambda n, sh, dt: nc.dram_tensor(n, sh, dt)
    t = {}
    casts, tps = [], []
    t["gw"] = names["gw"]
    t["wgT"], t["wuT"], t["wdT"] = [], [], []
    for e in range(NE):
        for key, lst, R, C in (("w1", t["wgT"], F, 1024),
                               ("w3", t["wuT"], F, 1024),
                               ("w2", t["wdT"], 1024, F)):
            bf = dr(f"{tag}_{key}{e}_bf", [R, C], BF16)
            wt = dr(f"{tag}_{key}{e}T", [(C // 128) * 128 * R], BF16)
            casts.append((names[key].ap()[e], bf, R, C))
            tps.append((bf, wt, R, C))
            lst.append(wt)
    if shared_F:
        t["swT"] = []
        for key, R, C in (("sw1", shared_F, 1024), ("sw3", shared_F, 1024)):
            bf = dr(f"{tag}_{key}_bf", [R, C], BF16)
            wt = dr(f"{tag}_{key}T", [(C // 128) * 128 * R], BF16)
            casts.append((names[key].ap(), bf, R, C))
            tps.append((bf, wt, R, C))
            t["swT"].append(wt)
        bf = dr(f"{tag}_sw2_bf", [1024, shared_F], BF16)
        wt = dr(f"{tag}_sw2T", [(shared_F // 128) * 128 * 1024], BF16)
        casts.append((names["sw2"].ap(), bf, 1024, shared_F))
        tps.append((bf, wt, 1024, shared_F))
        t["sw2T"] = wt
    return t, casts, tps


def moe_cfg_l0(E, w):
    t, casts, tps = moe_weight_scratch(
        E, "l0f", 1024, 8, 2048,
        dict(gw=w["gate_w"], w1=w["w1"], w3=w["w3"], w2=w["w2"],
             sw1=w["sw1"], sw3=w["sw3"], sw2=w["sw2"]))
    cfg = dict(tag="l0f", F=1024, n_exp=8, gate_mode="l0",
               shared=dict(F=2048), gw=t["gw"], wgT=t["wgT"], wuT=t["wuT"],
               wdT=t["wdT"], swT=t["swT"], sw2T=t["sw2T"],
               gate_bias=w["gate_bias"])
    return cfg, casts, tps


def moe_cfg_l1(E, w):
    t, casts, tps = moe_weight_scratch(
        E, "l1f", 4096, 8, None,
        dict(gw=w["W_router"], w1=w["W_gate"], w3=w["W_up"], w2=w["W_down"]))
    cfg = dict(tag="l1f", F=4096, n_exp=8, gate_mode="l1",
               shared=None, gw=t["gw"], wgT=t["wgT"], wuT=t["wuT"],
               wdT=t["wdT"])
    return cfg, casts, tps


def build_moe_single(kind):
    nc = bass.Bass("TRN2", target_bir_lowering=False)
    stack = ExitStack()
    E = Emitter(nc, stack)
    with stack:
        alloc_arenas(E)
        xin = nc.dram_tensor("xin", [S, D], F32, kind="ExternalInput")
        out = nc.dram_tensor("out", [S, D], F32, kind="ExternalOutput")
        identf = nc.dram_tensor("identf", [128, 128], F32, kind="ExternalInput")
        w = {}
        if kind == "l0":
            w["gate_w"] = nc.dram_tensor("gate_w", [8, 1024], F32, kind="ExternalInput")
            w["gate_bias"] = nc.dram_tensor("gate_bias", [8], F32, kind="ExternalInput")
            for n, sh in (("w1", [8, 1024, 1024]), ("w3", [8, 1024, 1024]),
                          ("w2", [8, 1024, 1024]), ("sw1", [2048, 1024]),
                          ("sw3", [2048, 1024]), ("sw2", [1024, 2048])):
                w[n] = nc.dram_tensor(n, sh, F32, kind="ExternalInput")
            cfg, casts, tps = moe_cfg_l0(E, w)
        else:
            w["W_router"] = nc.dram_tensor("W_router", [8, 1024], F32, kind="ExternalInput")
            for n, sh in (("W_gate", [8, 4096, 1024]), ("W_up", [8, 4096, 1024]),
                          ("W_down", [8, 1024, 4096])):
                w[n] = nc.dram_tensor(n, sh, F32, kind="ExternalInput")
            cfg, casts, tps = moe_cfg_l1(E, w)
        R = nc.dram_tensor("Rres", [S, D], F32)
        Gf = nc.dram_tensor("Gf", [8, S], F32)
        cfg["R"], cfg["Gf"] = R, Gf

        s_init_y = E.sem("inity")
        s_init_g = E.sem("initg")
        E.op("y", lambda y: y.dma_start(E.t["IDF"][:, :], identf.ap()), s_init_y, 16)
        E.op("g", lambda g: g.dma_start(R.ap(), xin.ap()), s_init_g, 16)
        emit_weight_prep(E, casts, tps)
        E.wait("y", s_init_y)
        E.wait("g", s_init_g)
        E.wait("t", s_init_y)
        E.flush()

        E.begin_phase()
        emit_moe_phase(E, cfg)
        E.flush()
        E.end_phase()

        s_out = E.sem("outc")
        E.op("g", lambda g: g.dma_start(out.ap(), R.ap()), s_out, 16)
        import os
        if os.environ.get("DBGDUMP") == "1":
            dXT = nc.dram_tensor("dXT", [128, NB * 1024], F32, kind="ExternalOutput")
            dXF = nc.dram_tensor("dXF", [128, DCH * TC], F32, kind="ExternalOutput")
            dHH = nc.dram_tensor("dHH", [128, 32 * TC], F32, kind="ExternalOutput")
            dYY = nc.dram_tensor("dYY", [128, NB * 1024], F32, kind="ExternalOutput")
            dGf = nc.dram_tensor("dGf", [8, S], F32, kind="ExternalOutput")
            E.op("g", lambda g: g.dma_start(dXT.ap(), E.t["XT"][:, :]), s_out, 16)
            E.op("g", lambda g: g.dma_start(dXF.ap(), E.t["XF"][:, :]), s_out, 16)
            E.op("g", lambda g: g.dma_start(dHH.ap(), E.t["HH"][:, :]), s_out, 16)
            E.op("g", lambda g: g.dma_start(dYY.ap(), E.t["YY"][:, :]), s_out, 16)
            E.op("g", lambda g: g.dma_start(dGf.ap(), Gf.ap()), s_out, 16)
        E.wait("g", s_out)
        E.flush()
    return nc



def emit_l0attn_phase(E, cfg):
    tag = "l0a"
    R = cfg["R"]
    W = cfg["w"]
    PHIQ, ZBUF, RDH, KSROW = cfg["PHIQ"], cfg["ZBUF"], cfg["RDH"], cfg["KSROW"]

    XT, XF, HH, YY = E.t["XT"], E.t["XF"], E.t["HH"], E.t["YY"]
    WA, GS, GB, SS, MISC = E.t["WA"], E.t["GS"], E.t["GB"], E.t["SS"], E.t["MISC"]
    IDF, CONST = E.t["IDF"], E.t["CONST"]
    ROWS = E.salloc("ROWS", [1, 4096], BF16)
    BIAS = E.salloc("BIAS", [128, 64], F32)
    ONESR = E.salloc("ONESR", [1, 128], BF16)
    ONESC = E.salloc("ONESC", [128, 1], BF16)
    KVS = E.salloc("KVS", [128, 512], BF16)
    P = [E.t[f"P{i}"] for i in range(8)]
    KEPS = 0.001
    NT = NB * DCH

    s = {n: E.sem(f"{tag}_{n}") for n in
         ("x sq var rstd ln tp tpe bld bt bte rows wkv wq zacc zst qm qe "
          "kvm kve acc kevac ks1 ks2 ksst ksld pqst pqld pqtd zld den rdst "
          "num ae wom woe ye yst wo w1 w2 mm mhe dmm wdf").split()}
    s_rb = [E.sem(f"{tag}_rb{i}") for i in range(2)]
    s_w1p = [E.sem(f"{tag}_w1p{i}") for i in range(2)]
    s_w2p = [E.sem(f"{tag}_w2p{i}") for i in range(2)]

    E.op("v", lambda v: v.memset(ONESR[0:1, 0:128], 1.0))
    E.op("v", lambda v: v.memset(ONESC[:, 0:1], 1.0))
    E.op("v", lambda v: v.memset(CONST[:, 2:3], KEPS), s["bte"])
    bias_routes = [
        (cfg["ln1_scale"], 0, 8), (cfg["ln1_bias"], 8, 8),
        (cfg["ln2_scale"], 16, 8), (cfg["ln2_bias"], 24, 8),
        (cfg["b1"], 32, 32),
    ]
    for i, (src, col, w_) in enumerate(bias_routes):
        if i >= 1:
            E.wait("y", s["bt"], i)
        E.op("y", lambda y, a=src, ww=w_: y.dma_start(
            MISC[0:ww, 0:128], a.ap().rearrange("(c p) -> c p", p=128)),
            s["bld"], 16)
        E.wait("t", s["bld"])
        bank = P[i % 2]
        if i >= 2:
            E.wait("t", s["bte"], i)
        E.op("t", lambda t, ww=w_, bk=bank: t.transpose(
            bk[:, 0:ww], MISC[0:ww, 0:128], IDF[0:ww, 0:ww]), s["bt"])
        E.wait("v", s["bt"])
        E.op("v", lambda v, ww=w_, cc=col, bk=bank: v.tensor_copy(
            BIAS[:, cc:cc + ww], bk[:, 0:ww]), s["bte"])
    E.op("g", lambda g: g.dma_start(
        ROWS[0:1, 0:1024], cfg["bk"].ap().rearrange("h m -> (h m)")
        .rearrange("(a b) -> a b", a=1)), s["rows"], 16)
    E.op("g", lambda g: g.dma_start(
        ROWS[0:1, 1024:2048], cfg["bv"].ap().rearrange("h m -> (h m)")
        .rearrange("(a b) -> a b", a=1)), s["rows"], 16)
    E.op("g", lambda g: g.dma_start(
        ROWS[0:1, 3072:4096], cfg["bo"].ap().rearrange("(a b) -> a b", a=1)),
        s["rows"], 16)
    E.op("g", lambda g: g.dma_start(
        ROWS[0:1, 2048:3072], cfg["bq"].ap().rearrange("h m -> (h m)")
        .rearrange("(a b) -> a b", a=1)), s["rows"], 16)
    WQ_OFF = 24576
    for c in range(DCH):
        E.op("y", lambda y, cc=c: y.dma_start(
            WA[:, cc * 2048: cc * 2048 + 1024],
            W["wk"].ap()[cc * 128:(cc + 1) * 128, :]), s["wkv"], 16)
        E.op("y", lambda y, cc=c: y.dma_start(
            WA[:, cc * 2048 + 1024: (cc + 1) * 2048],
            W["wv"].ap()[cc * 128:(cc + 1) * 128, :]), s["wkv"], 16)
        E.op("y", lambda y, cc=c: y.dma_start(
            HH[:, WQ_OFF + cc * 1024: WQ_OFF + (cc + 1) * 1024],
            W["wq"].ap()[cc * 128:(cc + 1) * 128, :]), s["wq"], 16)

    PQ_OFF, PK_OFF, PV_OFF = 0, 8192, 16384
    kv_started = {0: False, 64: False}
    ks_started = [False]

    for k in range(NCH):
        if k >= 1:
            E.wait("g", s["tp"], k * NT)
        E.op("g", lambda g, kk=k: g.dma_start(
            XT[:, :].rearrange("p (b d) -> p b d", b=NB),
            R.ap().rearrange("(k b p) d -> k p b d", p=128, b=NB)[kk]),
            s["x"], 16)
        E.pump("g", 2)
        if k == 0:
            E.wait("s", s["bt"], 5)
        emit_ln(E, (s["x"], s["sq"], s["var"], s["rstd"], s["ln"]), k, 1e-5)
        E.wait("g", s["ln"])
        E.op("g", lambda g, kk=k: g.dma_start(
            R.ap().rearrange("(k b p) d -> k p b d", p=128, b=NB)[kk],
            XT[:, :].rearrange("p (b d) -> p b d", b=NB),
            accum_op=ALU.add), s["zacc"], 16)
        E.op("g", lambda g, kk=k: g.dma_start(
            ZBUF.ap().rearrange("(k b p) d -> k p b d", p=128, b=NB)[kk],
            XT[:, :].rearrange("p (b d) -> p b d", b=NB)), s["zst"], 16)
        E.pump("g", 2)
        E.wait("v", s["zacc"])
        E.wait("s", s["zacc"])
        emit_ln(E, (s["zst"], s["sq"], s["var"], s["rstd"], s["ln"]), k, 1e-6)
        for ti in range(NT):
            bank = (P[0], P[1])[ti % 2]
            b, c = ti // DCH, ti % DCH
            E.wait("t", s["ln"], 2 * k * NB + NB + b + 1)
            if ti >= 2:
                E.wait("t", s["tpe"], k * NT + ti - 1)
            else:
                E.wait("t", s["qe"])
                E.wait("t", s["bte"])
            E.op("t", lambda t, bb=b, cc=c, bk=bank: t.transpose(
                bk[:, 0:128],
                XT[:, bb * 1024 + cc * 128: bb * 1024 + (cc + 1) * 128],
                IDF[:, :]), s["tp"])
            E.wait("v", s["tp"])
            E.op("v", lambda v, bb=b, cc=c, bk=bank: v.tensor_scalar(
                XF[:, cc * TC + bb * 128: cc * TC + (bb + 1) * 128],
                bk[:, 0:128], BIAS[:, cc:cc + 1], BIAS[:, 8 + cc:9 + cc],
                ALU.mult, ALU.add), s["tpe"])
        E.wait("t", s["wkv"])
        E.wait("t", s["wq"])
        E.wait("t", s["rows"], 64)
        for b in range(NB):
            if b == 0:
                E.wait("t", s["tpe"], (k + 1) * NT)
            else:
                E.wait("t", s["qe"])
            for g_ in range(2):
                bank = P[g_]
                for c in range(DCH):
                    E.op("t", lambda t, bb=b, cc=c, gg=g_, bk=bank: t.matmul(
                        bk[:, :],
                        XF[:, cc * TC + bb * 128: cc * TC + (bb + 1) * 128],
                        HH[:, WQ_OFF + cc * 1024 + gg * 512:
                           WQ_OFF + cc * 1024 + (gg + 1) * 512],
                        start=(cc == 0), stop=False))
                E.op("t", lambda t, gg=g_, bk=bank: t.matmul(
                    bk[:, :], ONESR[0:1, 0:128],
                    ROWS[0:1, 2048 + gg * 512: 2048 + (gg + 1) * 512],
                    start=False, stop=True), s["qm"])
            if b == 0:
                E.wait("s", s["pqst"])
            for g_ in range(2):
                E.wait("s", s["qm"], s["qm"].n - 2 + g_ + 1)
                E.op("s", lambda sc, bb=b, gg=g_, bk=P[g_]: sc.activation(
                    HH[:, PQ_OFF + bb * 1024 + gg * 512:
                       PQ_OFF + bb * 1024 + (gg + 1) * 512],
                    bk[:, :], AF.Relu))
                E.op("s", lambda sc, bb=b, gg=g_: sc.add(
                    HH[:, PQ_OFF + bb * 1024 + gg * 512:
                       PQ_OFF + bb * 1024 + (gg + 1) * 512],
                    HH[:, PQ_OFF + bb * 1024 + gg * 512:
                       PQ_OFF + bb * 1024 + (gg + 1) * 512],
                    CONST[:, 2:3]), s["qe"])
            E.wait("t", s["kve"])
            for g_ in range(4):
                bank = P[2 + g_]
                for c in range(DCH):
                    E.op("t", lambda t, bb=b, cc=c, gg=g_, bk=bank: t.matmul(
                        bk[:, :],
                        XF[:, cc * TC + bb * 128: cc * TC + (bb + 1) * 128],
                        WA[:, cc * 2048 + gg * 512:cc * 2048 + (gg + 1) * 512],
                        start=(cc == 0), stop=False))
                E.op("t", lambda t, gg=g_, bk=bank: t.matmul(
                    bk[:, :], ONESR[0:1, 0:128],
                    ROWS[0:1, gg * 512:(gg + 1) * 512],
                    start=False, stop=True), s["kvm"])
            if b == 0:
                E.wait("s", s["acc"])
            for g_ in range(4):
                E.wait("s", s["kvm"], s["kvm"].n - 4 + g_ + 1)
                if g_ < 2:
                    E.op("s", lambda sc, bb=b, gg=g_, bk=P[2 + g_]: sc.activation(
                        HH[:, PK_OFF + bb * 1024 + gg * 512:
                           PK_OFF + bb * 1024 + (gg + 1) * 512],
                        bk[:, :], AF.Relu))
                    E.op("s", lambda sc, bb=b, gg=g_: sc.add(
                        HH[:, PK_OFF + bb * 1024 + gg * 512:
                           PK_OFF + bb * 1024 + (gg + 1) * 512],
                        HH[:, PK_OFF + bb * 1024 + gg * 512:
                           PK_OFF + bb * 1024 + (gg + 1) * 512],
                        CONST[:, 2:3]), s["kve"])
                else:
                    E.op("s", lambda sc, bb=b, gg=g_, bk=P[2 + g_]: sc.copy(
                        HH[:, PV_OFF + bb * 1024 + (gg - 2) * 512:
                           PV_OFF + bb * 1024 + (gg - 1) * 512],
                        bk[:, :]), s["kve"])
            E.wait("t", s["kve"])
            for h in range(16):
                pb_ = 64 * (h % 2)
                st_ = not kv_started[pb_]
                kv_started[pb_] = True
                E.op("t", lambda t, bb=b, hh=h, st2=st_: t.matmul(
                    P[7][64 * (hh % 2):64 * (hh % 2) + 64,
                         (hh // 2) * 64:(hh // 2) * 64 + 64],
                    HH[:, PK_OFF + bb * 1024 + hh * 64:
                       PK_OFF + bb * 1024 + (hh + 1) * 64],
                    HH[:, PV_OFF + bb * 1024 + hh * 64:
                       PV_OFF + bb * 1024 + (hh + 1) * 64],
                    start=st2, stop=False, skip_group_check=True), s["acc"])
            for hs in range(DCH):
                st_ = not ks_started[0]
                ks_started[0] = True
                E.op("t", lambda t, bb=b, ss2=hs, st2=st_: t.matmul(
                    P[6][:, ss2:ss2 + 1],
                    HH[:, PK_OFF + bb * 1024 + ss2 * 128:
                       PK_OFF + bb * 1024 + (ss2 + 1) * 128],
                    ONESC[:, 0:1],
                    start=st2, stop=False, skip_group_check=True), s["acc"])
        E.wait("g", s["qe"])
        E.op("g", lambda g, kk=k: g.dma_start(
            PHIQ.ap().rearrange("(k b p) d -> k p b d", p=128, b=NB)[kk],
            HH[:, PQ_OFF:PQ_OFF + 8192].rearrange("p (b d) -> p b d", b=NB)),
            s["pqst"], 16)
        E.pump("g", 2)

    E.wait("v", s["acc"])
    E.op("v", lambda v: v.tensor_copy(KVS[:, 0:512], P[7][:, 0:512]))
    E.op("v", lambda v: v.tensor_copy(SS[:, 64:72], P[6][:, 0:8]), s["kevac"])
    E.wait("t", s["kevac"])
    E.op("t", lambda t: t.transpose(P[0][0:8, 0:128], SS[:, 64:72], IDF[:, :]),
         s["ks1"])
    E.wait("v", s["ks1"])
    E.op("v", lambda v: v.tensor_copy(MISC[0:8, 0:128], P[0][0:8, 0:128]),
         s["ks2"])
    E.wait("g", s["ks2"])
    E.op("g", lambda g: g.dma_start(KSROW.ap(), MISC[0:8, 0:128]), s["ksst"], 16)
    E.wait("g", s["ksst"])
    E.op("g", lambda g: g.dma_start(
        GS[:, :], bass.AP(KSROW, 0, [[0, 128], [1, 1024]])), s["ksld"], 16)
    if cfg.get("KVSD") is not None:
        s["kvd"] = E.sem("l0a_kvd")
        E.wait("g", s["kevac"])
        E.op("g", lambda g: g.dma_start(cfg["KVSD"].ap(), KVS[:, :]), s["kvd"], 16)
        E.wait("g", s["kvd"])
    E.wait("y", s["acc"])
    for c in range(DCH):
        E.op("y", lambda y, cc=c: y.dma_start(
            WA[:, cc * 1024:(cc + 1) * 1024],
            W["wo"].ap()[cc * 128:(cc + 1) * 128, :]), s["wo"], 16)

    E.wait("g", s["kvm"])
    E.op("g", lambda g: g.dma_start(
        ROWS[0:1, 0:1024], cfg["b2m"].ap().rearrange("(a b) -> a b", a=1)),
        s["rows"], 16)
    W1_OFF = [8192, 12288]
    W2_OFF = [16384, 17408]
    slab_i = 0
    dfc_i = 0
    rd_i = 0
    ft_w1 = []
    ft_w2 = []
    ft_rb = []
    for k in range(NCH):
        E.wait("g", s["dmm"])
        E.wait("g", s["pqst"])
        E.op("g", lambda g, kk=k: g.dma_start(
            HH[:, PQ_OFF:PQ_OFF + 8192].rearrange("p (b d) -> p b d", b=NB),
            PHIQ.ap().rearrange("(k b p) d -> k p b d", p=128, b=NB)[kk]),
            s["pqld"], 16)
        E.pump("g", 2)
        E.wait("y", s["mm"])
        for c in range(DCH):
            E.op("y", lambda y, kk=k, cc=c: y.dma_start(
                XF[:, cc * TC:(cc + 1) * TC],
                PHIQ.ap()[kk * TC:(kk + 1) * TC, cc * 128:(cc + 1) * 128],
                transpose=True), s["pqtd"], 16)
        E.wait("v", s["pqld"])
        E.wait("v", s["ksld"])
        E.wait("v", s["sq"])
        for b in range(NB):
            if k > 0 or b > 0:
                E.wait("v", s["rdst"])
            E.op("v", lambda v, bb=b: v.tensor_mul(
                MISC[:, 0:1024],
                HH[:, PQ_OFF + bb * 1024:PQ_OFF + (bb + 1) * 1024], GS[:, :]))
            E.op("v", lambda v: v.tensor_reduce(
                SS[:, 80:96], MISC[:, 0:1024].rearrange("p (h m) -> p h m", h=16),
                AX, ALU.add))
            E.op("v", lambda v: v.reciprocal(SS[:, 96:112], SS[:, 80:96]),
                 s["den"])
            E.wait("t", s["den"])
            if k > 0 or b > 0:
                E.wait("t", s["ks2"])
            E.op("t", lambda t: t.transpose(
                P[6][0:16, 0:128], SS[:, 96:112], IDF[:, :]), s["ks1"])
            E.wait("v", s["ks1"])
            E.op("v", lambda v: v.tensor_copy(SS[0:16, 144:272],
                                              P[6][0:16, 0:128]), s["ks2"])
            E.wait("g", s["ks2"])
            E.op("g", lambda g, kk=k, bb=b: g.dma_start(
                RDH.ap()[:, kk * TC + bb * 128: kk * TC + (bb + 1) * 128],
                SS[0:16, 144:272]), s["rdst"], 16)
        E.wait("g", s["rdst"])
        rb_need = {}

        def load_rb(h, kk):
            nonlocal rd_i
            pb = 64 * (h % 2)
            slot = rd_i % 2
            if rd_i >= 2:
                fsem, fval = ft_rb[rd_i - 2]
                E.wait("g", fsem, fval)
            E.op("g", lambda g, hh=h, k2=kk, sl=slot, pb2=pb: g.dma_start(
                GB[pb2:pb2 + 64, sl * 1024:(sl + 1) * 1024],
                bass.AP(RDH, hh * S + k2 * TC, [[0, 64], [1, TC]])),
                s_rb[slot], 16)
            E.pump("g", 1)
            rb_need[h] = (slot, s_rb[slot].n)
            rd_i += 1

        load_rb(0, k)
        load_rb(1, k)
        ae_base = s["ae"].n
        for h in range(16):
            pb = 64 * (h % 2)
            slot, rbv = rb_need[h]
            c = h // 2
            for tg in range(2):
                pi = h * 2 + tg
                bank = P[2 + pi % 4]
                if pi >= 4:
                    E.wait("t", s["ae"], ae_base + pi - 4 + 1)
                else:
                    E.wait("t", s["mhe"])
                E.wait("t", s["pqtd"], (k + 1) * DCH * 16)
                E.op("t", lambda t, hh=h, cc=c, tgg=tg, bk=bank, pb2=pb: t.matmul(
                    bk[pb2:pb2 + 64, :],
                    KVS[64 * (hh % 2):64 * (hh % 2) + 64,
                        (hh // 2) * 64:(hh // 2) * 64 + 64],
                    XF[64 * (hh % 2):64 * (hh % 2) + 64,
                       cc * TC + tgg * 512: cc * TC + (tgg + 1) * 512],
                    start=True, stop=True), s["num"])
                E.wait("v", s["num"])
                E.wait("v", s_rb[slot], rbv)
                E.op("v", lambda v, cc=c, tgg=tg, bk=bank, pb2=pb, sl=slot: v.tensor_mul(
                    XF[pb2:pb2 + 64, cc * TC + tgg * 512: cc * TC + (tgg + 1) * 512],
                    bk[pb2:pb2 + 64, :],
                    GB[pb2:pb2 + 64, sl * 1024 + tgg * 512:
                       sl * 1024 + (tgg + 1) * 512]), s["ae"])
            ft_rb.append((s["ae"], s["ae"].n))
            if h + 2 < 16:
                load_rb(h + 2, k)
        import os as _os2
        if _os2.environ.get("DBGDUMP") == "1" and "DATTN" in E.t:
            s.setdefault("datn", E.sem("l0a_datn"))
            E.wait("g", s["ae"])
            E.op("g", lambda g, kk=k: g.dma_start(
                E.t["DATTN"].ap()[:, kk * 8192:(kk + 1) * 8192], XF[:, :]),
                s["datn"], 16)
            E.wait("t", s["datn"])
        E.op("g", lambda g, kk=k: g.dma_start(
            XT[:, :].rearrange("p (b d) -> p b d", b=NB),
            ZBUF.ap().rearrange("(k b p) d -> k p b d", p=128, b=NB)[kk]),
            s["zld"], 16)
        E.wait("t", s["wo"])
        E.wait("t", s["ae"])
        E.wait("t", s["tpe"])
        E.wait("t", s["ye"])
        for b in range(NB):
            if not (b == 0 and k == 0):
                E.wait("t", s["woe"])
            for g_ in range(2):
                bank = P[g_]
                for c in range(DCH):
                    E.op("t", lambda t, bb=b, cc=c, gg=g_, bk=bank: t.matmul(
                        bk[:, :],
                        XF[:, cc * TC + bb * 128: cc * TC + (bb + 1) * 128],
                        WA[:, cc * 1024 + gg * 512: cc * 1024 + (gg + 1) * 512],
                        start=(cc == 0), stop=False))
                E.op("t", lambda t, gg=g_, bk=bank: t.matmul(
                    bk[:, :], ONESR[0:1, 0:128],
                    ROWS[0:1, 3072 + gg * 512: 3072 + (gg + 1) * 512],
                    start=False, stop=True), s["wom"])
            E.wait("v", s["wom"])
            E.wait("v", s["zld"])
            if b == 0 and k > 0:
                E.wait("v", s["yst"], k * 16)
            for g_ in range(2):
                bank = P[g_]
                yy = YY[:, b * 1024 + g_ * 512: b * 1024 + (g_ + 1) * 512]
                xt_ = XT[:, b * 1024 + g_ * 512: b * 1024 + (g_ + 1) * 512]
                E.op("v", lambda v, bk=bank, yy=yy: v.tensor_copy(yy, bk[:, :]))
                E.op("v", lambda v, bk=bank, xt2=xt_: v.tensor_add(
                    xt2, xt2, bk[:, :]), s["woe"])
        import os as _os
        if _os.environ.get("DBGDUMP") == "1" and "DXRES" in E.t:
            s.setdefault("dxr", E.sem("l0a_dxr"))
            E.wait("g", s["woe"])
            E.op("g", lambda g, kk=k: g.dma_start(
                E.t["DXRES"].ap().rearrange("(k b p) d -> k p b d", p=128, b=NB)[kk],
                XT[:, :].rearrange("p (b d) -> p b d", b=NB)), s["dxr"], 16)
            E.wait("v", s["dxr"])
            E.wait("s", s["dxr"])
        emit_ln(E, (s["woe"], s["sq"], s["var"], s["rstd"], s["ln"]), k, 1e-6)
        for ti in range(NT):
            bank = (P[0], P[1])[ti % 2]
            b, c = ti // DCH, ti % DCH
            E.wait("t", s["ln"], (2 * NCH + k) * NB + b + 1)
            if ti >= 2:
                E.wait("t", s["tpe"], (NCH + k) * NT + ti - 1)
            else:
                E.wait("t", s["woe"])
            E.op("t", lambda t, bb=b, cc=c, bk=bank: t.transpose(
                bk[:, 0:128],
                XT[:, bb * 1024 + cc * 128: bb * 1024 + (cc + 1) * 128],
                IDF[:, :]), s["tp"])
            E.wait("v", s["tp"])
            E.op("v", lambda v, bb=b, cc=c, bk=bank: v.tensor_scalar(
                XF[:, cc * TC + bb * 128: cc * TC + (bb + 1) * 128],
                bk[:, 0:128], BIAS[:, 16 + cc:17 + cc], BIAS[:, 24 + cc:25 + cc],
                ALU.mult, ALU.add), s["tpe"])
        w1need = 0
        slab_par = 0
        for fs in range(32):
            if fs % 4 == 0:
                sl = fs // 4
                par = slab_i % 2
                if slab_i >= 2:
                    fsem, fval = ft_w1[slab_i - 2]
                    E.wait("y", fsem, fval)
                E.op("y", lambda y, ss=sl, p=par: y.dma_start(
                    WA[:, W1_OFF[p]:W1_OFF[p] + 4096]
                    .rearrange("p (d c) -> p d c", d=8),
                    W["w1"].ap().rearrange("(d p) f -> p d f", p=128)
                    [:, :, ss * 512:(ss + 1) * 512]), s_w1p[par], 16)
                w1need = s_w1p[par].n
                slab_par = par
                slab_i += 1
            for tg in range(NTG):
                pi = fs * 2 + tg
                gpi = k * 64 + pi
                bank = P[2 + pi % 4]
                if gpi >= 4:
                    E.wait("t", s["mhe"], gpi - 4 + 1)
                else:
                    E.wait("t", s["ae"])
                E.wait("t", s_w1p[slab_par], w1need)
                if pi == 0:
                    E.wait("t", s["tpe"], (NCH + k + 1) * NT)
                for c in range(DCH):
                    E.op("t", lambda t, cc=c, fss=fs, tgg=tg, p=slab_par,
                         bk=bank: t.matmul(
                        bk[:, :],
                        WA[:, W1_OFF[p] + cc * 512 + (fss % 4) * 128:
                           W1_OFF[p] + cc * 512 + (fss % 4) * 128 + 128],
                        XF[:, cc * TC + tgg * 512: cc * TC + (tgg + 1) * 512],
                        start=(cc == 0), stop=(cc == DCH - 1)),
                        s["mm"] if c == DCH - 1 else None)
                if tg == NTG - 1 and fs % 4 == 3:
                    ft_w1.append((s["mm"], s["mm"].n))
                E.wait("s", s["mm"])
                if gpi >= 4:
                    E.wait("s", s["mhe"], gpi - 4 + 1)
                if pi == 0:
                    E.wait("s", s["dmm"])
                E.op("s", lambda sc, fss=fs, tgg=tg, bk=bank: sc.activation(
                    HH[:, fss * TC + tgg * 512: fss * TC + (tgg + 1) * 512],
                    bk[:, :], AF.Gelu_apprx_tanh,
                    bias=BIAS[:, 32 + fss:33 + fss]), s["mhe"])
        E.wait("t", s["mhe"])
        for ps_ in range(2):
            for fc in range(32):
                par = dfc_i % 2
                if dfc_i >= 2:
                    fsem, fval = ft_w2[dfc_i - 2]
                    E.wait("y", fsem, fval)
                E.op("y", lambda y, f=fc, p=par: y.dma_start(
                    WA[:, W2_OFF[p]:W2_OFF[p] + 1024],
                    W["w2"].ap()[f * 128:(f + 1) * 128, :]), s_w2p[par], 16)
                need = s_w2p[par].n
                E.wait("t", s_w2p[par], need)
                if fc == 0:
                    E.wait("t", s["ye"])
                    E.wait("t", s["woe"])
                    E.wait("t", s["ks2"])
                last_fc = (fc == 31)
                if fc == 0:
                    E.wait("t", s["rows"], 80)
                for ts_ in range(4):
                    tsub = ps_ * 4 + ts_
                    for half in range(2):
                        bk = P[2 * ts_ + half]
                        msem = s["wdf"] if (ts_ == 3 and half == 1) else None
                        E.op("t", lambda t, f=fc, tb=tsub, h=half, bk=bk,
                             p=par: t.matmul(
                            bk[:, :],
                            HH[:, f * TC + tb * 128: f * TC + (tb + 1) * 128],
                            WA[:, W2_OFF[p] + h * 512: W2_OFF[p] + (h + 1) * 512],
                            start=(f == 0), stop=False), msem)
                if last_fc:
                    for ts_ in range(4):
                        for half in range(2):
                            bk = P[2 * ts_ + half]
                            E.op("t", lambda t, h=half, bk=bk: t.matmul(
                                bk[:, :], ONESR[0:1, 0:128],
                                ROWS[0:1, h * 512:(h + 1) * 512],
                                start=False, stop=True),
                                s["dmm"] if half == 1 else None)
                    ft_w2.append((s["dmm"], s["dmm"].n))
                else:
                    ft_w2.append((s["wdf"], s["wdf"].n))
                dfc_i += 1
            for ts_ in range(4):
                tsub = ps_ * 4 + ts_
                E.wait("v", s["dmm"], s["dmm"].n - 4 + ts_ + 1)
                for half in range(2):
                    bk = P[2 * ts_ + half]
                    yy = YY[:, tsub * 1024 + half * 512:
                            tsub * 1024 + (half + 1) * 512]
                    E.op("v", lambda v, bk=bk, yy=yy: v.tensor_add(
                        yy, yy, bk[:, :]), s["ye"])
        E.wait("g", s["ye"])
        E.op("g", lambda g, kk=k: g.dma_start(
            R.ap().rearrange("(k b p) d -> k p b d", p=128, b=NB)[kk],
            YY[:, :].rearrange("p (b d) -> p b d", b=NB),
            accum_op=ALU.add), s["yst"], 16)
        E.pump("g", 2)
    E.wait("g", s["yst"])
    return s

def build_l0attn_single():
    nc = bass.Bass("TRN2", target_bir_lowering=False)
    stack = ExitStack()
    E = Emitter(nc, stack)
    with stack:
        alloc_arenas(E)
        xin = nc.dram_tensor("xin", [S, D], F32, kind="ExternalInput")
        out = nc.dram_tensor("out", [S, D], F32, kind="ExternalOutput")
        identf = nc.dram_tensor("identf", [128, 128], F32, kind="ExternalInput")
        ext = {}
        for n, sh in (("Wq", [1024, 16, 64]), ("bq", [16, 64]),
                      ("Wk", [1024, 16, 64]), ("bk", [16, 64]),
                      ("Wv", [1024, 16, 64]), ("bv", [16, 64]),
                      ("Wo", [16, 64, 1024]), ("bo", [1024]),
                      ("ln1_scale", [1024]), ("ln1_bias", [1024]),
                      ("ln2_scale", [1024]), ("ln2_bias", [1024]),
                      ("W1", [1024, 4096]), ("b1", [4096]),
                      ("W2", [4096, 1024]), ("b2", [1024])):
            ext[n] = nc.dram_tensor(n, sh, F32, kind="ExternalInput")
        wbf = {}
        casts = []
        for n, src_ap, sh in (
            ("wq", ext["Wq"].ap().rearrange("a h m -> a (h m)"), [1024, 1024]),
            ("wk", ext["Wk"].ap().rearrange("a h m -> a (h m)"), [1024, 1024]),
            ("wv", ext["Wv"].ap().rearrange("a h m -> a (h m)"), [1024, 1024]),
            ("wo", ext["Wo"].ap().rearrange("h m d -> (h m) d"), [1024, 1024]),
            ("w1", ext["W1"].ap(), [1024, 4096]),
            ("w2", ext["W2"].ap(), [4096, 1024]),
        ):
            wbf[n] = nc.dram_tensor(f"l0a_{n}_bf", sh, BF16)
            casts.append((src_ap, wbf[n]))
        R = nc.dram_tensor("Rres", [S, D], F32)
        import os as _os0
        DXRES = nc.dram_tensor("DXRES", [S, D], F32)
        KVSD = nc.dram_tensor("KVSD", [128, 512], F32)
        PHIQ = nc.dram_tensor("PHIQ", [S, D], BF16)
        ZBUF = nc.dram_tensor("ZBUF", [S, D], F32)
        RDH = nc.dram_tensor("RDH", [16, S], F32)
        KSROW = nc.dram_tensor("KSROW", [8, 128], F32)
        cfg = dict(R=R, PHIQ=PHIQ, ZBUF=ZBUF, RDH=RDH, KSROW=KSROW,
                   w=wbf,
                   ln1_scale=ext["ln1_scale"], ln1_bias=ext["ln1_bias"],
                   ln2_scale=ext["ln2_scale"], ln2_bias=ext["ln2_bias"],
                   b1=ext["b1"],
                   bk=ext["bk"], bv=ext["bv"], bq=ext["bq"],
                   bo=ext["bo"], b2m=ext["b2"])

        s_init_y = E.sem("inity")
        s_init_g = E.sem("initg")
        E.op("y", lambda y: y.dma_start(E.t["IDF"][:, :], identf.ap()),
             s_init_y, 16)
        E.op("g", lambda g: g.dma_start(R.ap(), xin.ap()), s_init_g, 16)
        emit_weight_prep(E, casts, [])
        E.wait("y", s_init_y)
        E.wait("g", s_init_g)
        E.wait("t", s_init_y)
        E.flush()

        E.begin_phase()
        if _os0.environ.get("DBGDUMP") == "1":
            E.t["DXRES"] = DXRES
            E.t["DATTN"] = nc.dram_tensor("DATTN", [128, 4 * 8192], F32)
            cfg["KVSD"] = KVSD
        emit_l0attn_phase(E, cfg)
        E.flush()
        E.end_phase()

        s_out = E.sem("outc")
        E.op("g", lambda g: g.dma_start(out.ap(), R.ap()), s_out, 16)
        import os as _os
        if _os.environ.get("DBGDUMP") == "1":
            dkv = nc.dram_tensor("dkv", [128, 512], F32, kind="ExternalOutput")
            datn = nc.dram_tensor("datn", [128, 4 * 8192], F32, kind="ExternalOutput")
            E.op("g", lambda g: g.dma_start(datn.ap(), E.t["DATTN"].ap()), s_out, 16)
            dxr2 = nc.dram_tensor("dxr2", [S, D], F32, kind="ExternalOutput")
            E.op("g", lambda g: g.dma_start(dxr2.ap(), E.t2_DXRES.ap() if False else DXRES.ap()), s_out, 16)
            E.op("g", lambda g: g.dma_start(dkv.ap(), KVSD.ap()), s_out, 16)
            dz = nc.dram_tensor("dz", [S, D], F32, kind="ExternalOutput")
            dpq = nc.dram_tensor("dpq", [S, D], F32, kind="ExternalOutput")
            drd = nc.dram_tensor("drd", [16, S], F32, kind="ExternalOutput")
            dks = nc.dram_tensor("dks", [8, 128], F32, kind="ExternalOutput")
            E.op("g", lambda g: g.dma_start(dz.ap(), ZBUF.ap()), s_out, 16)
            E.op("g", lambda g: g.dma_start(dpq.ap(), PHIQ.ap()), s_out, 16)
            E.op("g", lambda g: g.dma_start(drd.ap(), RDH.ap()), s_out, 16)
            E.op("g", lambda g: g.dma_start(dks.ap(), KSROW.ap()), s_out, 16)
        E.wait("g", s_out)
        E.flush()
    return nc

def emit_l1attn_phase(E, cfg):
    tag = "l1a"
    R = cfg["R"]
    UVT, OWT = cfg["uvT"], cfg["owT"]
    UF, VTOK, WPAD = cfg["UF"], cfg["VT"], cfg["WPAD"]

    XT, XF, HH, YY = E.t["XT"], E.t["XF"], E.t["HH"], E.t["YY"]
    WA, GS, GB, SS, MISC = E.t["WA"], E.t["GS"], E.t["GB"], E.t["SS"], E.t["MISC"]
    IDF, CONST = E.t["IDF"], E.t["CONST"]
    BIAS = E.salloc("BIAS", [128, 8], F32)
    GBR = E.salloc("GBR", [128, 640], F32)
    TMP = E.salloc("TMP", [128, 256], F32)
    VSTG = E.salloc("VSTG", [128, 2048], BF16)
    P = [E.t[f"P{i}"] for i in range(8)]
    NT = NB * DCH
    SQS = 0.08838834764831845

    s = {n: E.sem(f"{tag}_{n}") for n in
         ("x sq var rstd ln fac t1 t1st tp tpe gld wp1 wp2 rcs "
          "um ue ust vm ve vst bm be rop rtp rte uvb ow qkm qke sqr "
          "acc accp ae ul om oe ot ote ye yst").split()}
    s_uv = [E.sem(f"{tag}_uv{i}") for i in range(2)]
    s_vl = [E.sem(f"{tag}_vl{i}") for i in range(2)]
    s_wr = [E.sem(f"{tag}_wr{i}") for i in range(2)]
    s_vb = [E.sem(f"{tag}_vb{i}") for i in range(2)]

    E.op("v", lambda v: v.memset(CONST[:, 4:5], SQS), s["rte"])
    E.op("g", lambda g: g.dma_start(
        WPAD.ap().rearrange("(a b) -> a b", a=1)[:, 6144:8191],
        cfg["w_rel_rev"].ap().rearrange("(a b) -> a b", a=1)), s["wp1"], 16)
    E.wait("g", s["wp1"])
    E.op("g", lambda g: g.dma_start(
        WPAD.ap().rearrange("(a b) -> a b", a=1)[:, 0:6144],
        bass.AP(WPAD, 6144, [[0, 6144], [1, 1]])), s["wp2"], 16)
    WREP = cfg["WREP"]
    E.wait("g", s["wp2"])
    for p_ in range(128):
        E.op("g", lambda g, pp=p_: g.dma_start(
            WREP.ap().rearrange("(a b) -> a b", a=128)[pp:pp + 1, :],
            bass.AP(WPAD, 127 - pp, [[8064, 1], [1, 8064]])), s["wp2"], 16)
    for i, off in ((0, 0), (1, 128), (2, 256), (3, 384)):
        src = cfg["gamma"] if i % 2 == 0 else cfg["beta"]
        row = i // 2
        E.op("g", lambda g, a=src, r=row, o=off: g.dma_start(
            GBR[:, o:o + 128], bass.AP(a, r * 128, [[0, 128], [1, 128]])),
            s["gld"], 16)
    E.op("g", lambda g: g.dma_start(
        BIAS[:, 4:5], bass.AP(cfg["g"], 0, [[0, 128], [1, 1]])), s["gld"], 16)
    E.op("g", lambda g: g.dma_start(
        GB[:, 0:1024], bass.AP(cfg["res_scale"], 0, [[0, 128], [1, 1024]])),
        s["gld"], 16)

    UOFF = 0
    QOFF, KOFF = 24576, 28672
    ft_uv = []
    ft_vl = []

    for k in range(NCH):
        if k >= 1:
            E.wait("g", s["tp"], k * NT)
        E.op("g", lambda g, kk=k: g.dma_start(
            XT[:, :].rearrange("p (b d) -> p b d", b=NB),
            R.ap().rearrange("(k b p) d -> k p b d", p=128, b=NB)[kk]),
            s["x"], 16)
        E.wait("s", s["rtp"])
        emit_ln(E, (s["x"], s["sq"], s["var"], s["rstd"], s["ln"]), k, 1e-5,
                apply=False)
        E.wait("v", s["rstd"])
        E.op("v", lambda v: v.tensor_mul(SS[:, 112:120], SS[:, 32:40],
                                         SS[:, 32:40]))
        E.op("v", lambda v: v.tensor_mul(SS[:, 120:128], SS[:, 24:32],
                                         SS[:, 112:120]))
        E.op("v", lambda v: v.scalar_tensor_tensor(
            SS[:, 120:128], SS[:, 112:120], -1e-5, SS[:, 120:128],
            ALU.mult, ALU.add))
        E.op("v", lambda v: v.tensor_scalar_max(SS[:, 120:128], SS[:, 120:128],
                                                1e-10), s["fac"])
        E.wait("s", s["fac"])
        E.op("s", lambda sc: sc.activation(SS[:, 104:112], SS[:, 120:128],
                                           AF.Ln))
        E.op("s", lambda sc: sc.activation(SS[:, 104:112], SS[:, 104:112],
                                           AF.Exp, scale=CONST[:, 1:2]),
             s["fac"])
        E.wait("v", s["fac"])
        E.wait("v", s["gld"])
        E.op("v", lambda v: v.tensor_scalar_mul(SS[:, 104:112], SS[:, 104:112],
                                                BIAS[:, 4:5]))
        E.op("v", lambda v: v.tensor_mul(SS[:, 104:112], SS[:, 104:112],
                                         SS[:, 32:40]), s["fac"])
        for b in range(NB):
            E.op("v", lambda v, bb=b: v.tensor_scalar(
                XT[:, bb * 1024:(bb + 1) * 1024],
                XT[:, bb * 1024:(bb + 1) * 1024],
                SS[:, bb:bb + 1], SS[:, 104 + bb:105 + bb],
                ALU.subtract, ALU.mult), s["ln"])
        if k > 0:
            E.wait("v", s["t1st"], k * 16)
        for b in range(NB):
            E.op("v", lambda v, bb=b: v.tensor_mul(
                YY[:, bb * 1024:(bb + 1) * 1024],
                XT[:, bb * 1024:(bb + 1) * 1024], GB[:, 0:1024]), s["t1"])
        E.wait("g", s["t1"])
        E.op("g", lambda g, kk=k: g.dma_start(
            R.ap().rearrange("(k b p) d -> k p b d", p=128, b=NB)[kk],
            YY[:, :].rearrange("p (b d) -> p b d", b=NB),
            accum_op=ALU.add), s["t1st"], 16)
        emit_xpose(E, k, s["ln"], s["tp"], s["tpe"], s["rte"], s["rte"].n)
        E.wait("t", s["tpe"], (k + 1) * NT)
        for es in range(16):
            ui = k * 16 + es
            par = ui % 2
            if ui >= 2:
                fsem, fval = ft_uv[ui - 2]
                E.wait("y", fsem, fval)
            E.op("y", lambda y, ess=es, p=par: y.dma_start(
                WA[:, p * 1024:(p + 1) * 1024].rearrange("q (d c) -> q d c", d=8),
                UVT.ap().rearrange("(c p r) -> p c r", p=128, r=4224)
                [:, :, ess * 128:(ess + 1) * 128]), s_uv[par], 16)
            need = s_uv[par].n
            E.wait("t", s_uv[par], need)
            for tg in range(2):
                bank = P[tg]
                E.wait("t", s["ue"], ui * 2 + tg - 1)
                for c in range(DCH):
                    E.op("t", lambda t, cc=c, tgg=tg, p=par, bk=bank: t.matmul(
                        bk[:, :],
                        WA[:, p * 1024 + cc * 128: p * 1024 + (cc + 1) * 128],
                        XF[:, cc * TC + tgg * 512: cc * TC + (tgg + 1) * 512],
                        start=(cc == 0), stop=(cc == DCH - 1)),
                        s["um"] if c == DCH - 1 else None)
                E.wait("s", s["um"])
                if es == 0 and tg == 0:
                    E.wait("s", s["ust"])
                E.op("s", lambda sc, ess=es, tgg=tg, bk=bank: sc.activation(
                    HH[:, UOFF + ess * 1024 + tgg * 512:
                       UOFF + ess * 1024 + (tgg + 1) * 512],
                    bk[:, :], AF.Silu), s["ue"])
            ft_uv.append((s["um"], s["um"].n))
        E.wait("g", s["ue"])
        E.op("g", lambda g, kk=k: g.dma_start(
            UF.ap().rearrange("(e p r) -> p e r", p=128, r=S)
            [:, :, kk * TC:(kk + 1) * TC],
            HH[:, UOFF:UOFF + 16384].rearrange("p (e c) -> p e c", e=16)),
            s["ust"], 16)
        for b in range(NB):
            vi = k * NB + b
            if vi >= 1:
                E.wait("t", s["ve"], vi * 4)
            for c in range(DCH):
                vli = vi * DCH + c
                par = vli % 2
                if vli >= 2:
                    fsem, fval = ft_vl[vli - 2]
                    E.wait("y", fsem, fval)
                E.op("y", lambda y, cc=c, p=par: y.dma_start(
                    WA[:, 2048 + p * 2048: 2048 + (p + 1) * 2048],
                    UVT.ap().rearrange("(c p r) -> c p r", p=128, r=4224)
                    [cc][:, 2048:4096]), s_vl[par], 16)
                need = s_vl[par].n
                E.wait("t", s_vl[par], need)
                for g_ in range(4):
                    E.op("t", lambda t, bb=b, cc=c, gg=g_, p=par,
                         bk=P[2 + g_]: t.matmul(
                        bk[:, :],
                        XF[:, cc * TC + bb * 128: cc * TC + (bb + 1) * 128],
                        WA[:, 2048 + p * 2048 + gg * 512:
                           2048 + p * 2048 + (gg + 1) * 512],
                        start=(cc == 0), stop=(cc == DCH - 1)),
                        s["vm"] if g_ == 3 else None)
                ft_vl.append((s["vm"], s["vm"].n))
            E.wait("s", s["vm"])
            if vi >= 1:
                E.wait("s", s["vst"], vi * 16)
            for g_ in range(4):
                E.op("s", lambda sc, gg=g_, bk=P[2 + g_]: sc.activation(
                    VSTG[:, gg * 512:(gg + 1) * 512], bk[:, :], AF.Silu),
                    s["ve"])
            E.wait("g", s["ve"])
            E.op("g", lambda g, kk=k, bb=b: g.dma_start(
                VTOK.ap()[kk * TC + bb * 128: kk * TC + (bb + 1) * 128, :],
                VSTG[:, :]), s["vst"], 16)
        E.op("y", lambda y: y.dma_start(
            WA[:, 6144:7168].rearrange("q (d c) -> q d c", d=8),
            UVT.ap().rearrange("(c p r) -> p c r", p=128, r=4224)
            [:, :, 4096:4224]), s["uvb"], 16)
        E.wait("t", s["uvb"], (k + 1) * 16)
        for b in range(NB):
            bi = k * NB + b
            E.wait("g", s["rop"], bi * 2)
            E.op("g", lambda g, kk=k, bb=b: g.dma_start(
                GBR[:, 512:576],
                cfg["rcos"].ap()[kk * TC + bb * 128: kk * TC + (bb + 1) * 128, :]),
                s["rcs"], 16)
            E.op("g", lambda g, kk=k, bb=b: g.dma_start(
                GBR[:, 576:640],
                cfg["rsin"].ap()[kk * TC + bb * 128: kk * TC + (bb + 1) * 128, :]),
                s["rcs"], 16)
            E.wait("t", s["be"])
            for c in range(DCH):
                E.op("t", lambda t, bb=b, cc=c: t.matmul(
                    P[6][:, 0:128],
                    XF[:, cc * TC + bb * 128: cc * TC + (bb + 1) * 128],
                    WA[:, 6144 + cc * 128: 6144 + (cc + 1) * 128],
                    start=(cc == 0), stop=(cc == DCH - 1)),
                    s["bm"] if c == DCH - 1 else None)
            E.wait("s", s["bm"])
            E.wait("s", s["rop"])
            E.op("s", lambda sc: sc.activation(MISC[:, 0:128], P[6][:, 0:128],
                                               AF.Silu), s["be"])
            E.wait("v", s["be"])
            E.wait("v", s["rcs"])
            for qk_i, (gc, bc) in enumerate(((0, 128), (256, 384))):
                pre = TMP[:, 0:128]
                m1 = MISC[:, 128:192]
                m2 = MISC[:, 192:256]
                t1 = TMP[:, 128:192]
                E.op("v", lambda v, gcc=gc: v.tensor_mul(
                    pre, MISC[:, 0:128], GBR[:, gcc:gcc + 128]))
                E.op("v", lambda v, bcc=bc: v.tensor_add(
                    pre, pre, GBR[:, bcc:bcc + 128]))
                E.op("v", lambda v, m1=m1: v.tensor_mul(
                    m1, pre[:, 0:64], GBR[:, 512:576]))
                E.op("v", lambda v, t1=t1: v.tensor_mul(
                    t1, pre[:, 64:128], GBR[:, 576:640]))
                E.op("v", lambda v, m1=m1, t1=t1: v.tensor_sub(m1, m1, t1))
                E.op("v", lambda v, m2=m2: v.tensor_mul(
                    m2, pre[:, 64:128], GBR[:, 512:576]))
                E.op("v", lambda v, t1=t1: v.tensor_mul(
                    t1, pre[:, 0:64], GBR[:, 576:640]))
                E.op("v", lambda v, m2=m2, t1=t1: v.tensor_add(m2, m2, t1),
                     s["rop"])
                E.wait("t", s["rop"])
                E.wait("t", s["rte"])
                E.op("t", lambda t: t.transpose(
                    P[7][:, 0:128], MISC[:, 128:256], IDF[:, :]), s["rtp"])
                E.wait("v", s["rtp"])
                off = QOFF if qk_i == 0 else KOFF
                E.op("v", lambda v, kk=k, bb=b, o=off: v.tensor_copy(
                    HH[:, o + kk * TC + bb * 128: o + kk * TC + (bb + 1) * 128],
                    P[7][:, 0:128]), s["rte"])
    E.wait("g", s["ust"])
    E.wait("g", s["vst"])
    E.wait("g", s["wp2"])
    import os as _os3
    if _os3.environ.get("DBGDUMP") == "1" and "DQK" in E.t:
        s["dqk"] = E.sem("l1a_dqk")
        E.wait("g", s["rte"])
        E.op("g", lambda g: g.dma_start(
            E.t["DQK"].ap()[:, 0:4096], HH[:, QOFF:QOFF + 4096]), s["dqk"], 16)
        E.op("g", lambda g: g.dma_start(
            E.t["DQK"].ap()[:, 4096:8192], HH[:, KOFF:KOFF + 4096]), s["dqk"], 16)
        E.wait("y", s["dqk"])

    E.wait("y", s["vm"])
    E.wait("y", s["um"])
    E.wait("y", s["bm"])
    E.wait("y", s["wp2"])
    for ec in range(16):
        E.op("y", lambda y, cc=ec: y.dma_start(
            WA[:, cc * 1024:(cc + 1) * 1024],
            OWT.ap().rearrange("(c p r) -> c p r", p=128, r=1024)[cc]),
            s["ow"], 16)
    KER, VSL, USL = 0, 16384, 20480
    ft_v2 = []
    v2_i = 0
    for ib in range(8):
        for js in range(32):
            qi = ib * 32 + js
            bank = P[js % 2]
            wslot = js % 2
            if qi >= 2:
                E.wait("y", s["qke"], qi - 1)
            E.op("y", lambda y, i0=ib * 512, j0=js * 128, w=wslot: y.dma_start(
                GB[:, 1024 + w * 512: 1024 + (w + 1) * 512],
                bass.AP(cfg["WREP"], 3968 + i0 - j0, [[8064, 128], [1, 512]])),
                s_wr[wslot], 16)
            wrneed = s_wr[wslot].n
            if ib == 0 and js < 2:
                E.wait("t", s["ue"])
                E.wait("t", s["rte"])
            if qi >= 2:
                E.wait("t", s["qke"], qi - 1)
            E.op("t", lambda t, jss=js, i0=ib * 512, bk=bank: t.matmul(
                bk[:, :],
                HH[:, KOFF + jss * 128: KOFF + (jss + 1) * 128],
                HH[:, QOFF + i0: QOFF + i0 + 512],
                start=True, stop=True), s["qkm"])
            E.wait("v", s["qkm"])
            E.wait("v", s_wr[wslot], wrneed)
            if qi >= 1:
                E.wait("v", s["sqr"], qi)
            E.op("v", lambda v, w=wslot, bk=bank: v.tensor_add(
                MISC[:, 512:1024], bk[:, :],
                GB[:, 1024 + w * 512: 1024 + (w + 1) * 512]))
            E.op("v", lambda v: v.tensor_scalar_max(
                MISC[:, 512:1024], MISC[:, 512:1024], 0.0), s["qke"])
            E.wait("s", s["qke"])
            if ib == 0 and js == 0:
                E.wait("s", s["ust"])
            if ib > 0 and js == 0:
                E.wait("s", s["acc"])
            E.op("s", lambda sc, jss=js: sc.activation(
                HH[:, KER + jss * 512: KER + (jss + 1) * 512],
                MISC[:, 512:1024], AF.Square, scale=CONST[:, 4:5]), s["sqr"])
        for eg in range(2):
            for js in range(32):
                par = v2_i % 2
                if v2_i == 0:
                    E.wait("y", s["vst"])
                if v2_i >= 2:
                    fsem, fval = ft_v2[v2_i - 2]
                    E.wait("y", fsem, fval)
                E.op("y", lambda y, jss=js, p=par: y.dma_start(
                    HH[:, VSL + p * 2048: VSL + (p + 1) * 2048],
                    VTOK.ap()[jss * 128:(jss + 1) * 128, :]), s_vb[par], 16)
                vneed = s_vb[par].n
                E.wait("t", s_vb[par], vneed)
                if js == 0:
                    E.wait("t", s["ae"])
                    E.wait("t", s["sqr"], (ib + 1) * 32)
                for es8 in range(8):
                    es = eg * 8 + es8
                    msem = None
                    if js == 31:
                        msem = s["acc"]
                    elif es8 == 7:
                        msem = s["accp"]
                    E.op("t", lambda t, jss=js, ess=es, p=par,
                         bk=P[es8]: t.matmul(
                        bk[:, :],
                        HH[:, VSL + p * 2048 + ess * 128:
                           VSL + p * 2048 + ess * 128 + 128],
                        HH[:, KER + jss * 512: KER + (jss + 1) * 512],
                        start=(jss == 0), stop=(jss == 31)), msem)
                if js == 31:
                    ft_v2.append((s["acc"], s["acc"].n))
                else:
                    ft_v2.append((s["accp"], s["accp"].n))
                v2_i += 1
            E.wait("y", s["ae"])
            E.op("y", lambda y, egg=eg, i0=ib: y.dma_start(
                HH[:, USL:USL + 4096].rearrange("q (e c) -> q e c", e=8),
                UF.ap().rearrange("(e p r) -> p e r", p=128, r=S)
                [:, egg * 8:(egg + 1) * 8, i0 * 512:(i0 + 1) * 512]),
                s["ul"], 16)
            E.wait("v", s["ul"])
            E.wait("v", s["acc"])
            for es8 in range(8):
                es = eg * 8 + es8
                E.op("v", lambda v, ess=es, e8=es8, bk=P[es8]: v.tensor_mul(
                    XF[:, ess * 512:(ess + 1) * 512], bk[:, :],
                    HH[:, USL + e8 * 512: USL + (e8 + 1) * 512]), s["ae"])
        E.wait("t", s["ow"])
        E.wait("t", s["ae"])
        if ib == 0:
            E.wait("v", s["t1st"])
        if ib > 0:
            E.wait("v", s["yst"], ib * 16)
        for dsg in range(2):
            for ds4 in range(4):
                ds = dsg * 4 + ds4
                bank = P[ds4]
                if dsg == 1 or ib > 0:
                    E.wait("t", s["oe"], s["oe"].n - 3 + ds4)
                for ec in range(16):
                    E.op("t", lambda t, dd=ds, cc=ec, bk=bank: t.matmul(
                        bk[:, :],
                        WA[:, cc * 1024 + dd * 128: cc * 1024 + (dd + 1) * 128],
                        XF[:, cc * 512:(cc + 1) * 512],
                        start=(cc == 0), stop=(cc == 15)),
                        s["om"] if ec == 15 else None)
            for ds4 in range(4):
                ds = dsg * 4 + ds4
                E.wait("v", s["om"], s["om"].n - 4 + ds4 + 1)
                if s["ote"].n > 0:
                    E.wait("v", s["ote"])
                E.op("v", lambda v, bk=P[ds4]: v.tensor_copy(
                    MISC[:, 0:512], bk[:, :]), s["oe"])
                for isub in range(4):
                    E.wait("t", s["oe"])
                    E.wait("t", s["ote"])
                    E.op("t", lambda t, ii=isub: t.transpose(
                        P[6][:, 0:128], MISC[:, ii * 128:(ii + 1) * 128],
                        IDF[:, :]), s["ot"])
                    E.wait("v", s["ot"])
                    E.op("v", lambda v, dd=ds, ii=isub: v.tensor_copy(
                        YY[:, ii * 1024 + dd * 128: ii * 1024 + (dd + 1) * 128],
                        P[6][:, 0:128]), s["ote"])
        E.wait("g", s["ote"])
        E.op("g", lambda g, i0=ib: g.dma_start(
            R.ap().rearrange("(k b p) d -> k p b d", p=128, b=4)[i0],
            YY[:, 0:4096].rearrange("p (b d) -> p b d", b=4),
            accum_op=ALU.add), s["yst"], 16)
    E.wait("g", s["yst"])
    E.wait("g", s["t1st"])
    return s


def build_l1attn_single():
    nc = bass.Bass("TRN2", target_bir_lowering=False)
    stack = ExitStack()
    E = Emitter(nc, stack)
    consts = host_constants()
    with stack:
        alloc_arenas(E)
        xin = nc.dram_tensor("xin", [S, D], F32, kind="ExternalInput")
        out = nc.dram_tensor("out", [S, D], F32, kind="ExternalOutput")
        identf = nc.dram_tensor("identf", [128, 128], F32, kind="ExternalInput")
        rcos = nc.dram_tensor("rcos", [S, 64], F32, kind="ExternalInput")
        rsin = nc.dram_tensor("rsin", [S, 64], F32, kind="ExternalInput")
        ext = {}
        for n, sh in (("g", [1]), ("uv_w", [4224, 1024]), ("gamma", [2, 128]),
                      ("beta", [2, 128]), ("w_rel", [2047]),
                      ("w_rel_rev", [2047]),
                      ("o_w", [1024, 2048]), ("res_scale", [1024])):
            ext[n] = nc.dram_tensor(n, sh, F32, kind="ExternalInput")
        uv_bf = nc.dram_tensor("uv_bf", [4224, 1024], BF16)
        ow_bf = nc.dram_tensor("ow_bf", [1024, 2048], BF16)
        uvT = nc.dram_tensor("uvT", [8 * 128 * 4224], BF16)
        owT = nc.dram_tensor("owT", [16 * 128 * 1024], BF16)
        casts = [(ext["uv_w"].ap(), uv_bf), (ext["o_w"].ap(), ow_bf)]
        tps = [(uv_bf, uvT, 4224, 1024), (ow_bf, owT, 1024, 2048)]
        R = nc.dram_tensor("Rres", [S, D], F32)
        UF = nc.dram_tensor("UF", [16 * 128 * S], BF16)
        VTOK = nc.dram_tensor("VTOK", [S, 2048], BF16)
        WPAD = nc.dram_tensor("WPAD", [8192], F32)
        WREP = nc.dram_tensor("WREP", [128 * 8064], F32)
        cfg = dict(R=R, uvT=uvT, owT=owT, UF=UF, VT=VTOK, WPAD=WPAD, WREP=WREP,
                   rcos=rcos, rsin=rsin,
                   g=ext["g"], gamma=ext["gamma"], beta=ext["beta"],
                   w_rel=ext["w_rel"], w_rel_rev=ext["w_rel_rev"],
                   res_scale=ext["res_scale"])

        s_init_y = E.sem("inity")
        s_init_g = E.sem("initg")
        E.op("y", lambda y: y.dma_start(E.t["IDF"][:, :], identf.ap()),
             s_init_y, 16)
        E.op("g", lambda g: g.dma_start(R.ap(), xin.ap()), s_init_g, 16)
        emit_weight_prep(E, casts, tps)
        E.wait("y", s_init_y)
        E.wait("g", s_init_g)
        E.wait("t", s_init_y)
        E.flush()

        import os as _os4
        E.begin_phase()
        if _os4.environ.get("DBGDUMP") == "1":
            E.t["DQK"] = nc.dram_tensor("DQK", [128, 8192], F32)
        emit_l1attn_phase(E, cfg)
        E.flush()
        E.end_phase()

        s_out = E.sem("outc")
        E.op("g", lambda g: g.dma_start(out.ap(), R.ap()), s_out, 16)
        if _os4.environ.get("DBGDUMP") == "1":
            for nm, src in (("duf", UF), ("dvt", VTOK)):
                dd = nc.dram_tensor(nm, src.shape, F32, kind="ExternalOutput")
                E.op("g", lambda g, a=src, b=dd: g.dma_start(b.ap(), a.ap()),
                     s_out, 16)
            dwp = nc.dram_tensor("dwp", [8191], F32, kind="ExternalOutput")
            E.op("g", lambda g: g.dma_start(
                dwp.ap().rearrange("(a b) -> a b", a=1),
                WPAD.ap().rearrange("(a b) -> a b", a=1)[:, 0:8191]), s_out, 16)
            dqk2 = nc.dram_tensor("dqk2", [128, 8192], F32, kind="ExternalOutput")
            E.op("g", lambda g: g.dma_start(dqk2.ap(), E.t["DQK"].ap()),
                 s_out, 16)
        E.wait("g", s_out)
        E.flush()
    return nc


_PROG_CACHE = {}


def _get_prog(kind):
    if kind not in _PROG_CACHE:
        if kind == "l0":
            _PROG_CACHE[kind] = build_moe_single("l0")
        elif kind == "l1":
            _PROG_CACHE[kind] = build_moe_single("l1")
        elif kind == "l0a":
            _PROG_CACHE[kind] = build_l0attn_single()
        elif kind == "l1a":
            _PROG_CACHE[kind] = build_l1attn_single()
    return _PROG_CACHE[kind]


def kernel_unfused(**inputs):
    consts = host_constants()
    ident = np.eye(128, dtype=np.float32)
    f32 = lambda a: np.ascontiguousarray(np.asarray(a, dtype=np.float32))
    R = [f32(inputs["L0_attn_inputs"][b]) for b in range(B)]

    def launch(kind, names):
        nc = _get_prog(kind)
        base = {k: f32(inputs[v]) for k, v in names.items()}
        base["identf"] = ident
        if kind == "l1a":
            base["w_rel_rev"] = np.ascontiguousarray(
                f32(inputs["L1_attn_w_rel"])[::-1])
            base["rcos"] = consts["rope_cos"]
            base["rsin"] = consts["rope_sin"]
        in_maps = [dict(base, xin=R[b]) for b in range(B)]
        res = run_bass_kernel_spmd(nc, in_maps, core_ids=list(range(B)))
        for b in range(B):
            R[b] = np.ascontiguousarray(res.results[b]["out"])

    l0f = dict(gate_w="L0_ffn_gate_w", gate_bias="L0_ffn_gate_bias",
               w1="L0_ffn_w1", w3="L0_ffn_w3", w2="L0_ffn_w2",
               sw1="L0_ffn_sw1", sw3="L0_ffn_sw3", sw2="L0_ffn_sw2")
    l0a = dict(Wq="L0_attn_Wq", bq="L0_attn_bq", Wk="L0_attn_Wk",
               bk="L0_attn_bk", Wv="L0_attn_Wv", bv="L0_attn_bv",
               Wo="L0_attn_Wo", bo="L0_attn_bo",
               ln1_scale="L0_attn_ln1_scale", ln1_bias="L0_attn_ln1_bias",
               ln2_scale="L0_attn_ln2_scale", ln2_bias="L0_attn_ln2_bias",
               W1="L0_attn_W1", b1="L0_attn_b1", W2="L0_attn_W2",
               b2="L0_attn_b2")
    l1f = dict(W_router="L1_ffn_W_router", W_gate="L1_ffn_W_gate",
               W_up="L1_ffn_W_up", W_down="L1_ffn_W_down")
    l1a = dict(g="L1_attn_g", uv_w="L1_attn_uv_w", gamma="L1_attn_gamma",
               beta="L1_attn_beta", w_rel="L1_attn_w_rel",
               o_w="L1_attn_o_w", res_scale="L1_attn_res_scale")

    launch("l0", l0f)
    launch("l0a", l0a)
    launch("l0", l0f)
    launch("l1", l1f)
    launch("l1a", l1a)
    launch("l1", l1f)
    return np.stack(R).astype(np.float32)


def build_full():
    nc = bass.Bass("TRN2", target_bir_lowering=False)
    stack = ExitStack()
    E = Emitter(nc, stack)
    with stack:
        alloc_arenas(E)
        xin = nc.dram_tensor("xin", [S, D], F32, kind="ExternalInput")
        out = nc.dram_tensor("out", [S, D], F32, kind="ExternalOutput")
        identf = nc.dram_tensor("identf", [128, 128], F32, kind="ExternalInput")
        rcos = nc.dram_tensor("rcos", [S, 64], F32, kind="ExternalInput")
        rsin = nc.dram_tensor("rsin", [S, 64], F32, kind="ExternalInput")
        wrev = nc.dram_tensor("w_rel_rev", [2047], F32, kind="ExternalInput")
        ext = {}
        shapes = {
            "L0_attn_ln1_scale": [1024], "L0_attn_ln1_bias": [1024],
            "L0_attn_Wq": [1024, 16, 64], "L0_attn_bq": [16, 64],
            "L0_attn_Wk": [1024, 16, 64], "L0_attn_bk": [16, 64],
            "L0_attn_Wv": [1024, 16, 64], "L0_attn_bv": [16, 64],
            "L0_attn_Wo": [16, 64, 1024], "L0_attn_bo": [1024],
            "L0_attn_ln2_scale": [1024], "L0_attn_ln2_bias": [1024],
            "L0_attn_W1": [1024, 4096], "L0_attn_b1": [4096],
            "L0_attn_W2": [4096, 1024], "L0_attn_b2": [1024],
            "L0_ffn_gate_w": [8, 1024], "L0_ffn_gate_bias": [8],
            "L0_ffn_w1": [8, 1024, 1024], "L0_ffn_w3": [8, 1024, 1024],
            "L0_ffn_w2": [8, 1024, 1024], "L0_ffn_sw1": [2048, 1024],
            "L0_ffn_sw3": [2048, 1024], "L0_ffn_sw2": [1024, 2048],
            "L1_attn_g": [1], "L1_attn_uv_w": [4224, 1024],
            "L1_attn_gamma": [2, 128], "L1_attn_beta": [2, 128],
            "L1_attn_w_rel": [2047], "L1_attn_o_w": [1024, 2048],
            "L1_attn_res_scale": [1024],
            "L1_ffn_W_router": [8, 1024], "L1_ffn_W_gate": [8, 4096, 1024],
            "L1_ffn_W_up": [8, 4096, 1024], "L1_ffn_W_down": [8, 1024, 4096],
        }
        for n, sh in shapes.items():
            ext[n] = nc.dram_tensor(n, sh, F32, kind="ExternalInput")

        l0f_t, c0, t0 = moe_weight_scratch(
            E, "l0f", 1024, 8, 2048,
            dict(gw=ext["L0_ffn_gate_w"], w1=ext["L0_ffn_w1"],
                 w3=ext["L0_ffn_w3"], w2=ext["L0_ffn_w2"],
                 sw1=ext["L0_ffn_sw1"], sw3=ext["L0_ffn_sw3"],
                 sw2=ext["L0_ffn_sw2"]))
        l1f_t, c1, t1 = moe_weight_scratch(
            E, "l1f", 4096, 8, None,
            dict(gw=ext["L1_ffn_W_router"], w1=ext["L1_ffn_W_gate"],
                 w3=ext["L1_ffn_W_up"], w2=ext["L1_ffn_W_down"]))
        l0a_bf = {}
        c2 = []
        for n, src_ap, sh in (
            ("wq", ext["L0_attn_Wq"].ap().rearrange("a h m -> a (h m)"), [1024, 1024]),
            ("wk", ext["L0_attn_Wk"].ap().rearrange("a h m -> a (h m)"), [1024, 1024]),
            ("wv", ext["L0_attn_Wv"].ap().rearrange("a h m -> a (h m)"), [1024, 1024]),
            ("wo", ext["L0_attn_Wo"].ap().rearrange("h m d -> (h m) d"), [1024, 1024]),
            ("w1", ext["L0_attn_W1"].ap(), [1024, 4096]),
            ("w2", ext["L0_attn_W2"].ap(), [4096, 1024]),
        ):
            l0a_bf[n] = nc.dram_tensor(f"l0a_{n}_bf", sh, BF16)
            c2.append((src_ap, l0a_bf[n], sh[0], sh[1]))
        uv_bf = nc.dram_tensor("uv_bf", [4224, 1024], BF16)
        ow_bf = nc.dram_tensor("ow_bf", [1024, 2048], BF16)
        uvT = nc.dram_tensor("uvT", [8 * 128 * 4224], BF16)
        owT = nc.dram_tensor("owT", [16 * 128 * 1024], BF16)
        c3 = [(ext["L1_attn_uv_w"].ap(), uv_bf, 4224, 1024),
              (ext["L1_attn_o_w"].ap(), ow_bf, 1024, 2048)]
        t3 = [(uv_bf, uvT, 4224, 1024), (ow_bf, owT, 1024, 2048)]

        R = nc.dram_tensor("Rres", [S, D], F32)
        GF = nc.dram_tensor("GF", [8, S], F32)
        PHIQ = nc.dram_tensor("PHIQ", [S, D], BF16)
        ZBUF = nc.dram_tensor("ZBUF", [S, D], F32)
        RDH = nc.dram_tensor("RDH", [16, S], F32)
        KSROW = nc.dram_tensor("KSROW", [8, 128], F32)
        UF = nc.dram_tensor("UF", [16 * 128 * S], BF16)
        VTOK = nc.dram_tensor("VTOK", [S, 2048], BF16)
        WPAD = nc.dram_tensor("WPAD", [8192], F32)
        WREP = nc.dram_tensor("WREP", [128 * 8064], F32)

        cfg_l0f = dict(tag="l0f", F=1024, n_exp=8, gate_mode="l0",
                       shared=dict(F=2048), gw=l0f_t["gw"], wgT=l0f_t["wgT"],
                       wuT=l0f_t["wuT"], wdT=l0f_t["wdT"], swT=l0f_t["swT"],
                       sw2T=l0f_t["sw2T"], gate_bias=ext["L0_ffn_gate_bias"],
                       R=R, Gf=GF)
        cfg_l1f = dict(tag="l1f", F=4096, n_exp=8, gate_mode="l1",
                       shared=None, gw=l1f_t["gw"], wgT=l1f_t["wgT"],
                       wuT=l1f_t["wuT"], wdT=l1f_t["wdT"], R=R, Gf=GF)
        cfg_l0a = dict(R=R, PHIQ=PHIQ, ZBUF=ZBUF, RDH=RDH, KSROW=KSROW,
                       w=l0a_bf,
                       ln1_scale=ext["L0_attn_ln1_scale"],
                       ln1_bias=ext["L0_attn_ln1_bias"],
                       ln2_scale=ext["L0_attn_ln2_scale"],
                       ln2_bias=ext["L0_attn_ln2_bias"],
                       b1=ext["L0_attn_b1"],
                       bk=ext["L0_attn_bk"], bv=ext["L0_attn_bv"],
                       bq=ext["L0_attn_bq"], bo=ext["L0_attn_bo"],
                       b2m=ext["L0_attn_b2"])
        cfg_l1a = dict(R=R, uvT=uvT, owT=owT, UF=UF, VT=VTOK, WPAD=WPAD,
                       WREP=WREP, rcos=rcos, rsin=rsin,
                       g=ext["L1_attn_g"], gamma=ext["L1_attn_gamma"],
                       beta=ext["L1_attn_beta"], w_rel=ext["L1_attn_w_rel"],
                       w_rel_rev=wrev, res_scale=ext["L1_attn_res_scale"])

        s_init_y = E.sem("inity")
        s_init_g = E.sem("initg")
        E.op("y", lambda y: y.dma_start(E.t["IDF"][:, :], identf.ap()),
             s_init_y, 16)
        E.op("g", lambda g: g.dma_start(R.ap(), xin.ap()), s_init_g, 16)
        emit_weight_prep(E, c0, t0)
        cast_marks, seg_a_sts, st_finals = queue_bg_prep(
            E, c1[:3] + c2 + c1[3:] + c3, t1 + t3, {"l0a": 8})
        E.pump("g", 16)
        E.wait("y", s_init_y)
        E.wait("g", s_init_g)
        E.wait("t", s_init_y)
        E.flush()

        for pi, (emitfn, cfg) in enumerate((
                (emit_moe_phase, cfg_l0f),
                (emit_l0attn_phase, cfg_l0a),
                (emit_moe_phase, cfg_l0f),
                (emit_moe_phase, cfg_l1f),
                (emit_l1attn_phase, cfg_l1a),
                (emit_moe_phase, cfg_l1f))):
            E.begin_phase()
            if pi == 1:
                E.wait("y", E.bg_cast, cast_marks["l0a"])
                for sem, cnt in seg_a_sts:
                    if cnt:
                        E.wait("s", sem, cnt)
                        E.wait("v", sem, cnt)
            elif pi == 2:
                E.pop_stop("s")
            elif pi == 3:
                E.drain_bg()
                for sem, cnt in st_finals:
                    if cnt:
                        for eng in ENGS:
                            E.wait(eng, sem, cnt)
                E.wait("y", E.bg_cast, E.bg_cast.n)
            emitfn(E, cfg)
            E.flush()
            E.end_phase()

        s_out = E.sem("outc")
        E.op("g", lambda g: g.dma_start(out.ap(), R.ap()), s_out, 16)
        E.wait("g", s_out)
        E.flush()
    return nc


def kernel(**inputs):
    consts = host_constants()
    f32 = lambda a: np.ascontiguousarray(np.asarray(a, dtype=np.float32))
    base = {}
    for n in ("L0_attn_ln1_scale L0_attn_ln1_bias L0_attn_Wq L0_attn_bq "
              "L0_attn_Wk L0_attn_bk L0_attn_Wv L0_attn_bv L0_attn_Wo "
              "L0_attn_bo L0_attn_ln2_scale L0_attn_ln2_bias L0_attn_W1 "
              "L0_attn_b1 L0_attn_W2 L0_attn_b2 L0_ffn_gate_w "
              "L0_ffn_gate_bias L0_ffn_w1 L0_ffn_w3 L0_ffn_w2 L0_ffn_sw1 "
              "L0_ffn_sw3 L0_ffn_sw2 L1_attn_g L1_attn_uv_w L1_attn_gamma "
              "L1_attn_beta L1_attn_w_rel L1_attn_o_w L1_attn_res_scale "
              "L1_ffn_W_router L1_ffn_W_gate L1_ffn_W_up "
              "L1_ffn_W_down").split():
        base[n] = f32(inputs[n])
    base["identf"] = np.eye(128, dtype=np.float32)
    base["rcos"] = consts["rope_cos"]
    base["rsin"] = consts["rope_sin"]
    base["w_rel_rev"] = np.ascontiguousarray(f32(inputs["L1_attn_w_rel"])[::-1])
    x = f32(inputs["L0_attn_inputs"])
    nc = build_full()
    in_maps = [dict(base, xin=np.ascontiguousarray(x[b])) for b in range(B)]
    res = run_bass_kernel_spmd(nc, in_maps, core_ids=list(range(B)))
    return np.stack([res.results[b]["out"] for b in range(B)]).astype(np.float32)
